# version 05
# baseline (speedup 1.0000x reference)
import numpy as np
import ml_dtypes
import concourse.bass as bass
import concourse.mybir as mybir
from concourse.bass_core_rust import IndirectOffsetOnAxis
from concourse.bass_utils import run_bass_kernel_spmd

F32 = mybir.dt.float32
BF16 = mybir.dt.bfloat16
U32 = mybir.dt.uint32
F16 = mybir.dt.float16
AF = mybir.ActivationFunctionType
ALU = mybir.AluOpType
AX = mybir.AxisListType.X

NCORES = 8
B, S, D = 16, 2048, 1024
SEQ_PER_CORE = 2
TOK = SEQ_PER_CORE * S
CORE_BATCHES = [[i, 15 - i] for i in range(NCORES)]
CAP0 = 3072
EPS = 1e-5
TWO_PI = 6.283185307179586
PI = 3.141592653589793
HALF_PI = 1.5707963267948966


class Sem:
    def __init__(self, handle):
        self.h = handle
        self.n = 0

class Emitter:
    def __init__(self, auto=None):
        self.ops = []
        self.auto = auto or {}

    def emit(self, engine, fn=None, wait=(), inc=()):
        self.ops.append((engine, fn, list(wait), list(inc)))

    def dma(self, engine, out, in_, sem, wait=()):
        def f(eng, out=out, in_=in_):
            return eng.dma_start(out=out, in_=in_)
        sem.n += 16
        self.emit(engine, f, wait=wait, inc=[(sem, 16)])
        return sem.n

    def step(self, engine, fn, sem=None, wait=()):
        if engine in self.auto:
            a = self.auto[engine]
            assert sem is None or sem is a
            w = [(a, a.n)] + [x for x in wait if not (x[0] is a)]
            a.n += 1
            self.emit(engine, fn, wait=w, inc=[(a, 1)])
            return a.n
        inc = []
        if sem is not None:
            sem.n += 1
            inc = [(sem, 1)]
        self.emit(engine, fn, wait=wait, inc=inc)
        return sem.n if sem is not None else None

    def replay(self, block):
        def section(eng_name):
            def body(eng):
                for engine, fn, waits, incs in self.ops:
                    if engine != eng_name:
                        continue
                    for (s, v) in waits:
                        eng.wait_ge(s.h, v)
                    if fn is not None:
                        inst = fn(eng)
                        for (s, a) in incs:
                            assert inst is not None, "inc attached to op that returned None"
                            inst = inst.then_inc(s.h, a)
            return body
        block.sync(section("sync"))
        block.scalar(section("scalar"))
        block.vector(section("vector"))
        block.tensor(section("tensor"))
        block.gpsimd(section("gpsimd"))


def emit_ln_stats(E, bn6, bnag, xt, width, mean, rstd, dv, ac, eps_col, wait=()):
    n_sub = width // 512

    def f_bn(eng, n_sub=n_sub, xt=xt):
        i = None
        for s0 in range(n_sub):
            i = eng.bn_stats(out=bn6[:, 6 * s0:6 * s0 + 6], in_=xt[:, 512 * s0:512 * (s0 + 1)])
        return i
    E.step("vector", f_bn, wait=list(wait) + [(ac, ac.n)])

    def f_ag(eng, n_sub=n_sub):
        return eng.bn_aggr(out=bnag[:, :], in_=bn6[:, 0:6 * n_sub])
    E.step("vector", f_ag)

    def f_mean(eng, mean=mean):
        return eng.tensor_copy(out=mean[:, :], in_=bnag[:, 0:1])
    E.step("vector", f_mean, sem=dv)

    def f_ln(eng, rstd=rstd):
        return eng.activation(out=rstd[:, :], in_=bnag[:, 1:2], func=AF.Ln, bias=eps_col, scale=1.0)
    E.step("scalar", f_ln, wait=[(dv, dv.n)])

    def f_exp(eng, rstd=rstd):
        return eng.activation(out=rstd[:, :], in_=rstd[:, :], func=AF.Exp, bias=0.0, scale=-0.5)
    E.step("scalar", f_exp, sem=ac)


def emit_transpose_tile(E, src, dst_fn, ps_tr, ident, pe, dv, wait=(), n_chunks=8):
    for c in range(n_chunks):
        def f_tr(eng, c=c):
            return eng.transpose(out=ps_tr[:, :], in_=src[:, 128 * c:128 * (c + 1)], identity=ident[:, :])
        E.step("tensor", f_tr, sem=pe, wait=list(wait) + [(dv, dv.n)])

        def f_ev(eng, c=c):
            return eng.tensor_copy(out=dst_fn(c), in_=ps_tr[:, :])
        E.step("vector", f_ev, sem=dv, wait=[(pe, pe.n)])



def emit_attn(nc, E, sems, io, layer):
    x = io["x"]
    identb = io["identb"]
    if layer == 1:
        wq, wk, wv, wo = io["wq"], io["wk"], io["wv"], io["wo"]
        gamma, beta = io["gamma"], io["beta"]
    else:
        wqkvT, woT = io["wqkvT"], io["woT"]
        freqs = io["freqs"]
        repmat, repsign, swapm = io["repmat"], io["repsign"], io["swapm"]
        pa_rows = io["pa_rows"]
    xo = io["xo"]

    NHEAD = 16
    NSTRIPE = S // 512
    NKC = S // 128
    SCALE = 0.125

    from contextlib import ExitStack
    with ExitStack() as ctx:
        xaT = ctx.enter_context(nc.sbuf_tensor("xaT", [128, 8, S], BF16))
        kT = ctx.enter_context(nc.sbuf_tensor("kT", [128, 8, S], BF16))
        vtm = ctx.enter_context(nc.sbuf_tensor("vtm", [128, NKC, D], BF16))
        qT = ctx.enter_context(nc.sbuf_tensor("qT", [128, 8, 512], BF16))
        expT = ctx.enter_context(nc.sbuf_tensor("expT", [128, NKC, 512], BF16))
        aoT = ctx.enter_context(nc.sbuf_tensor("aoT", [128, 8, 512], BF16))
        wbuf = ctx.enter_context(nc.sbuf_tensor("wbuf", [128, 8, 2 * D], BF16))
        xt = ctx.enter_context(nc.sbuf_tensor("xt", [128, D], F32))
        xab = ctx.enter_context(nc.sbuf_tensor("xab", [128, D], BF16))
        bn6 = ctx.enter_context(nc.sbuf_tensor("bn6", [128, 12], F32))
        bnag = ctx.enter_context(nc.sbuf_tensor("bnag", [128, 2], F32))
        stats = ctx.enter_context(nc.sbuf_tensor("stats", [128, 4], F32))
        cc = ctx.enter_context(nc.sbuf_tensor("cc", [128, 2], F32))
        if layer == 1:
            gb = ctx.enter_context(nc.sbuf_tensor("gb", [128, 2 * D], F32))
        onesb = ctx.enter_context(nc.sbuf_tensor("onesb", [128, 1], BF16))
        onesf = ctx.enter_context(nc.sbuf_tensor("onesf", [128, 1], F32))
        ones1 = ctx.enter_context(nc.sbuf_tensor("ones1", [1, 128], BF16))
        ones1f = ctx.enter_context(nc.sbuf_tensor("ones1f", [1, 128], F32))
        scr = ctx.enter_context(nc.sbuf_tensor("scr", [128, 512], F32))
        scr2 = ctx.enter_context(nc.sbuf_tensor("scr2", [128, 512], F32))
        tmp1 = ctx.enter_context(nc.sbuf_tensor("tmp1", [1, 4], F32))
        bsq = ctx.enter_context(nc.sbuf_tensor("bsq", [1, 16], F32))
        crow = ctx.enter_context(nc.sbuf_tensor("crow", [1, 512], F32))
        crow_bf = ctx.enter_context(nc.sbuf_tensor("crow_bf", [1, 512], BF16))
        srow = ctx.enter_context(nc.sbuf_tensor("srow", [1, 512], F32))
        rrow = ctx.enter_context(nc.sbuf_tensor("rrow", [1, 512], F32))
        rrow_bf = ctx.enter_context(nc.sbuf_tensor("rrow_bf", [1, 512], BF16))
        rb = ctx.enter_context(nc.sbuf_tensor("rb", [64, 512], F32))
        xres = ctx.enter_context(nc.sbuf_tensor("xres", [128, D], F32))
        xout = ctx.enter_context(nc.sbuf_tensor("xout", [128, D], F32))
        if layer == 0:
            ctabq = ctx.enter_context(nc.sbuf_tensor("ctabq", [128, S], BF16))
        if layer == 0:
            stabq = ctx.enter_context(nc.sbuf_tensor("stabq", [128, S], BF16))
        if layer == 0:
            freq_s = ctx.enter_context(nc.sbuf_tensor("freq_s", [1, 32], F32))
        if layer == 0:
            rep_s = ctx.enter_context(nc.sbuf_tensor("rep_s", [32, 128], BF16))
        if layer == 0:
            repsg_s = ctx.enter_context(nc.sbuf_tensor("repsg_s", [32, 128], BF16))
        if layer == 0:
            swap_s = ctx.enter_context(nc.sbuf_tensor("swap_s", [128, 128], BF16))
        if layer == 0:
            pa_s = ctx.enter_context(nc.sbuf_tensor("pa_s", [1, 32], F32))
        if layer == 0:
            pacol = ctx.enter_context(nc.sbuf_tensor("pacol", [128, 32], F32))
        if layer == 0:
            dist = ctx.enter_context(nc.sbuf_tensor("dist", [128, 2, 512], F16))
        if layer == 0:
            lnt = ctx.enter_context(nc.sbuf_tensor("lnt", [128, 2, 512], F16))
            scrP = ctx.enter_context(nc.sbuf_tensor("scrP", [128, 2, 512], F32))
            tb_bf = ctx.enter_context(nc.sbuf_tensor("tb_bf", [32, 512], BF16))
        identb_s = ctx.enter_context(nc.sbuf_tensor("identb_s", [128, 128], BF16))
        ps_a = ctx.enter_context(nc.psum_tensor("ps_a", [128, 512], F32))
        ps_a2 = ctx.enter_context(nc.psum_tensor("ps_a2", [128, 512], F32))
        ps_b = ctx.enter_context(nc.psum_tensor("ps_b", [128, 512], F32))
        ps_tr = ctx.enter_context(nc.psum_tensor("ps_tr", [128, 128], BF16))
        ps_pv = ctx.enter_context(nc.psum_tensor("ps_pv", [64, 512], F32))
        ps_s = ctx.enter_context(nc.psum_tensor("ps_s", [1, 512], F32))
        ps_r = ctx.enter_context(nc.psum_tensor("ps_r", [64, 512], F32))
        ps_c = ctx.enter_context(nc.psum_tensor("ps_c", [128, 512], F32))
        ld, pe, dv, ac, go, st, lg = sems

        E.step("vector", lambda eng: eng.memset(onesb[:, :], 1.0))
        E.step("vector", lambda eng: eng.memset(onesf[:, :], 1.0))
        E.step("vector", lambda eng: eng.memset(ones1[:, :], 1.0))
        E.step("vector", lambda eng: eng.memset(ones1f[:, :], 1.0))
        E.step("vector", lambda eng: eng.memset(cc[:, 0:1], EPS))
        E.step("vector", lambda eng: eng.memset(cc[:, 1:2], 1.0), sem=dv)
        E.dma("sync", identb_s[:, :], identb[:, :], ld)

        if layer == 1:
            E.dma("sync", gb[0:1, 0:D], gamma[:, :], ld)
            c_b = E.dma("sync", gb[0:1, D:2 * D], beta[:, :], ld)
            for half in range(4):
                def f_mm(eng, half=half):
                    return eng.matmul(out=ps_c[:, :], lhsT=ones1f[:, :],
                                      rhs=gb[0:1, 512 * half:512 * (half + 1)], start=True, stop=True)
                E.step("tensor", f_mm, sem=pe, wait=[(ld, c_b), (dv, dv.n)])

                def f_ev(eng, half=half):
                    return eng.tensor_copy(out=gb[:, 512 * half:512 * (half + 1)], in_=ps_c[:, :])
                E.step("vector", f_ev, sem=dv, wait=[(pe, pe.n)])
        else:
            E.dma("sync", freq_s[:, :], freqs[:, :], ld)
            E.dma("sync", rep_s[:, :], repmat[:, :], ld)
            E.dma("sync", repsg_s[:, :], repsign[:, :], ld)
            E.dma("sync", swap_s[:, :], swapm[:, :], ld)
            E.dma("sync", pa_s[0:1, 0:16], pa_rows[0:1, :], ld)
            c_pa = E.dma("sync", pa_s[0:1, 16:32], pa_rows[1:2, :], ld)
            E.step("vector", lambda eng: eng.tensor_scalar_max(pa_s[:, :], pa_s[:, :], 0.01),
                   sem=dv, wait=[(ld, c_pa)])

            def f_acol(eng):
                return eng.matmul(out=ps_c[:, 0:16], lhsT=ones1f[:, :], rhs=pa_s[0:1, 16:32],
                                  start=True, stop=True)
            E.step("tensor", f_acol, wait=[(dv, dv.n)])

            def f_pcol(eng):
                return eng.matmul(out=ps_c[:, 16:32], lhsT=ones1f[:, :], rhs=pa_s[0:1, 0:16],
                                  start=True, stop=True)
            E.step("tensor", f_pcol, sem=pe)

            def f_evpa(eng):
                return eng.tensor_copy(out=pacol[:, 0:16], in_=ps_c[:, 0:16])
            E.step("vector", f_evpa, wait=[(pe, pe.n)])

            def f_evpa2(eng):
                return eng.tensor_scalar_mul(pacol[:, 16:32], ps_c[:, 16:32], -1.0)
            E.step("vector", f_evpa2, sem=dv)

            for ch in range(S // 512):
                csl = slice(512 * ch, 512 * (ch + 1))

                def f_pos(eng, ch=ch):
                    return eng.iota(out=srow[:, :], pattern=[[1, 512]], base=512 * ch,
                                    channel_multiplier=0, allow_small_or_imprecise_dtypes=True)
                E.step("gpsimd", f_pos, sem=go, wait=[(pe, pe.n)])

                def f_ang(eng, csl=csl):
                    return eng.matmul(out=ps_a[0:32, 0:512], lhsT=freq_s[:, :],
                                      rhs=srow[:, :], start=True, stop=True)
                E.step("tensor", f_ang, sem=pe, wait=[(ld, ld.n), (dv, dv.n), (ac, ac.n), (go, go.n)])

                def f_evang(eng):
                    return eng.tensor_copy(out=scr[0:32, :], in_=ps_a[0:32, 0:512])
                E.step("vector", f_evang, sem=dv, wait=[(pe, pe.n)])
                MAGIC = 12582912.0
                INV2PI = 0.15915494309189535
                for is_cos in (True, False):
                    if is_cos:
                        E.step("vector", lambda eng: eng.tensor_scalar_add(scr2[0:32, :], scr[0:32, :],
                               HALF_PI), wait=[(ac, ac.n)])
                    else:
                        E.step("vector", lambda eng: eng.tensor_copy(out=scr2[0:32, :], in_=scr[0:32, :]),
                               wait=[(ac, ac.n)])
                    E.step("vector", lambda eng: eng.tensor_scalar(out=xt[0:32, 0:512], in0=scr2[0:32, :],
                           scalar1=INV2PI, scalar2=MAGIC, op0=ALU.mult, op1=ALU.add))
                    E.step("vector", lambda eng: eng.tensor_scalar_sub(xt[0:32, 0:512], xt[0:32, 0:512], MAGIC))
                    E.step("vector", lambda eng: eng.scalar_tensor_tensor(out=scr2[0:32, :], in0=xt[0:32, 0:512],
                           scalar=-TWO_PI, in1=scr2[0:32, :], op0=ALU.mult, op1=ALU.add), sem=dv)
                    E.step("scalar", lambda eng: eng.activation(out=scr2[0:32, :], in_=scr2[0:32, :], func=AF.Sin),
                           sem=ac, wait=[(dv, dv.n)])
                    E.step("vector", lambda eng: eng.tensor_copy(out=tb_bf[:, :], in_=scr2[0:32, :]),
                           sem=dv, wait=[(ac, ac.n)])
                    dstq, lhs = (ctabq, rep_s) if is_cos else (stabq, repsg_s)

                    def f_rep(eng, lhs=lhs):
                        return eng.matmul(out=ps_b[:, 0:512], lhsT=lhs[:, :], rhs=tb_bf[:, :], start=True, stop=True)
                    E.step("tensor", f_rep, sem=pe, wait=[(dv, dv.n)])

                    def f_evq(eng, dstq=dstq, csl=csl):
                        return eng.activation(out=dstq[:, csl], in_=ps_b[:, 0:512], func=AF.Copy, scale=SCALE)
                    E.step("scalar", f_evq, sem=ac, wait=[(pe, pe.n)])

        for sq in range(SEQ_PER_CORE):
            xseq = x[sq * S:(sq + 1) * S, :]
            xoseq = xo[sq * S:(sq + 1) * S, :]

            wprev = lg.n
            if layer == 1:
                for c in range(8):
                    E.dma("gpsimd", wbuf[:, c, 0:D], wk[128 * c:128 * (c + 1), :], lg, wait=[(dv, dv.n), (lg, wprev)])
                    E.dma("gpsimd", wbuf[:, c, D:2 * D], wv[128 * c:128 * (c + 1), :], lg, wait=[(dv, dv.n), (lg, wprev)])
            else:
                for c in range(8):
                    E.dma("gpsimd", wbuf[:, c, 0:D], wqkvT[128 * c:128 * (c + 1), D:2 * D], lg, wait=[(dv, dv.n), (lg, wprev)])
                    E.dma("gpsimd", wbuf[:, c, D:2 * D], wqkvT[128 * c:128 * (c + 1), 2 * D:3 * D], lg, wait=[(dv, dv.n), (lg, wprev)])
            ld_wkv = lg.n

            for i in range(S // 128):
                c_x = E.dma("sync", xt[:, :], xseq[128 * i:128 * (i + 1), :], ld, wait=[(dv, dv.n)])
                emit_ln_stats(E, bn6, bnag, xt, D, stats[:, 0:1], stats[:, 1:2], dv, ac, cc[:, 0:1],
                              wait=[(ld, c_x)])

                def f_norm(eng):
                    return eng.tensor_scalar(out=xt[:, :], in0=xt[:, :], scalar1=stats[:, 0:1],
                                             scalar2=stats[:, 1:2], op0=ALU.subtract, op1=ALU.mult)
                E.step("vector", f_norm, wait=[(ac, ac.n)])
                if layer == 1:
                    emit_ln_stats(E, bn6, bnag, xt, D, stats[:, 2:3], stats[:, 3:4], dv, ac, cc[:, 0:1])

                    def f_norm2(eng):
                        return eng.tensor_scalar(out=xt[:, :], in0=xt[:, :], scalar1=stats[:, 2:3],
                                                 scalar2=stats[:, 3:4], op0=ALU.subtract, op1=ALU.mult)
                    E.step("vector", f_norm2, wait=[(ac, ac.n)])

                    def f_gam(eng):
                        return eng.tensor_mul(out=xt[:, :], in0=xt[:, :], in1=gb[:, 0:D])
                    E.step("vector", f_gam)

                    def f_bet(eng):
                        return eng.tensor_add(out=xt[:, :], in0=xt[:, :], in1=gb[:, D:2 * D])
                    E.step("vector", f_bet)

                def f_cast(eng):
                    return eng.tensor_copy(out=xab[:, :], in_=xt[:, :])
                E.step("vector", f_cast, sem=dv)
                emit_transpose_tile(E, xab, lambda c, i=i: xaT[:, c, 128 * i:128 * (i + 1)],
                                    ps_tr, identb_s, pe, dv)

            for fch in range(8):
                for stp in range(NSTRIPE):
                    def f_mm(eng, fch=fch, stp=stp):
                        inst = None
                        for dch in range(8):
                            inst = eng.matmul(out=ps_a[:, :], lhsT=wbuf[:, dch, 128 * fch:128 * (fch + 1)],
                                              rhs=xaT[:, dch, 512 * stp:512 * (stp + 1)],
                                              start=(dch == 0), stop=(dch == 7))
                        return inst
                    E.step("tensor", f_mm, sem=pe, wait=[(lg, ld_wkv), (dv, dv.n)])

                    def f_ev(eng, fch=fch, stp=stp):
                        return eng.tensor_copy(out=kT[:, fch, 512 * stp:512 * (stp + 1)], in_=ps_a[:, :])
                    E.step("vector", f_ev, sem=dv, wait=[(pe, pe.n)])
            if layer == 0:
                for fch in range(8):
                    for stp in range(NSTRIPE):
                        sl = slice(512 * stp, 512 * (stp + 1))

                        def f_swapmm(eng, fch=fch, sl=sl):
                            return eng.matmul(out=ps_b[:, :], lhsT=swap_s[:, :], rhs=kT[:, fch, sl],
                                              start=True, stop=True)
                        E.step("tensor", f_swapmm, sem=pe, wait=[(dv, dv.n)])

                        def f_t1(eng, fch=fch, sl=sl):
                            return eng.scalar_tensor_tensor(out=scr[:, :], in0=kT[:, fch, sl], scalar=8.0,
                                                            in1=ctabq[:, sl], op0=ALU.mult, op1=ALU.mult)
                        E.step("vector", f_t1, wait=[(pe, pe.n)])

                        def f_t2(eng, sl=sl):
                            return eng.scalar_tensor_tensor(out=scr2[:, :], in0=ps_b[:, :], scalar=8.0,
                                                            in1=stabq[:, sl], op0=ALU.mult, op1=ALU.mult)
                        E.step("vector", f_t2)

                        def f_t3(eng, fch=fch, sl=sl):
                            return eng.tensor_add(out=kT[:, fch, sl], in0=scr[:, :], in1=scr2[:, :])
                        E.step("vector", f_t3, sem=dv)
            for i in range(NKC):
                for vh in range(2):
                    def f_mm(eng, i=i, vh=vh):
                        inst = None
                        for dch in range(8):
                            inst = eng.matmul(out=ps_a[:, :], lhsT=xaT[:, dch, 128 * i:128 * (i + 1)],
                                              rhs=wbuf[:, dch, D + 512 * vh:D + 512 * (vh + 1)],
                                              start=(dch == 0), stop=(dch == 7))
                        return inst
                    E.step("tensor", f_mm, sem=pe, wait=[(dv, dv.n), (lg, ld_wkv)])

                    def f_ev(eng, i=i, vh=vh):
                        return eng.tensor_copy(out=vtm[:, i, 512 * vh:512 * (vh + 1)], in_=ps_a[:, :])
                    E.step("vector", f_ev, sem=dv, wait=[(pe, pe.n)])

            wprev = lg.n
            if layer == 1:
                for c in range(8):
                    E.dma("gpsimd", wbuf[:, c, 0:D], wq[128 * c:128 * (c + 1), :], lg, wait=[(dv, dv.n), (lg, wprev)])
                    E.dma("gpsimd", wbuf[:, c, D:2 * D], wo[128 * c:128 * (c + 1), :], lg, wait=[(dv, dv.n), (lg, wprev)])
            else:
                for c in range(8):
                    E.dma("gpsimd", wbuf[:, c, 0:D], wqkvT[128 * c:128 * (c + 1), 0:D], lg, wait=[(dv, dv.n), (lg, wprev)])
                    E.dma("gpsimd", wbuf[:, c, D:2 * D], woT[128 * c:128 * (c + 1), :], lg, wait=[(dv, dv.n), (lg, wprev)])
            ld_qo = lg.n

            E.step("vector", lambda eng: eng.memset(bsq[:, :], 0.0), sem=dv, wait=[(ac, ac.n)])
            for fch in range(8):
                for kc in range(NSTRIPE):
                    def f_sq(eng, fch=fch, kc=kc):
                        return eng.activation(out=scr[:, :], in_=kT[:, fch, 512 * kc:512 * (kc + 1)],
                                              func=AF.Square)
                    E.step("scalar", f_sq, sem=ac, wait=[(dv, dv.n)])
                    for half in range(2):
                        h = 2 * fch + half

                        def f_sum(eng, half=half):
                            rh = scr[0:64, :] if half == 0 else scr[64:128, :]
                            on = onesf[0:64, :] if half == 0 else onesf[64:128, :]
                            return eng.matmul(out=ps_s[:, :], lhsT=on, rhs=rh,
                                              start=True, stop=True)
                        E.step("tensor", f_sum, sem=pe, wait=[(ac, ac.n), (dv, dv.n)])

                        def f_evk(eng):
                            return eng.tensor_copy(out=srow[:, :], in_=ps_s[:, :])
                        E.step("vector", f_evk, wait=[(pe, pe.n)])

                        def f_m1(eng):
                            return eng.reduce_max(out=tmp1[:, 0:1], in_=srow[:, :], axis=AX)
                        E.step("vector", f_m1)

                        def f_m2(eng, h=h):
                            return eng.tensor_max(out=bsq[:, h:h + 1], in0=bsq[:, h:h + 1], in1=tmp1[:, 0:1])
                        E.step("vector", f_m2, sem=dv)
            for h in range(NHEAD):
                def f_sqrt(eng, h=h):
                    return eng.activation(out=bsq[:, h:h + 1], in_=bsq[:, h:h + 1], func=AF.Sqrt)
                E.step("scalar", f_sqrt, sem=ac, wait=[(dv, dv.n)])

            for stp in range(NSTRIPE):
                sl = slice(512 * stp, 512 * (stp + 1))
                for fch in range(8):
                    def f_mm(eng, fch=fch, stp=stp):
                        inst = None
                        for dch in range(8):
                            inst = eng.matmul(out=ps_a[:, :], lhsT=wbuf[:, dch, 128 * fch:128 * (fch + 1)],
                                              rhs=xaT[:, dch, 512 * stp:512 * (stp + 1)],
                                              start=(dch == 0), stop=(dch == 7))
                        return inst
                    E.step("tensor", f_mm, sem=pe, wait=[(lg, ld_qo), (dv, dv.n), (ac, ac.n)])
                    if layer == 1:
                        def f_ev(eng, fch=fch):
                            return eng.tensor_scalar_mul(qT[:, fch, :], ps_a[:, :], SCALE)
                        E.step("vector", f_ev, sem=dv, wait=[(pe, pe.n)])
                    else:
                        def f_ev(eng, fch=fch):
                            return eng.tensor_copy(out=qT[:, fch, :], in_=ps_a[:, :])
                        E.step("vector", f_ev, sem=dv, wait=[(pe, pe.n)])
                if layer == 0:
                    for fch in range(8):
                        def f_swapmm(eng, fch=fch):
                            return eng.matmul(out=ps_b[:, :], lhsT=swap_s[:, :], rhs=qT[:, fch, :],
                                              start=True, stop=True)
                        E.step("tensor", f_swapmm, sem=pe, wait=[(dv, dv.n)])

                        def f_t1(eng, fch=fch, sl=sl):
                            return eng.tensor_mul(out=scr[:, :], in0=qT[:, fch, :], in1=ctabq[:, sl])
                        E.step("vector", f_t1, wait=[(pe, pe.n)])

                        def f_t2(eng, sl=sl):
                            return eng.tensor_mul(out=scr2[:, :], in0=ps_b[:, :], in1=stabq[:, sl])
                        E.step("vector", f_t2)

                        def f_t3(eng, fch=fch):
                            return eng.tensor_add(out=qT[:, fch, :], in0=scr[:, :], in1=scr2[:, :])
                        E.step("vector", f_t3, sem=dv)

                for h in range(NHEAD):
                    fch, rlo = h // 2, (h % 2) * 64

                    def rsl(t):
                        return t[0:64, :] if rlo == 0 else t[64:128, :]

                    def f_qsq(eng, fch=fch):
                        return eng.activation(out=scr[:, :], in_=qT[:, fch, :], func=AF.Square)
                    E.step("scalar", f_qsq, sem=ac, wait=[(dv, dv.n)])

                    def f_qsum(eng, rlo=rlo):
                        rh = scr[0:64, :] if rlo == 0 else scr[64:128, :]
                        on = onesf[0:64, :] if rlo == 0 else onesf[64:128, :]
                        return eng.matmul(out=ps_s[:, :], lhsT=on, rhs=rh, start=True, stop=True)
                    E.step("tensor", f_qsum, sem=pe, wait=[(ac, ac.n)])

                    def f_qs(eng):
                        return eng.activation(out=crow[:, :], in_=ps_s[:, :], func=AF.Sqrt)
                    E.step("scalar", f_qs, sem=ac, wait=[(pe, pe.n)])

                    def f_cmul(eng, h=h):
                        return eng.tensor_scalar(out=crow[:, :], in0=crow[:, :], scalar1=bsq[:, h:h + 1],
                                                 scalar2=-1.0, op0=ALU.mult, op1=ALU.mult)
                    E.step("vector", f_cmul, wait=[(ac, ac.n)])

                    def f_cbf(eng):
                        return eng.tensor_copy(out=crow_bf[:, :], in_=crow[:, :])
                    E.step("vector", f_cbf, sem=dv)

                    qk_done = {}
                    comb_done = {}
                    for kc in range(NKC):
                        ps = ps_a if kc % 2 == 0 else ps_a2
                        if layer == 0:
                            war = [(dv, comb_done.get(kc - 2, 0))]
                        else:
                            war = [(ac, ac.n if kc < 2 else qk_done.get(("exp", kc - 2), 0))]

                        def f_qk(eng, kc=kc, fch=fch, rlo=rlo, ps=ps):
                            lhs = kT[:, fch, 128 * kc:128 * (kc + 1)]
                            q_ = qT[:, fch, :]
                            lhs = lhs[0:64, :] if rlo == 0 else lhs[64:128, :]
                            q_ = q_[0:64, :] if rlo == 0 else q_[64:128, :]
                            eng.matmul(out=ps[:, :], lhsT=lhs, rhs=q_, start=True, stop=False)
                            return eng.matmul(out=ps[:, :], lhsT=ones1[:, :], rhs=crow_bf[:, :],
                                              start=False, stop=True)
                        qk_done[kc] = E.step("tensor", f_qk, sem=pe, wait=[(dv, dv.n)] + war)
                        if layer == 0:
                            def f_iota(eng, kc=kc, stp=stp):
                                return eng.iota(out=dist[:, kc % 2, :], pattern=[[-1, 512]],
                                                base=128 * kc - 512 * stp,
                                                channel_multiplier=1, allow_small_or_imprecise_dtypes=True)
                            E.step("gpsimd", f_iota, sem=go, wait=[(ac, ac.n)])

                            def f_abs(eng, kc=kc):
                                return eng.activation(out=dist[:, kc % 2, :], in_=dist[:, kc % 2, :], func=AF.Abs)
                            E.step("scalar", f_abs, sem=ac, wait=[(go, go.n)])

                            def f_ln(eng, h=h, kc=kc):
                                return eng.activation(out=lnt[:, kc % 2, :], in_=dist[:, kc % 2, :], func=AF.Ln,
                                                      bias=cc[:, 1:2], scale=pacol[:, h:h + 1])
                            E.step("scalar", f_ln, sem=ac)

                            def f_comb(eng, h=h, kc=kc, ps=ps):
                                return eng.scalar_tensor_tensor(out=scrP[:, kc % 2, :], in0=lnt[:, kc % 2, :],
                                                                scalar=pacol[:, 16 + h:17 + h], in1=ps[:, :],
                                                                op0=ALU.mult, op1=ALU.add)
                            comb_done[kc] = E.step("vector", f_comb, sem=dv,
                                                   wait=[(ac, ac.n), (pe, qk_done[kc])])
                            if kc % 2 == 1:
                                def f_exp(eng, kc=kc):
                                    return eng.activation(out=expT[:, kc - 1:kc + 1, :],
                                                          in_=scrP[:, 0:2, :], func=AF.Exp)
                                qk_done[("exp", kc)] = E.step("scalar", f_exp, sem=ac, wait=[(dv, dv.n)])
                        else:
                            def f_exp(eng, kc=kc, ps=ps):
                                return eng.activation(out=expT[:, kc, :], in_=ps[:, :], func=AF.Exp)
                            qk_done[("exp", kc)] = E.step("scalar", f_exp, sem=ac, wait=[(pe, qk_done[kc])])

                    def f_sums(eng):
                        inst = None
                        for kc in range(NKC):
                            inst = eng.matmul(out=ps_s[:, :], lhsT=onesb[:, :], rhs=expT[:, kc, :],
                                              start=(kc == 0), stop=(kc == NKC - 1))
                        return inst
                    E.step("tensor", f_sums, sem=pe, wait=[(ac, ac.n)])

                    def f_evs(eng):
                        return eng.tensor_copy(out=srow[:, :], in_=ps_s[:, :])
                    E.step("vector", f_evs, wait=[(pe, pe.n)])

                    def f_rec(eng):
                        return eng.reciprocal(out=rrow[:, :], in_=srow[:, :])
                    E.step("vector", f_rec)

                    def f_rbf(eng):
                        return eng.tensor_copy(out=rrow_bf[:, :], in_=rrow[:, :])
                    E.step("vector", f_rbf, sem=dv)

                    def f_pv(eng, h=h):
                        inst = None
                        for kc in range(NKC):
                            inst = eng.matmul(out=ps_pv[:, :], lhsT=vtm[:, kc, 64 * h:64 * (h + 1)],
                                              rhs=expT[:, kc, :], start=(kc == 0), stop=(kc == NKC - 1))
                        return inst
                    E.step("tensor", f_pv, wait=[(ac, ac.n)])

                    def f_rb(eng):
                        return eng.matmul(out=ps_r[:, :], lhsT=ones1[0:1, 0:64], rhs=rrow_bf[:, :],
                                          start=True, stop=True)
                    E.step("tensor", f_rb, sem=pe, wait=[(dv, dv.n)])

                    def f_evrb(eng):
                        return eng.tensor_copy(out=rb[:, :], in_=ps_r[:, :])
                    E.step("vector", f_evrb, wait=[(pe, pe.n)])

                    def f_ao(eng, fch=fch, rlo=rlo):
                        dst = aoT[:, fch, :]
                        dst = dst[0:64, :] if rlo == 0 else dst[64:128, :]
                        return eng.tensor_mul(out=dst, in0=ps_pv[:, :], in1=rb[:, :])
                    E.step("vector", f_ao, sem=dv)

                for tt in range(4):
                    t0 = 512 * stp + 128 * tt
                    c_r = E.dma("sync", xres[:, :], xseq[t0:t0 + 128, :], ld, wait=[(dv, dv.n), (st, st.n)])
                    for fh in range(2):
                        def f_mm(eng, tt=tt, fh=fh):
                            inst = None
                            for c in range(8):
                                inst = eng.matmul(out=ps_b[:, :], lhsT=aoT[:, c, 128 * tt:128 * (tt + 1)],
                                                  rhs=wbuf[:, c, D + 512 * fh:D + 512 * (fh + 1)],
                                                  start=(c == 0), stop=(c == 7))
                            return inst
                        E.step("tensor", f_mm, sem=pe, wait=[(dv, dv.n), (ld, c_r)])

                        def f_res(eng, fh=fh):
                            return eng.scalar_tensor_tensor(out=xout[:, 512 * fh:512 * (fh + 1)], in0=ps_b[:, :],
                                                            scalar=1.0, in1=xres[:, 512 * fh:512 * (fh + 1)],
                                                            op0=ALU.mult, op1=ALU.add)
                        E.step("vector", f_res, sem=dv, wait=[(pe, pe.n)])
                    E.dma("scalar", xoseq[t0:t0 + 128, :], xout[:, :], st, wait=[(dv, dv.n)])







class NcPrefix:
    def __init__(self, nc, prefix):
        self._nc = nc
        self._p = prefix

    def sbuf_tensor(self, name, *a, **k):
        return self._nc.sbuf_tensor(self._p + name, *a, **k)

    def psum_tensor(self, name, *a, **k):
        return self._nc.psum_tensor(self._p + name, *a, **k)


def new_program():
    nc = bass.Bass("TRN2", target_bir_lowering=False, detect_race_conditions=False)
    return nc


def make_sems(nc, ctx):
    return tuple(Sem(ctx.enter_context(nc.semaphore(nm)))
                 for nm in ("ld", "pe", "dv", "ac", "go", "st", "lg"))


def full_barrier(E, sems):
    waits = [(s, s.n) for s in sems]
    for e in ("sync", "scalar", "vector", "tensor", "gpsimd"):
        E.emit(e, None, wait=waits)


def finish_program(nc, ctx, E, sems):
    ld, pe, dv, ac, go, st, lg = sems
    E.emit("gpsimd", None, wait=[(st, st.n), (lg, lg.n)])
    block = ctx.enter_context(nc.Block())
    E.replay(block)


def build_attn(layer):
    from contextlib import ExitStack
    nc = new_program()
    io = {"x": nc.dram_tensor("x", [TOK, D], F32, kind="ExternalInput").ap(),
          "identb": nc.dram_tensor("identb", [128, 128], BF16, kind="ExternalInput").ap(),
          "xo": nc.dram_tensor("xo", [TOK, D], F32, kind="ExternalOutput").ap()}
    if layer == 1:
        for n, sh in (("wq", [D, D]), ("wk", [D, D]), ("wv", [D, D]), ("wo", [D, D]),
                      ("gamma", [1, D]), ("beta", [1, D])):
            io[n] = nc.dram_tensor(n, sh, F32, kind="ExternalInput").ap()
    else:
        io["wqkvT"] = nc.dram_tensor("wqkvT", [D, 3 * D], F32, kind="ExternalInput").ap()
        io["woT"] = nc.dram_tensor("woT", [D, D], F32, kind="ExternalInput").ap()
        io["freqs"] = nc.dram_tensor("freqs", [1, 32], F32, kind="ExternalInput").ap()
        io["repmat"] = nc.dram_tensor("repmat", [32, 128], BF16, kind="ExternalInput").ap()
        io["repsign"] = nc.dram_tensor("repsign", [32, 128], BF16, kind="ExternalInput").ap()
        io["swapm"] = nc.dram_tensor("swapm", [128, 128], BF16, kind="ExternalInput").ap()
        io["pa_rows"] = nc.dram_tensor("pa_rows", [2, 16], F32, kind="ExternalInput").ap()
    with ExitStack() as octx:
        sems = make_sems(nc, octx)
        ld, pe, dv, ac, go, st, lg = sems
        E = Emitter(auto={"vector": dv, "scalar": ac, "gpsimd": go})
        emit_attn(nc, E, sems, io, layer)
        finish_program(nc, octx, E, sems)
    return nc


def _bf(a):
    return np.asarray(a).astype(ml_dtypes.bfloat16)

def make_consts():
    identb = _bf(np.eye(128, dtype=np.float32))
    rep = np.zeros((32, 128), np.float32)
    for p in range(128):
        rep[p % 32, p] = 1.0
    repsign = np.zeros((32, 128), np.float32)
    for p in range(128):
        sgn = -1.0 if (p % 64) < 32 else 1.0
        repsign[p % 32, p] = sgn
    swap = np.zeros((128, 128), np.float32)
    for p in range(128):
        q = (p % 64)
        src = (p - p % 64) + ((q + 32) % 64)
        swap[src, p] = 1.0
    pos = np.arange(S, dtype=np.float32)[None, :]
    return identb, _bf(rep), _bf(repsign), _bf(swap), pos


def shard_x(xfull):
    return [np.ascontiguousarray(xfull[CORE_BATCHES[i]].reshape(TOK, D)) for i in range(NCORES)]


def unshard_x(shards):
    out = np.empty((B, S, D), np.float32)
    for i in range(NCORES):
        sh = shards[i].reshape(SEQ_PER_CORE, S, D)
        out[CORE_BATCHES[i][0]] = sh[0]
        out[CORE_BATCHES[i][1]] = sh[1]
    return out


def run_attn(layer, xs, weights):
    return run_attn_cached(layer, xs, weights)


def _run_attn_old(layer, xs, weights):
    nc = build_attn(layer)
    identb, rep, repsign, swap, pos = make_consts()
    in_maps = []
    for i in range(NCORES):
        m = {"x": xs[i], "identb": identb}
        if layer == 1:
            m.update(wq=weights["wq"], wk=weights["wk"], wv=weights["wv"], wo=weights["wo"],
                     gamma=weights["gamma"], beta=weights["beta"])
        else:
            m.update(wqkvT=weights["wqkvT"], woT=weights["woT"],
                     freqs=weights["freqs"], repmat=rep, repsign=repsign, swapm=swap,
                     pa_rows=weights["pa_rows"])
        in_maps.append(m)
    res = run_bass_kernel_spmd(nc, in_maps, core_ids=list(range(NCORES)))
    return [r["xo"] for r in res.results]





def emit_ffn_l1(nc, E, sems, io):
    x = io["x"]
    identb, identf = io["identb"], io["identf"]
    ln_w, ln_b = io["f1_ln_w"], io["f1_ln_b"]
    router_w = io["f1_router_w"]
    gate_w, up_w, down_w = io["f1_gate_w"], io["f1_up_w"], io["f1_down_w"]
    sh_gate, sh_up, sh_down = io["f1_sh_gate"], io["f1_sh_up"], io["f1_sh_down"]
    xo = io["xo"]
    xfT_dram = io["xfT_dram"]
    routed_dram = io["y_dram"]

    NT = TOK // 128
    NST = TOK // 512
    from contextlib import ExitStack
    with ExitStack() as ctx:
        wg = ctx.enter_context(nc.sbuf_tensor("wg", [128, 8, 2048], BF16))
        wu = ctx.enter_context(nc.sbuf_tensor("wu", [128, 8, 2048], BF16))
        wd = ctx.enter_context(nc.sbuf_tensor("wd", [128, 16, D], BF16))
        xfTs = ctx.enter_context(nc.sbuf_tensor("xfTs", [128, 8, 512], BF16))
        xeTs = ctx.enter_context(nc.sbuf_tensor("xeTs", [128, 8, 512], BF16))
        hT = ctx.enter_context(nc.sbuf_tensor("hT", [128, 16, 512], BF16))
        xt = ctx.enter_context(nc.sbuf_tensor("xt", [128, D], F32))
        xa = ctx.enter_context(nc.sbuf_tensor("xa", [128, D], F32))
        rt = ctx.enter_context(nc.sbuf_tensor("rt", [128, D], F32))
        xout = ctx.enter_context(nc.sbuf_tensor("xout", [128, D], F32))
        xfTc = ctx.enter_context(nc.sbuf_tensor("xfTc", [128, 128], F32))
        rw_s = ctx.enter_context(nc.sbuf_tensor("rw_s", [128, 8, 8], F32))
        bn6 = ctx.enter_context(nc.sbuf_tensor("bn6", [128, 12], F32))
        bnag = ctx.enter_context(nc.sbuf_tensor("bnag", [128, 2], F32))
        stats_all = ctx.enter_context(nc.sbuf_tensor("stats_all", [128, NT, 2], F32))
        stats2 = ctx.enter_context(nc.sbuf_tensor("stats2", [128, 2], F32))
        gb = ctx.enter_context(nc.sbuf_tensor("gb", [128, 2 * D], F32))
        cc = ctx.enter_context(nc.sbuf_tensor("cc", [128, 2], F32))
        lrow = ctx.enter_context(nc.sbuf_tensor("lrow", [128, 8], F32))
        m1c = ctx.enter_context(nc.sbuf_tensor("m1c", [128, 1], F32))
        wtok = ctx.enter_context(nc.sbuf_tensor("wtok", [128, 1], F32))
        s_store = ctx.enter_context(nc.sbuf_tensor("s_store", [128, NT, 8], F32))
        srowf = ctx.enter_context(nc.sbuf_tensor("srowf", [1, 512], F32))
        srowb = ctx.enter_context(nc.sbuf_tensor("srowb", [1, 512], BF16))
        sbcast = ctx.enter_context(nc.sbuf_tensor("sbcast", [128, 512], BF16))
        scr = ctx.enter_context(nc.sbuf_tensor("scr", [128, 512], F32))
        scrB = ctx.enter_context(nc.sbuf_tensor("scrB", [128, 512], F32))
        yt = ctx.enter_context(nc.sbuf_tensor("yt", [128, 512], F32))
        ytB = ctx.enter_context(nc.sbuf_tensor("ytB", [128, 512], F32))
        onesb = ctx.enter_context(nc.sbuf_tensor("onesb", [128, 1], BF16))
        ones1 = ctx.enter_context(nc.sbuf_tensor("ones1", [1, 128], BF16))
        ones1f = ctx.enter_context(nc.sbuf_tensor("ones1f", [1, 128], F32))
        identb_s = ctx.enter_context(nc.sbuf_tensor("identb_s", [128, 128], BF16))
        identf_s = ctx.enter_context(nc.sbuf_tensor("identf_s", [128, 128], F32))
        ps_a = ctx.enter_context(nc.psum_tensor("ps_a", [128, 512], F32))
        ps_a2 = ctx.enter_context(nc.psum_tensor("ps_a2", [128, 512], F32))
        ps_b = ctx.enter_context(nc.psum_tensor("ps_b", [128, 512], F32))
        ps_b2 = ctx.enter_context(nc.psum_tensor("ps_b2", [128, 512], F32))
        ps_l = ctx.enter_context(nc.psum_tensor("ps_l", [128, 8], F32))
        ps_trf = ctx.enter_context(nc.psum_tensor("ps_trf", [128, 128], F32))
        ps_d = ctx.enter_context(nc.psum_tensor("ps_d", [128, 512], F32))
        ps_d2 = ctx.enter_context(nc.psum_tensor("ps_d2", [128, 512], F32))
        ld, pe, dv, ac, go, st, lg = sems

        E.step("vector", lambda eng: eng.memset(onesb[:, :], 1.0))
        E.step("vector", lambda eng: eng.memset(ones1[:, :], 1.0))
        E.step("vector", lambda eng: eng.memset(ones1f[:, :], 1.0))
        E.step("vector", lambda eng: eng.memset(cc[:, 0:1], EPS))
        E.step("vector", lambda eng: eng.memset(cc[:, 1:2], 1.0), sem=dv)
        E.dma("sync", identb_s[:, :], identb[:, :], ld)
        E.dma("sync", identf_s[:, :], identf[:, :], ld)
        for c in range(8):
            E.dma("sync", rw_s[:, c, :], router_w[128 * c:128 * (c + 1), :], ld)
        E.dma("sync", gb[0:1, 0:D], ln_w[:, :], ld)
        c_b = E.dma("sync", gb[0:1, D:2 * D], ln_b[:, :], ld)
        for half in range(4):
            def f_mm(eng, half=half):
                return eng.matmul(out=ps_a[:, :], lhsT=ones1f[:, :],
                                  rhs=gb[0:1, 512 * half:512 * (half + 1)], start=True, stop=True)
            E.step("tensor", f_mm, sem=pe, wait=[(ld, c_b), (dv, dv.n)])

            def f_ev(eng, half=half):
                return eng.tensor_copy(out=gb[:, 512 * half:512 * (half + 1)], in_=ps_a[:, :])
            E.step("vector", f_ev, sem=dv, wait=[(pe, pe.n)])

        for i in range(NT):
            c_x = E.dma("sync", xt[:, :], x[128 * i:128 * (i + 1), :], ld, wait=[(dv, dv.n)])
            emit_ln_stats(E, bn6, bnag, xt, D, stats_all[:, i, 0:1], stats_all[:, i, 1:2], dv, ac, cc[:, 0:1],
                          wait=[(ld, c_x)])

            def f_norm(eng, i=i):
                return eng.tensor_scalar(out=xa[:, :], in0=xt[:, :], scalar1=stats_all[:, i, 0:1],
                                         scalar2=stats_all[:, i, 1:2], op0=ALU.subtract, op1=ALU.mult)
            E.step("vector", f_norm, wait=[(ac, ac.n)])
            emit_ln_stats(E, bn6, bnag, xa, D, stats2[:, 0:1], stats2[:, 1:2], dv, ac, cc[:, 0:1])

            def f_norm2(eng):
                return eng.tensor_scalar(out=xa[:, :], in0=xa[:, :], scalar1=stats2[:, 0:1],
                                         scalar2=stats2[:, 1:2], op0=ALU.subtract, op1=ALU.mult)
            E.step("vector", f_norm2, wait=[(ac, ac.n)])

            def f_gam(eng):
                return eng.tensor_mul(out=xa[:, :], in0=xa[:, :], in1=gb[:, 0:D])
            E.step("vector", f_gam)

            def f_bet(eng):
                return eng.tensor_add(out=xa[:, :], in0=xa[:, :], in1=gb[:, D:2 * D])
            E.step("vector", f_bet, sem=dv)
            for c in range(8):
                def f_tr(eng, c=c):
                    return eng.transpose(out=ps_trf[:, :], in_=xa[:, 128 * c:128 * (c + 1)],
                                         identity=identf_s[:, :])
                E.step("tensor", f_tr, sem=pe, wait=[(dv, dv.n), (go, go.n)])

                def f_evt(eng):
                    return eng.tensor_copy(out=xfTc[:, :], in_=ps_trf[:, :])
                E.step("vector", f_evt, wait=[(pe, pe.n)])

                def f_cast(eng, c=c):
                    return eng.tensor_copy(out=xfTs[:, c, 0:128], in_=xfTc[:, :])
                E.step("vector", f_cast, sem=dv, wait=[(lg, lg.n)])

                def f_rmm(eng, c=c):
                    return eng.matmul(out=ps_l[:, :], lhsT=xfTc[:, :], rhs=rw_s[:, c, :],
                                      start=(c == 0), stop=(c == 7))
                E.step("tensor", f_rmm, sem=pe, wait=[(dv, dv.n)])
                E.dma("gpsimd", xfT_dram[c, :, 128 * i:128 * (i + 1)], xfTs[:, c, 0:128], lg,
                      wait=[(dv, dv.n)])
            def f_evl(eng):
                return eng.tensor_copy(out=lrow[:, :], in_=ps_l[:, :])
            E.step("vector", f_evl, wait=[(pe, pe.n)])

            def f_m1(eng):
                return eng.reduce_max(out=m1c[:, :], in_=lrow[:, :], axis=AX)
            E.step("vector", f_m1, sem=dv)

            def f_sig(eng):
                return eng.activation(out=wtok[:, :], in_=m1c[:, :], func=AF.Sigmoid)
            E.step("scalar", f_sig, sem=ac, wait=[(dv, dv.n)])

            def f_oh(eng, i=i):
                return eng.tensor_scalar(out=s_store[:, i, :], in0=lrow[:, :], scalar1=m1c[:, :],
                                         scalar2=None, op0=ALU.is_equal)
            E.step("vector", f_oh, wait=[(ac, ac.n)])

            def f_sc(eng, i=i):
                return eng.tensor_scalar(out=s_store[:, i, :], in0=s_store[:, i, :], scalar1=wtok[:, :],
                                         scalar2=None, op0=ALU.mult)
            E.step("vector", f_sc, sem=dv)

        for e in range(9):
            wprev = lg.n
            if e < 9:
                gsrc = gate_w[e, :, :] if e < 8 else sh_gate[:, :]
                usrc = up_w[e, :, :] if e < 8 else sh_up[:, :]
                dsrc = down_w[e, :, :] if e < 8 else sh_down[:, :]
            for c in range(8):
                E.dma("gpsimd", wg[:, c, :], gsrc[128 * c:128 * (c + 1), :], lg,
                      wait=[(pe, pe.n), (lg, wprev)])
                E.dma("gpsimd", wu[:, c, :], usrc[128 * c:128 * (c + 1), :], lg,
                      wait=[(pe, pe.n), (lg, wprev)])
            for c in range(16):
                E.dma("gpsimd", wd[:, c, :], dsrc[128 * c:128 * (c + 1), :], lg,
                      wait=[(pe, pe.n), (lg, wprev)])
            ld_w = lg.n
            for stp in range(NST):
                c_xf = 0
                for c in range(8):
                    c_xf = E.dma("sync", xfTs[:, c, 0:512], xfT_dram[c, :, 512 * stp:512 * (stp + 1)], ld,
                                 wait=[(pe, pe.n), (dv, dv.n), (lg, lg.n)])
                if e < 8:
                    for q in range(4):
                        tile = 4 * stp + q

                        def f_trs(eng, tile=tile, e=e):
                            return eng.transpose(out=ps_trf[0:1, 0:128], in_=s_store[:, tile, e:e + 1],
                                                 identity=identf_s[:, :])
                        E.step("tensor", f_trs, sem=pe, wait=[(dv, dv.n)])

                        def f_evs(eng, q=q):
                            return eng.tensor_copy(out=srowf[:, 128 * q:128 * (q + 1)], in_=ps_trf[0:1, 0:128])
                        E.step("vector", f_evs, sem=dv, wait=[(pe, pe.n)])

                    def f_sbf(eng):
                        return eng.tensor_copy(out=srowb[:, :], in_=srowf[:, :])
                    E.step("vector", f_sbf, sem=dv)

                    def f_sb(eng):
                        return eng.matmul(out=ps_b[:, :], lhsT=ones1[:, :], rhs=srowb[:, :],
                                          start=True, stop=True)
                    E.step("tensor", f_sb, sem=pe, wait=[(dv, dv.n)])

                    def f_evsb(eng):
                        return eng.tensor_copy(out=sbcast[:, :], in_=ps_b[:, :])
                    E.step("vector", f_evsb, sem=dv, wait=[(pe, pe.n), (ld, c_xf)])
                    for c in range(8):
                        def f_xe(eng, c=c):
                            return eng.tensor_mul(out=xeTs[:, c, :], in0=xfTs[:, c, :], in1=sbcast[:, :])
                        E.step("vector", f_xe, sem=dv)
                    xin = xeTs
                    xe_done = dv.n
                else:
                    E.emit("vector", None, wait=[(ld, c_xf)])
                    xin = xfTs
                    xe_done = dv.n
                silu_done = {}
                h_done = {}
                for ich in range(16):
                    pg = ps_a if ich % 2 == 0 else ps_a2
                    pu = ps_b if ich % 2 == 0 else ps_b2
                    sc = scr if ich % 2 == 0 else scrB

                    def f_zg(eng, ich=ich, xin=xin, pg=pg):
                        inst = None
                        for dch in range(8):
                            inst = eng.matmul(out=pg[:, :], lhsT=wg[:, dch, 128 * ich:128 * (ich + 1)],
                                              rhs=xin[:, dch, :], start=(dch == 0), stop=(dch == 7))
                        return inst
                    E.step("tensor", f_zg, sem=pe,
                           wait=[(lg, ld_w), (ld, c_xf), (dv, xe_done),
                                 (ac, silu_done.get(ich - 2, ac.n if ich < 2 else 0))])

                    def f_silu(eng, sc=sc, pg=pg):
                        return eng.activation(out=sc[:, :], in_=pg[:, :], func=AF.Silu)
                    silu_done[ich] = E.step("scalar", f_silu, sem=ac,
                                            wait=[(pe, pe.n), (dv, h_done.get(ich - 2, dv.n if ich < 2 else 0))])

                    def f_zu(eng, ich=ich, xin=xin, pu=pu):
                        inst = None
                        for dch in range(8):
                            inst = eng.matmul(out=pu[:, :], lhsT=wu[:, dch, 128 * ich:128 * (ich + 1)],
                                              rhs=xin[:, dch, :], start=(dch == 0), stop=(dch == 7))
                        return inst
                    E.step("tensor", f_zu, sem=pe,
                           wait=[(dv, h_done.get(ich - 2, dv.n if ich < 2 else 0))])

                    def f_h(eng, ich=ich, sc=sc, pu=pu):
                        return eng.tensor_mul(out=hT[:, ich, :], in0=sc[:, :], in1=pu[:, :])
                    h_done[ich] = E.step("vector", f_h, sem=dv, wait=[(pe, pe.n), (ac, silu_done[ich])])
                evy_done = {}
                acc_done = {}
                for k8, (tt, dh) in enumerate([(a, b) for a in range(4) for b in range(2)]):
                    pd = ps_d if k8 % 2 == 0 else ps_d2
                    yb = yt if k8 % 2 == 0 else ytB

                    def f_dn(eng, tt=tt, dh=dh, pd=pd):
                        inst = None
                        for ich in range(16):
                            inst = eng.matmul(out=pd[:, :], lhsT=hT[:, ich, 128 * tt:128 * (tt + 1)],
                                              rhs=wd[:, ich, 512 * dh:512 * (dh + 1)],
                                              start=(ich == 0), stop=(ich == 15))
                        return inst
                    E.step("tensor", f_dn, sem=pe,
                           wait=[(dv, dv.n if k8 < 2 else evy_done.get(k8 - 2, 0)),
                                 (go, go.n if k8 == 0 else 0)])

                    def f_evy(eng, pd=pd, yb=yb):
                        return eng.tensor_copy(out=yb[:, :], in_=pd[:, :])
                    evy_done[k8] = E.step("vector", f_evy, sem=dv,
                                          wait=[(pe, pe.n), (lg, lg.n)])
                    t0 = 512 * stp + 128 * tt

                    def f_acc(eng, t0=t0, dh=dh, e=e, yb=yb):
                        return eng.dma_start(out=routed_dram[t0:t0 + 128, 512 * dh:512 * (dh + 1)],
                                             in_=yb[:, :],
                                             accum_op=(ALU.bypass if e == 0 else ALU.add))
                    lg.n += 16
                    acc_done[k8] = lg.n
                    E.emit("gpsimd", f_acc, wait=[(dv, dv.n)], inc=[(lg, 16)])

        acc_done = lg.n
        for i in range(NT):
            c_x = E.dma("sync", xt[:, :], x[128 * i:128 * (i + 1), :], ld,
                        wait=[(dv, dv.n), (lg, acc_done)])
            c_r = E.dma("sync", rt[:, :], routed_dram[128 * i:128 * (i + 1), :], ld,
                        wait=[(st, st.n)])

            def f_h2(eng, i=i):
                return eng.tensor_scalar(out=xa[:, :], in0=xt[:, :], scalar1=stats_all[:, i, 0:1],
                                         scalar2=stats_all[:, i, 1:2], op0=ALU.subtract, op1=ALU.mult)
            E.step("vector", f_h2, wait=[(ld, c_r)])

            def f_add(eng):
                return eng.tensor_add(out=xa[:, :], in0=xa[:, :], in1=rt[:, :])
            E.step("vector", f_add)

            def f_out(eng):
                return eng.scalar_tensor_tensor(out=xout[:, :], in0=xa[:, :], scalar=0.5, in1=xt[:, :],
                                                op0=ALU.mult, op1=ALU.add)
            E.step("vector", f_out, sem=dv)
            E.dma("scalar", xo[128 * i:128 * (i + 1), :], xout[:, :], st, wait=[(dv, dv.n)])





def ffn1_io(nc, io):
    io["f1_ln_w"] = nc.dram_tensor("f1_ln_w", [1, D], F32, kind="ExternalInput").ap()
    io["f1_ln_b"] = nc.dram_tensor("f1_ln_b", [1, D], F32, kind="ExternalInput").ap()
    io["f1_router_w"] = nc.dram_tensor("f1_router_w", [D, 8], F32, kind="ExternalInput").ap()
    io["f1_gate_w"] = nc.dram_tensor("f1_gate_w", [8, D, 2048], F32, kind="ExternalInput").ap()
    io["f1_up_w"] = nc.dram_tensor("f1_up_w", [8, D, 2048], F32, kind="ExternalInput").ap()
    io["f1_down_w"] = nc.dram_tensor("f1_down_w", [8, 2048, D], F32, kind="ExternalInput").ap()
    io["f1_sh_gate"] = nc.dram_tensor("f1_sh_gate", [D, 2048], F32, kind="ExternalInput").ap()
    io["f1_sh_up"] = nc.dram_tensor("f1_sh_up", [D, 2048], F32, kind="ExternalInput").ap()
    io["f1_sh_down"] = nc.dram_tensor("f1_sh_down", [2048, D], F32, kind="ExternalInput").ap()


def build_ffn_l1():
    from contextlib import ExitStack
    nc = new_program()
    io = {"x": nc.dram_tensor("x", [TOK, D], F32, kind="ExternalInput").ap(),
          "identb": nc.dram_tensor("identb", [128, 128], BF16, kind="ExternalInput").ap(),
          "identf": nc.dram_tensor("identf", [128, 128], F32, kind="ExternalInput").ap(),
          "xo": nc.dram_tensor("xo", [TOK, D], F32, kind="ExternalOutput").ap(),
          "xfT_dram": nc.dram_tensor("xfT_dram", [8, 128, TOK], BF16, kind="Internal").ap(),
          "y_dram": nc.dram_tensor("y_dram", [TOK, D], F32, kind="Internal").ap()}
    ffn1_io(nc, io)
    with ExitStack() as octx:
        sems = make_sems(nc, octx)
        ld, pe, dv, ac, go, st, lg = sems
        E = Emitter(auto={"vector": dv, "scalar": ac, "gpsimd": go})
        emit_ffn_l1(nc, E, sems, io)
        finish_program(nc, octx, E, sems)
    return nc

def run_ffn_l1(xs, weights):
    return run_ffn_l1_cached(xs, weights)


def _run_ffn_l1_old(xs, weights):
    nc = build_ffn_l1()
    identb, rep, repsign, swap, pos = make_consts()
    identf = np.eye(128, dtype=np.float32)
    in_maps = []
    for i in range(NCORES):
        m = {"x": xs[i], "identb": identb, "identf": identf}
        m.update(weights)
        in_maps.append(m)
    res = run_bass_kernel_spmd(nc, in_maps, core_ids=list(range(NCORES)))
    return [r["xo"] for r in res.results]





def emit_ffn_l0(nc, E, sems, io, counts_only):
    x = io["x"]
    identf = io["identf"]
    router_w = io["f0_router_w"]
    ltri = io["ltri"]
    if not counts_only:
        identb = io["identb"]
        w1, b1, w2, b2 = io["f0_w1"], io["f0_b1"], io["f0_w2"], io["f0_b2"]
        cnt_all = io["cnt_all"]
        bmask = io["bmask"]
        xo = io["xo"]
        xfT_dram = io["xfT_dram"]
        y_dram = io["y_dram"]
    else:
        cnt_out = io["cnt_out"]

    NT = TOK // 128
    NST = TOK // 512
    TPB = S // 128
    CAP = float(CAP0)
    from contextlib import ExitStack
    with ExitStack() as ctx:
        if not counts_only:
            wA = ctx.enter_context(nc.sbuf_tensor("wA", [128, 8, 4096], BF16))
            wB = ctx.enter_context(nc.sbuf_tensor("wB", [128, 32, D], BF16))
            hT = ctx.enter_context(nc.sbuf_tensor("hT", [128, 32, 512], BF16))
            xfTs = ctx.enter_context(nc.sbuf_tensor("xfTs", [128, 8, 512], BF16))
            b1c = ctx.enter_context(nc.sbuf_tensor("b1c", [128, 32], F32))
            b1r = ctx.enter_context(nc.sbuf_tensor("b1r", [1, 4096], F32))
            b2r = ctx.enter_context(nc.sbuf_tensor("b2r", [1, D], BF16))
            geff_store = ctx.enter_context(nc.sbuf_tensor("geff_store", [128, NT, 8], F32))
            cnt_s = ctx.enter_context(nc.sbuf_tensor("cnt_s", [16, 8], F32))
            bmask_s = ctx.enter_context(nc.sbuf_tensor("bmask_s", [16, 2], F32))
            offr = ctx.enter_context(nc.sbuf_tensor("offr", [2, 8], F32))
            offr1 = ctx.enter_context(nc.sbuf_tensor("offr1", [1, 8], F32))
            obc = ctx.enter_context(nc.sbuf_tensor("obc", [128, 16], F32))
            yt = ctx.enter_context(nc.sbuf_tensor("yt", [128, 512], F32))
            ytB = ctx.enter_context(nc.sbuf_tensor("ytB", [128, 512], F32))
            identb_s = ctx.enter_context(nc.sbuf_tensor("identb_s", [128, 128], BF16))
            rt = ctx.enter_context(nc.sbuf_tensor("rt", [128, D], F32))
        xt = ctx.enter_context(nc.sbuf_tensor("xt", [128, D], F32))
        xa = ctx.enter_context(nc.sbuf_tensor("xa", [128, D], F32))
        xfTc = ctx.enter_context(nc.sbuf_tensor("xfTc", [128, 128], F32))
        xfb = ctx.enter_context(nc.sbuf_tensor("xfb", [128, 128], BF16))
        rw_s = ctx.enter_context(nc.sbuf_tensor("rw_s", [128, 8, 8], F32))
        ltri_s = ctx.enter_context(nc.sbuf_tensor("ltri_s", [128, 128], F32))
        bn6 = ctx.enter_context(nc.sbuf_tensor("bn6", [128, 12], F32))
        bnag = ctx.enter_context(nc.sbuf_tensor("bnag", [128, 2], F32))
        stats = ctx.enter_context(nc.sbuf_tensor("stats", [128, 2], F32))
        cc = ctx.enter_context(nc.sbuf_tensor("cc", [128, 2], F32))
        lpad = ctx.enter_context(nc.sbuf_tensor("lpad", [128, 16], F32))
        m8 = ctx.enter_context(nc.sbuf_tensor("m8", [128, 8], F32))
        oh1 = ctx.enter_context(nc.sbuf_tensor("oh1", [128, 8], F32))
        oh2 = ctx.enter_context(nc.sbuf_tensor("oh2", [128, 8], F32))
        ohb = ctx.enter_context(nc.sbuf_tensor("ohb", [128, 8], F32))
        cum = ctx.enter_context(nc.sbuf_tensor("cum", [128, 8], F32))
        carry = ctx.enter_context(nc.sbuf_tensor("carry", [1, 8], F32))
        tsc = ctx.enter_context(nc.sbuf_tensor("tsc", [128, 8], F32))
        col = ctx.enter_context(nc.sbuf_tensor("col", [128, 8], F32))
        onesf = ctx.enter_context(nc.sbuf_tensor("onesf", [128, 1], F32))
        ones1 = ctx.enter_context(nc.sbuf_tensor("ones1", [1, 128], BF16))
        ones1f = ctx.enter_context(nc.sbuf_tensor("ones1f", [1, 128], F32))
        identf_s = ctx.enter_context(nc.sbuf_tensor("identf_s", [128, 128], F32))
        ps_a = ctx.enter_context(nc.psum_tensor("ps_a", [128, 512], F32))
        ps_a2 = ctx.enter_context(nc.psum_tensor("ps_a2", [128, 512], F32))
        ps_d = ctx.enter_context(nc.psum_tensor("ps_d", [128, 512], F32))
        ps_d2 = ctx.enter_context(nc.psum_tensor("ps_d2", [128, 512], F32))
        ps_l = ctx.enter_context(nc.psum_tensor("ps_l", [128, 8], F32))
        ps_cum = ctx.enter_context(nc.psum_tensor("ps_cum", [128, 8], F32))
        ps_cnt = ctx.enter_context(nc.psum_tensor("ps_cnt", [1, 8], F32))
        ps_trf = ctx.enter_context(nc.psum_tensor("ps_trf", [128, 128], F32))
        ld, pe, dv, ac, go, st, lg = sems

        E.step("vector", lambda eng: eng.memset(onesf[:, :], 1.0))
        E.step("vector", lambda eng: eng.memset(ones1[:, :], 1.0))
        E.step("vector", lambda eng: eng.memset(ones1f[:, :], 1.0))
        E.step("vector", lambda eng: eng.memset(cc[:, 0:1], EPS))
        E.step("vector", lambda eng: eng.memset(cc[:, 1:2], 1.0))
        E.step("vector", lambda eng: eng.memset(lpad[:, 8:16], -1e30))
        E.step("vector", lambda eng: eng.memset(carry[:, :], 0.0), sem=dv)
        E.dma("sync", identf_s[:, :], identf[:, :], ld)
        E.dma("sync", ltri_s[:, :], ltri[:, :], ld)
        for c in range(8):
            E.dma("sync", rw_s[:, c, :], router_w[128 * c:128 * (c + 1), :], ld)
        if not counts_only:
            E.dma("sync", identb_s[:, :], identb[:, :], ld)
            E.dma("sync", cnt_s[:, :], cnt_all[:, :], ld)
            c_bm = E.dma("sync", bmask_s[:, :], bmask[:, :], ld)
            def f_off(eng):
                return eng.matmul(out=ps_cnt[0:1, 0:8] if False else ps_l[0:8, 0:2], lhsT=cnt_s[:, :],
                                  rhs=bmask_s[:, :], start=True, stop=True)
            E.step("tensor", f_off, sem=pe, wait=[(ld, c_bm)])

            def f_evoff(eng):
                return eng.tensor_copy(out=offr[:, :] if False else col[0:8, 0:2], in_=ps_l[0:8, 0:2])
            E.step("vector", f_evoff, sem=dv, wait=[(pe, pe.n)])

            def f_troff(eng):
                return eng.transpose(out=ps_trf[0:2, 0:8], in_=col[0:8, 0:2], identity=identf_s[0:8, 0:8])
            E.step("tensor", f_troff, sem=pe, wait=[(dv, dv.n)])

            def f_evoff2(eng):
                return eng.tensor_copy(out=offr[:, :], in_=ps_trf[0:2, 0:8])
            E.step("vector", f_evoff2, sem=dv, wait=[(pe, pe.n)])
            E.dma("gpsimd", offr1[:, :], offr[1:2, :], lg, wait=[(dv, dv.n)])
            def f_ob0(eng):
                return eng.matmul(out=ps_l[:, 0:8], lhsT=ones1f[:, :], rhs=offr[0:1, :], start=True, stop=True)
            E.step("tensor", f_ob0, sem=pe, wait=[(lg, lg.n), (dv, dv.n)])

            def f_evob0(eng):
                return eng.tensor_copy(out=obc[:, 0:8], in_=ps_l[:, 0:8])
            E.step("vector", f_evob0, sem=dv, wait=[(pe, pe.n)])

            def f_ob1(eng):
                return eng.matmul(out=ps_l[:, 0:8], lhsT=ones1f[:, :], rhs=offr1[0:1, :], start=True, stop=True)
            E.step("tensor", f_ob1, sem=pe, wait=[(dv, dv.n)])

            def f_evob1(eng):
                return eng.tensor_copy(out=obc[:, 8:16], in_=ps_l[:, 0:8])
            E.step("vector", f_evob1, sem=dv, wait=[(pe, pe.n)])

        for i in range(NT):
            b_loc = i // TPB
            ib = i % TPB
            c_x = E.dma("sync", xt[:, :], x[128 * i:128 * (i + 1), :], ld, wait=[(dv, dv.n)])
            emit_ln_stats(E, bn6, bnag, xt, D, stats[:, 0:1], stats[:, 1:2], dv, ac, cc[:, 0:1],
                          wait=[(ld, c_x)])

            def f_norm(eng):
                return eng.tensor_scalar(out=xa[:, :], in0=xt[:, :], scalar1=stats[:, 0:1],
                                         scalar2=stats[:, 1:2], op0=ALU.subtract, op1=ALU.mult)
            E.step("vector", f_norm, sem=dv, wait=[(ac, ac.n)])
            for c in range(8):
                def f_tr(eng, c=c):
                    return eng.transpose(out=ps_trf[:, :], in_=xa[:, 128 * c:128 * (c + 1)],
                                         identity=identf_s[:, :])
                E.step("tensor", f_tr, sem=pe, wait=[(dv, dv.n), (go, go.n)])

                def f_evt(eng):
                    return eng.tensor_copy(out=xfTc[:, :], in_=ps_trf[:, :])
                E.step("vector", f_evt, wait=[(pe, pe.n)])
                if not counts_only:
                    def f_cast(eng, c=c):
                        return eng.tensor_copy(out=xfb[:, :], in_=xfTc[:, :])
                    E.step("vector", f_cast, sem=dv, wait=[(lg, lg.n)])
                    E.dma("gpsimd", xfT_dram[c, :, 128 * i:128 * (i + 1)], xfb[:, :], lg,
                          wait=[(dv, dv.n)])
                else:
                    E.step("vector", lambda eng: eng.engine_nop(), sem=dv)

                def f_rmm(eng, c=c):
                    return eng.matmul(out=ps_l[:, 0:8], lhsT=xfTc[:, :], rhs=rw_s[:, c, :],
                                      start=(c == 0), stop=(c == 7))
                E.step("tensor", f_rmm, sem=pe, wait=[(dv, dv.n)])

            def f_evl(eng):
                return eng.tensor_copy(out=lpad[:, 0:8], in_=ps_l[:, 0:8])
            E.step("vector", f_evl, wait=[(pe, pe.n)])

            def f_max8(eng):
                return eng.max(out=m8[:, :], in_=lpad[:, :])
            E.step("vector", f_max8)

            def f_oh1(eng):
                return eng.tensor_scalar(out=oh1[:, :], in0=lpad[:, 0:8], scalar1=m8[:, 0:1],
                                         scalar2=None, op0=ALU.is_equal)
            E.step("vector", f_oh1)

            def f_ohm(eng):
                return eng.tensor_scalar(out=oh2[:, :], in0=lpad[:, 0:8], scalar1=m8[:, 1:2],
                                         scalar2=None, op0=ALU.is_equal)
            E.step("vector", f_ohm)

            def f_not1(eng):
                return eng.tensor_scalar(out=tsc[:, :], in0=oh1[:, :], scalar1=-1.0, scalar2=1.0,
                                         op0=ALU.mult, op1=ALU.add)
            E.step("vector", f_not1)

            def f_oh2(eng):
                return eng.tensor_mul(out=oh2[:, :], in0=oh2[:, :], in1=tsc[:, :])
            E.step("vector", f_oh2)

            def f_ohb(eng):
                return eng.tensor_add(out=ohb[:, :], in0=oh1[:, :], in1=oh2[:, :])
            E.step("vector", f_ohb, sem=dv)

            def f_cum(eng, ib=ib):
                inst = eng.matmul(out=ps_cum[:, 0:8], lhsT=ltri_s[:, :], rhs=ohb[:, :],
                                  start=True, stop=(ib == 0))
                if ib != 0:
                    inst = eng.matmul(out=ps_cum[:, 0:8], lhsT=ones1f[:, :], rhs=carry[:, :],
                                      start=False, stop=True)
                return inst
            E.step("tensor", f_cum, sem=pe, wait=[(dv, dv.n)])

            def f_ccnt(eng):
                return eng.matmul(out=ps_cnt[:, 0:8], lhsT=onesf[:, :], rhs=ohb[:, :],
                                  start=True, stop=True)
            E.step("tensor", f_ccnt, sem=pe)

            def f_evcum(eng):
                return eng.tensor_copy(out=cum[:, :], in_=ps_cum[:, 0:8])
            E.step("vector", f_evcum, wait=[(pe, pe.n)])

            if ib == 0:
                def f_carry(eng):
                    return eng.tensor_copy(out=carry[:, :], in_=ps_cnt[:, 0:8])
                E.step("vector", f_carry, sem=dv, wait=[(lg, lg.n)])
            else:
                def f_carry(eng):
                    return eng.tensor_add(out=carry[:, :], in0=carry[:, :], in1=ps_cnt[:, 0:8])
                E.step("vector", f_carry, sem=dv)

            if counts_only:
                if ib == TPB - 1:
                    E.dma("gpsimd", cnt_out[b_loc:b_loc + 1, :], carry[:, :], lg, wait=[(dv, dv.n)])
                continue

            def f_p1(eng):
                return eng.tensor_mul(out=tsc[:, :], in0=oh1[:, :], in1=cum[:, :])
            E.step("vector", f_p1)

            def f_p1s(eng):
                return eng.reduce_sum(out=col[:, 0:1], in_=tsc[:, :], axis=AX)
            E.step("vector", f_p1s)

            def f_o1(eng, b_loc=b_loc):
                return eng.tensor_mul(out=tsc[:, :], in0=oh1[:, :], in1=obc[:, 8 * b_loc:8 * (b_loc + 1)])
            E.step("vector", f_o1)

            def f_o1s(eng):
                return eng.reduce_sum(out=col[:, 1:2], in_=tsc[:, :], axis=AX)
            E.step("vector", f_o1s)

            def f_p2(eng):
                return eng.tensor_mul(out=tsc[:, :], in0=oh2[:, :], in1=cum[:, :])
            E.step("vector", f_p2)

            def f_p2s(eng):
                return eng.reduce_sum(out=col[:, 2:3], in_=tsc[:, :], axis=AX)
            E.step("vector", f_p2s)

            def f_o2(eng, b_loc=b_loc):
                return eng.tensor_mul(out=tsc[:, :], in0=oh2[:, :], in1=obc[:, 8 * b_loc:8 * (b_loc + 1)])
            E.step("vector", f_o2)

            def f_o2s(eng):
                return eng.reduce_sum(out=col[:, 3:4], in_=tsc[:, :], axis=AX)
            E.step("vector", f_o2s)

            def f_k1(eng):
                return eng.tensor_add(out=col[:, 4:5], in0=col[:, 0:1], in1=col[:, 1:2])
            E.step("vector", f_k1)

            def f_k1b(eng):
                return eng.tensor_scalar(out=col[:, 4:5], in0=col[:, 4:5], scalar1=-1.0, scalar2=CAP,
                                         op0=ALU.mult, op1=ALU.add)
            E.step("vector", f_k1b)

            def f_k1c(eng):
                return eng.tensor_scalar_max(col[:, 4:5], col[:, 4:5], 0.0)
            E.step("vector", f_k1c)

            def f_k1d(eng):
                return eng.tensor_scalar_min(col[:, 4:5], col[:, 4:5], 1.0)
            E.step("vector", f_k1d)

            def f_k2(eng):
                return eng.tensor_add(out=col[:, 5:6], in0=col[:, 2:3], in1=col[:, 3:4])
            E.step("vector", f_k2)

            def f_k2b(eng):
                return eng.tensor_scalar(out=col[:, 5:6], in0=col[:, 5:6], scalar1=-1.0, scalar2=CAP,
                                         op0=ALU.mult, op1=ALU.add)
            E.step("vector", f_k2b)

            def f_k2c(eng):
                return eng.tensor_scalar_max(col[:, 5:6], col[:, 5:6], 0.0)
            E.step("vector", f_k2c)

            def f_k2d(eng):
                return eng.tensor_scalar_min(col[:, 5:6], col[:, 5:6], 1.0)
            E.step("vector", f_k2d)

            def f_d(eng):
                return eng.tensor_sub(out=col[:, 6:7], in0=m8[:, 1:2], in1=m8[:, 0:1])
            E.step("vector", f_d, sem=dv)

            def f_e(eng):
                return eng.activation(out=col[:, 6:7], in_=col[:, 6:7], func=AF.Exp)
            E.step("scalar", f_e, sem=ac, wait=[(dv, dv.n)])

            def f_g1(eng):
                return eng.tensor_scalar_add(col[:, 7:8], col[:, 6:7], 1.0)
            E.step("vector", f_g1, wait=[(ac, ac.n)])

            def f_g1b(eng):
                return eng.reciprocal(out=col[:, 6:7], in_=col[:, 7:8])
            E.step("vector", f_g1b)

            def f_g2(eng):
                return eng.tensor_scalar(out=col[:, 7:8], in0=col[:, 6:7], scalar1=-1.0, scalar2=1.0,
                                         op0=ALU.mult, op1=ALU.add)
            E.step("vector", f_g2)

            def f_gk1(eng):
                return eng.tensor_mul(out=col[:, 4:5], in0=col[:, 4:5], in1=col[:, 6:7])
            E.step("vector", f_gk1)

            def f_gk2(eng):
                return eng.tensor_mul(out=col[:, 5:6], in0=col[:, 5:6], in1=col[:, 7:8])
            E.step("vector", f_gk2)

            def f_ge1(eng, i=i):
                return eng.tensor_scalar(out=geff_store[:, i, :], in0=oh1[:, :], scalar1=col[:, 4:5],
                                         scalar2=None, op0=ALU.mult)
            E.step("vector", f_ge1)

            def f_ge2(eng):
                return eng.tensor_scalar(out=tsc[:, :], in0=oh2[:, :], scalar1=col[:, 5:6],
                                         scalar2=None, op0=ALU.mult)
            E.step("vector", f_ge2)

            def f_ge(eng, i=i):
                return eng.tensor_add(out=geff_store[:, i, :], in0=geff_store[:, i, :], in1=tsc[:, :])
            E.step("vector", f_ge, sem=dv)

        if counts_only:
            return

        for e in range(8):
            wprev = lg.n
            for c in range(8):
                E.dma("gpsimd", wA[:, c, :], w1[e, 128 * c:128 * (c + 1), :], lg,
                      wait=[(pe, pe.n), (lg, wprev)])
            for c in range(32):
                E.dma("gpsimd", wB[:, c, :], w2[e, 128 * c:128 * (c + 1), :], lg,
                      wait=[(pe, pe.n), (lg, wprev)])
            E.dma("sync", b1r[:, :], b1[e:e + 1, :], ld, wait=[(pe, pe.n)])
            c_b2 = E.dma("gpsimd", b2r[:, :], b2[e:e + 1, :], lg, wait=[(pe, pe.n), (lg, wprev)])
            ld_w = lg.n
            for c in range(32):
                def f_trb(eng, c=c):
                    return eng.transpose(out=ps_trf[0:128, 0:1], in_=b1r[:, 128 * c:128 * (c + 1)],
                                         identity=identf_s[0:1, 0:1])
                E.step("tensor", f_trb, sem=pe, wait=[(ld, ld.n), (ac, ac.n), (dv, dv.n)])

                def f_evb(eng, c=c):
                    return eng.tensor_copy(out=b1c[:, c:c + 1], in_=ps_trf[0:128, 0:1])
                E.step("vector", f_evb, sem=dv, wait=[(pe, pe.n)])
            for stp in range(NST):
                c_xf = 0
                for c in range(8):
                    c_xf = E.dma("sync", xfTs[:, c, 0:512], xfT_dram[c, :, 512 * stp:512 * (stp + 1)], ld,
                                 wait=[(pe, pe.n), (dv, dv.n), (lg, lg.n)])
                relu_done = {}
                for ich in range(32):
                    pz = ps_a if ich % 2 == 0 else ps_a2

                    def f_z(eng, ich=ich, pz=pz):
                        inst = None
                        for dch in range(8):
                            inst = eng.matmul(out=pz[:, :], lhsT=wA[:, dch, 128 * ich:128 * (ich + 1)],
                                              rhs=xfTs[:, dch, :], start=(dch == 0), stop=(dch == 7))
                        return inst
                    E.step("tensor", f_z, sem=pe,
                           wait=[(lg, ld_w), (ld, c_xf),
                                 (ac, relu_done.get(ich - 2, ac.n if ich < 2 else 0)),
                                 (dv, dv.n if ich == 0 else 0)])

                    def f_relu(eng, ich=ich, pz=pz):
                        return eng.activation(out=hT[:, ich, :], in_=pz[:, :], func=AF.Relu,
                                              bias=b1c[:, ich:ich + 1], scale=1.0)
                    relu_done[ich] = E.step("scalar", f_relu, sem=ac,
                                            wait=[(pe, pe.n), (dv, dv.n if ich == 0 else 0)])
                evy_done = {}
                for k8, (tt, dh) in enumerate([(a, b) for a in range(4) for b in range(2)]):
                    tile = 4 * stp + tt
                    pd = ps_d if k8 % 2 == 0 else ps_d2
                    yb = yt if k8 % 2 == 0 else ytB

                    def f_dn(eng, tt=tt, dh=dh, pd=pd):
                        inst = None
                        for ich in range(32):
                            inst = eng.matmul(out=pd[:, :], lhsT=hT[:, ich, 128 * tt:128 * (tt + 1)],
                                              rhs=wB[:, ich, 512 * dh:512 * (dh + 1)],
                                              start=(ich == 0), stop=False)
                        return eng.matmul(out=pd[:, :], lhsT=ones1[:, :],
                                          rhs=b2r[:, 512 * dh:512 * (dh + 1)], start=False, stop=True)
                    E.step("tensor", f_dn, sem=pe,
                           wait=[(ac, ac.n if k8 == 0 else 0), (lg, c_b2),
                                 (dv, dv.n if k8 < 2 else evy_done.get(k8 - 2, 0))])

                    def f_evy(eng, tile=tile, e=e, pd=pd, yb=yb):
                        return eng.tensor_scalar(out=yb[:, :], in0=pd[:, :],
                                                 scalar1=geff_store[:, tile, e:e + 1],
                                                 scalar2=None, op0=ALU.mult)
                    evy_done[k8] = E.step("vector", f_evy, sem=dv, wait=[(pe, pe.n), (lg, lg.n)])
                    t0 = 512 * stp + 128 * tt

                    def f_acc(eng, t0=t0, dh=dh, e=e, yb=yb):
                        return eng.dma_start(out=y_dram[t0:t0 + 128, 512 * dh:512 * (dh + 1)],
                                             in_=yb[:, :],
                                             accum_op=(ALU.bypass if e == 0 else ALU.add))
                    lg.n += 16
                    E.emit("gpsimd", f_acc, wait=[(dv, dv.n)], inc=[(lg, 16)])

        acc_done = lg.n
        for i in range(NT):
            c_x = E.dma("sync", xt[:, :], x[128 * i:128 * (i + 1), :], ld,
                        wait=[(dv, dv.n), (lg, acc_done)])
            c_r = E.dma("sync", rt[:, :], y_dram[128 * i:128 * (i + 1), :], ld,
                        wait=[(st, st.n)])

            def f_out(eng):
                return eng.scalar_tensor_tensor(out=xa[:, :], in0=rt[:, :], scalar=0.5, in1=xt[:, :],
                                                op0=ALU.mult, op1=ALU.add)
            E.step("vector", f_out, sem=dv, wait=[(ld, c_r)])
            E.dma("scalar", xo[128 * i:128 * (i + 1), :], xa[:, :], st, wait=[(dv, dv.n)])





def ffn0_io(nc, io):
    io["f0_router_w"] = nc.dram_tensor("f0_router_w", [D, 8], F32, kind="ExternalInput").ap()
    io["f0_w1"] = nc.dram_tensor("f0_w1", [8, D, 4096], F32, kind="ExternalInput").ap()
    io["f0_b1"] = nc.dram_tensor("f0_b1", [8, 4096], F32, kind="ExternalInput").ap()
    io["f0_w2"] = nc.dram_tensor("f0_w2", [8, 4096, D], F32, kind="ExternalInput").ap()
    io["f0_b2"] = nc.dram_tensor("f0_b2", [8, D], F32, kind="ExternalInput").ap()


def build_ffn_l0(counts_only):
    from contextlib import ExitStack
    nc = new_program()
    io = {"x": nc.dram_tensor("x", [TOK, D], F32, kind="ExternalInput").ap(),
          "identf": nc.dram_tensor("identf", [128, 128], F32, kind="ExternalInput").ap(),
          "f0_router_w": nc.dram_tensor("f0_router_w", [D, 8], F32, kind="ExternalInput").ap(),
          "ltri": nc.dram_tensor("ltri", [128, 128], F32, kind="ExternalInput").ap()}
    if counts_only:
        io["cnt_out"] = nc.dram_tensor("cnt_out", [2, 8], F32, kind="ExternalOutput").ap()
    else:
        io["identb"] = nc.dram_tensor("identb", [128, 128], BF16, kind="ExternalInput").ap()
        for n, sh in (("f0_w1", [8, D, 4096]), ("f0_b1", [8, 4096]), ("f0_w2", [8, 4096, D]),
                      ("f0_b2", [8, D])):
            io[n] = nc.dram_tensor(n, sh, F32, kind="ExternalInput").ap()
        io["cnt_all"] = nc.dram_tensor("cnt_all", [16, 8], F32, kind="ExternalInput").ap()
        io["bmask"] = nc.dram_tensor("bmask", [16, 2], F32, kind="ExternalInput").ap()
        io["xo"] = nc.dram_tensor("xo", [TOK, D], F32, kind="ExternalOutput").ap()
        io["xfT_dram"] = nc.dram_tensor("xfT_dram", [8, 128, TOK], BF16, kind="Internal").ap()
        io["y_dram"] = nc.dram_tensor("y_dram", [TOK, D], F32, kind="Internal").ap()
    with ExitStack() as octx:
        sems = make_sems(nc, octx)
        ld, pe, dv, ac, go, st, lg = sems
        E = Emitter(auto={"vector": dv, "scalar": ac, "gpsimd": go})
        emit_ffn_l0(nc, E, sems, io, counts_only)
        finish_program(nc, octx, E, sems)
    return nc

def make_l0_consts():
    identf = np.eye(128, dtype=np.float32)
    ltri = np.tril(np.ones((128, 128), np.float32), -1).T
    return identf, ltri


def make_bmask(core):
    bm = np.zeros((16, 2), np.float32)
    for b_loc, gb_ in enumerate(CORE_BATCHES[core]):
        bm[:gb_, b_loc] = 1.0
    return bm


_NC_CACHE = {}

def _get_nc(key, builder, *args):
    if key not in _NC_CACHE:
        _NC_CACHE[key] = builder(*args)
    return _NC_CACHE[key]


def _run_l0_ffn(xs, l0w, identf, ltri, identb):
    nc = _get_nc("l0cnt", build_ffn_l0, True)
    in_maps = [{"x": xs[i], "identf": identf, "ltri": ltri, "f0_router_w": l0w["router_w"]}
               for i in range(NCORES)]
    res = run_bass_kernel_spmd(nc, in_maps, core_ids=list(range(NCORES)))
    cnt_all = np.zeros((16, 8), np.float32)
    for i in range(NCORES):
        c = res.results[i]["cnt_out"]
        cnt_all[CORE_BATCHES[i][0]] = c[0]
        cnt_all[CORE_BATCHES[i][1]] = c[1]
    nc = _get_nc("l0ffn", build_ffn_l0, False)
    in_maps = []
    for i in range(NCORES):
        m = {"x": xs[i], "identf": identf, "identb": identb, "ltri": ltri,
             "cnt_all": cnt_all, "bmask": make_bmask(i)}
        m.update({"f0_" + k: v for k, v in l0w.items()})
        in_maps.append(m)
    res = run_bass_kernel_spmd(nc, in_maps, core_ids=list(range(NCORES)))
    return [r["xo"] for r in res.results]





def attn_io(nc, io, layer):
    if layer == 1:
        for n, sh in (("wq", [D, D]), ("wk", [D, D]), ("wv", [D, D]), ("wo", [D, D]),
                      ("gamma", [1, D]), ("beta", [1, D])):
            io[n] = nc.dram_tensor(n, sh, F32, kind="ExternalInput").ap()
    else:
        io["wqkvT"] = nc.dram_tensor("wqkvT", [D, 3 * D], F32, kind="ExternalInput").ap()
        io["woT"] = nc.dram_tensor("woT", [D, D], F32, kind="ExternalInput").ap()
        io["freqs"] = nc.dram_tensor("freqs", [1, 32], F32, kind="ExternalInput").ap()
        io["repmat"] = nc.dram_tensor("repmat", [32, 128], BF16, kind="ExternalInput").ap()
        io["repsign"] = nc.dram_tensor("repsign", [32, 128], BF16, kind="ExternalInput").ap()
        io["swapm"] = nc.dram_tensor("swapm", [128, 128], BF16, kind="ExternalInput").ap()
        io["pa_rows"] = nc.dram_tensor("pa_rows", [2, 16], F32, kind="ExternalInput").ap()


def build_part1():
    from contextlib import ExitStack
    nc = new_program()
    io = {"x": nc.dram_tensor("x", [TOK, D], F32, kind="ExternalInput").ap(),
          "identb": nc.dram_tensor("identb", [128, 128], BF16, kind="ExternalInput").ap(),
          "identf": nc.dram_tensor("identf", [128, 128], F32, kind="ExternalInput").ap(),
          "ltri": nc.dram_tensor("ltri", [128, 128], F32, kind="ExternalInput").ap(),
          "cnt_all": nc.dram_tensor("cnt_all", [16, 8], F32, kind="ExternalInput").ap(),
          "bmask": nc.dram_tensor("bmask", [16, 2], F32, kind="ExternalInput").ap(),
          "xo": nc.dram_tensor("xo", [TOK, D], F32, kind="ExternalOutput").ap(),
          "cnt_out": nc.dram_tensor("cnt_out", [2, 8], F32, kind="ExternalOutput").ap(),
          "xfT_dram": nc.dram_tensor("xfT_dram", [8, 128, TOK], BF16, kind="Internal").ap(),
          "y_dram": nc.dram_tensor("y_dram", [TOK, D], F32, kind="Internal").ap(),
          "xmid": nc.dram_tensor("xmid", [TOK, D], F32, kind="Internal").ap()}
    ffn0_io(nc, io)
    attn_io(nc, io, 0)
    with ExitStack() as octx:
        sems = make_sems(nc, octx)
        ld, pe, dv, ac, go, st, lg = sems
        E = Emitter(auto={"vector": dv, "scalar": ac, "gpsimd": go})
        io1 = dict(io); io1["xo"] = io["xmid"]
        emit_ffn_l0(NcPrefix(nc, "s1_"), E, sems, io1, False)
        full_barrier(E, sems)
        io2 = dict(io); io2["x"] = io["xmid"]
        emit_attn(NcPrefix(nc, "s2_"), E, sems, io2, 0)
        full_barrier(E, sems)
        io3 = dict(io); io3["x"] = io["xo"]
        emit_ffn_l0(NcPrefix(nc, "s3_"), E, sems, io3, True)
        finish_program(nc, octx, E, sems)
    return nc


def build_part2():
    from contextlib import ExitStack
    nc = new_program()
    io = {"x": nc.dram_tensor("x", [TOK, D], F32, kind="ExternalInput").ap(),
          "identb": nc.dram_tensor("identb", [128, 128], BF16, kind="ExternalInput").ap(),
          "identf": nc.dram_tensor("identf", [128, 128], F32, kind="ExternalInput").ap(),
          "ltri": nc.dram_tensor("ltri", [128, 128], F32, kind="ExternalInput").ap(),
          "cnt_all": nc.dram_tensor("cnt_all", [16, 8], F32, kind="ExternalInput").ap(),
          "bmask": nc.dram_tensor("bmask", [16, 2], F32, kind="ExternalInput").ap(),
          "xo": nc.dram_tensor("xo", [TOK, D], F32, kind="ExternalOutput").ap(),
          "xfT_dram": nc.dram_tensor("xfT_dram", [8, 128, TOK], BF16, kind="Internal").ap(),
          "y_dram": nc.dram_tensor("y_dram", [TOK, D], F32, kind="Internal").ap(),
          "xmid": nc.dram_tensor("xmid", [TOK, D], F32, kind="Internal").ap(),
          "xmid2": nc.dram_tensor("xmid2", [TOK, D], F32, kind="Internal").ap()}
    ffn0_io(nc, io)
    ffn1_io(nc, io)
    attn_io(nc, io, 1)
    with ExitStack() as octx:
        sems = make_sems(nc, octx)
        ld, pe, dv, ac, go, st, lg = sems
        E = Emitter(auto={"vector": dv, "scalar": ac, "gpsimd": go})
        io1 = dict(io); io1["xo"] = io["xmid"]
        emit_ffn_l0(NcPrefix(nc, "s1_"), E, sems, io1, False)
        full_barrier(E, sems)
        io2 = dict(io); io2["x"] = io["xmid"]; io2["xo"] = io["xmid2"]
        emit_ffn_l1(NcPrefix(nc, "s2_"), E, sems, io2)
        full_barrier(E, sems)
        io3 = dict(io); io3["x"] = io["xmid2"]; io3["xo"] = io["xmid"]
        emit_attn(NcPrefix(nc, "s3_"), E, sems, io3, 1)
        full_barrier(E, sems)
        io4 = dict(io); io4["x"] = io["xmid"]
        emit_ffn_l1(NcPrefix(nc, "s4_"), E, sems, io4)
        finish_program(nc, octx, E, sems)
    return nc


def kernel(**inputs):
    f32 = lambda a: np.ascontiguousarray(np.asarray(a), dtype=np.float32)
    x = f32(inputs["L0_attn_x"])
    identb, rep, repsign, swap, pos = make_consts()
    identf, ltri = make_l0_consts()
    common = {"identb": identb, "identf": identf, "ltri": ltri}
    f0 = dict(f0_router_w=f32(inputs["L0_ffn_w_router"]), f0_w1=f32(inputs["L0_ffn_w1"]),
              f0_b1=f32(inputs["L0_ffn_b1"]), f0_w2=f32(inputs["L0_ffn_w2"]),
              f0_b2=f32(inputs["L0_ffn_b2"]))
    a0 = dict(wqkvT=f32(inputs["L0_attn_qkv_w"]).T.copy(),
              woT=f32(inputs["L0_attn_out_w"]).T.copy(),
              freqs=f32(inputs["L0_attn_rope_freqs"]).reshape(1, 32),
              repmat=rep, repsign=repsign, swapm=swap,
              pa_rows=np.stack([f32(inputs["L0_attn_bias_p"]).reshape(16),
                                f32(inputs["L0_attn_bias_a"]).reshape(16)]).astype(np.float32))
    a1 = dict(wq=f32(inputs["L1_attn_Wq"]), wk=f32(inputs["L1_attn_Wk"]),
              wv=f32(inputs["L1_attn_Wv"]), wo=f32(inputs["L1_attn_Wo"]),
              gamma=f32(inputs["L1_attn_ln_gamma"]).reshape(1, D),
              beta=f32(inputs["L1_attn_ln_beta"]).reshape(1, D))
    f1 = dict(f1_ln_w=f32(inputs["L1_ffn_ln_w"]).reshape(1, D),
              f1_ln_b=f32(inputs["L1_ffn_ln_b"]).reshape(1, D),
              f1_router_w=f32(inputs["L1_ffn_router_w"]), f1_gate_w=f32(inputs["L1_ffn_gate_w"]),
              f1_up_w=f32(inputs["L1_ffn_up_w"]), f1_down_w=f32(inputs["L1_ffn_down_w"]),
              f1_sh_gate=f32(inputs["L1_ffn_sh_gate"]), f1_sh_up=f32(inputs["L1_ffn_sh_up"]),
              f1_sh_down=f32(inputs["L1_ffn_sh_down"]))
    xs = shard_x(x)

    nc = _get_nc("l0cnt", build_ffn_l0, True)
    in_maps = [{"x": xs[i], "identf": identf, "ltri": ltri, "f0_router_w": f0["f0_router_w"]}
               for i in range(NCORES)]
    res = run_bass_kernel_spmd(nc, in_maps, core_ids=list(range(NCORES)))
    cnt1 = np.zeros((16, 8), np.float32)
    for i in range(NCORES):
        c = res.results[i]["cnt_out"]
        cnt1[CORE_BATCHES[i][0]] = c[0]
        cnt1[CORE_BATCHES[i][1]] = c[1]

    nc = _get_nc("part1", build_part1)
    in_maps = []
    for i in range(NCORES):
        m = {"x": xs[i], "cnt_all": cnt1, "bmask": make_bmask(i)}
        m.update(common); m.update(f0); m.update(a0)
        in_maps.append(m)
    res = run_bass_kernel_spmd(nc, in_maps, core_ids=list(range(NCORES)))
    xs = [r["xo"] for r in res.results]
    cnt2 = np.zeros((16, 8), np.float32)
    for i in range(NCORES):
        c = res.results[i]["cnt_out"]
        cnt2[CORE_BATCHES[i][0]] = c[0]
        cnt2[CORE_BATCHES[i][1]] = c[1]

    nc = _get_nc("part2", build_part2)
    in_maps = []
    for i in range(NCORES):
        m = {"x": xs[i], "cnt_all": cnt2, "bmask": make_bmask(i)}
        m.update(common); m.update(f0); m.update(f1); m.update(a1)
        in_maps.append(m)
    res = run_bass_kernel_spmd(nc, in_maps, core_ids=list(range(NCORES)))
    return unshard_x([r["xo"] for r in res.results])


def kernel_8launch(**inputs):
    f32 = lambda a: np.ascontiguousarray(np.asarray(a), dtype=np.float32)
    x = f32(inputs["L0_attn_x"])
    identb, rep, repsign, swap, pos = make_consts()
    identf, ltri = make_l0_consts()

    l0w = dict(router_w=f32(inputs["L0_ffn_w_router"]), w1=f32(inputs["L0_ffn_w1"]),
               b1=f32(inputs["L0_ffn_b1"]), w2=f32(inputs["L0_ffn_w2"]), b2=f32(inputs["L0_ffn_b2"]))
    a0w = dict(wqkvT=f32(inputs["L0_attn_qkv_w"]).T.copy(),
               woT=f32(inputs["L0_attn_out_w"]).T.copy(),
               freqs=f32(inputs["L0_attn_rope_freqs"]).reshape(1, 32),
               pa_rows=np.stack([f32(inputs["L0_attn_bias_p"]).reshape(16),
                                 f32(inputs["L0_attn_bias_a"]).reshape(16)]).astype(np.float32))
    a1w = dict(wq=f32(inputs["L1_attn_Wq"]), wk=f32(inputs["L1_attn_Wk"]),
               wv=f32(inputs["L1_attn_Wv"]), wo=f32(inputs["L1_attn_Wo"]),
               gamma=f32(inputs["L1_attn_ln_gamma"]).reshape(1, D),
               beta=f32(inputs["L1_attn_ln_beta"]).reshape(1, D))
    f1w = dict(ln_w=f32(inputs["L1_ffn_ln_w"]).reshape(1, D),
               ln_b=f32(inputs["L1_ffn_ln_b"]).reshape(1, D),
               router_w=f32(inputs["L1_ffn_router_w"]), gate_w=f32(inputs["L1_ffn_gate_w"]),
               up_w=f32(inputs["L1_ffn_up_w"]), down_w=f32(inputs["L1_ffn_down_w"]),
               sh_gate=f32(inputs["L1_ffn_sh_gate"]), sh_up=f32(inputs["L1_ffn_sh_up"]),
               sh_down=f32(inputs["L1_ffn_sh_down"]))

    xs = shard_x(x)
    xs = _run_l0_ffn(xs, l0w, identf, ltri, identb)
    xs = run_attn(0, xs, a0w)
    xs = _run_l0_ffn(xs, l0w, identf, ltri, identb)
    xs = run_ffn_l1_cached(xs, f1w)
    xs = run_attn_cached(1, xs, a1w)
    xs = run_ffn_l1_cached(xs, f1w)
    return unshard_x(xs)


def run_ffn_l1_cached(xs, weights):
    nc = _get_nc("l1ffn", build_ffn_l1)
    identb, rep, repsign, swap, pos = make_consts()
    identf = np.eye(128, dtype=np.float32)
    in_maps = []
    for i in range(NCORES):
        m = {"x": xs[i], "identb": identb, "identf": identf}
        m.update({"f1_" + k: v for k, v in weights.items()})
        in_maps.append(m)
    res = run_bass_kernel_spmd(nc, in_maps, core_ids=list(range(NCORES)))
    return [r["xo"] for r in res.results]


def run_attn_cached(layer, xs, weights):
    nc = _get_nc(("attn", layer), build_attn, layer)
    identb, rep, repsign, swap, pos = make_consts()
    in_maps = []
    for i in range(NCORES):
        m = {"x": xs[i], "identb": identb}
        if layer == 1:
            m.update(weights)
        else:
            m.update(weights, repmat=rep, repsign=repsign, swapm=swap)
        in_maps.append(m)
    res = run_bass_kernel_spmd(nc, in_maps, core_ids=list(range(NCORES)))
    return [r["xo"] for r in res.results]



# version 07
# speedup vs baseline: 1.4289x; 1.4289x over previous
import numpy as np
import ml_dtypes
import concourse.bass as bass
import concourse.mybir as mybir
from concourse.bass_core_rust import IndirectOffsetOnAxis
from concourse.bass_utils import run_bass_kernel_spmd

F32 = mybir.dt.float32
BF16 = mybir.dt.bfloat16
U32 = mybir.dt.uint32
F16 = mybir.dt.float16
AF = mybir.ActivationFunctionType
ALU = mybir.AluOpType
AX = mybir.AxisListType.X

NCORES = 8
B, S, D = 16, 2048, 1024
SEQ_PER_CORE = 2
TOK = SEQ_PER_CORE * S
CORE_BATCHES = [[i, 15 - i] for i in range(NCORES)]
CAP0 = 3072
CLOC = 1024
BIGIDX = 4000000.0
EPS = 1e-5
TWO_PI = 6.283185307179586
PI = 3.141592653589793
HALF_PI = 1.5707963267948966


class Sem:
    def __init__(self, handle):
        self.h = handle
        self.n = 0

class Emitter:
    def __init__(self, auto=None):
        self.ops = []
        self.auto = auto or {}

    def emit(self, engine, fn=None, wait=(), inc=()):
        self.ops.append((engine, fn, list(wait), list(inc)))

    def dma(self, engine, out, in_, sem, wait=()):
        def f(eng, out=out, in_=in_):
            return eng.dma_start(out=out, in_=in_)
        sem.n += 16
        self.emit(engine, f, wait=wait, inc=[(sem, 16)])
        return sem.n

    def step(self, engine, fn, sem=None, wait=()):
        if engine in self.auto:
            a = self.auto[engine]
            assert sem is None or sem is a
            w = [(a, a.n)] + [x for x in wait if not (x[0] is a)]
            a.n += 1
            self.emit(engine, fn, wait=w, inc=[(a, 1)])
            return a.n
        inc = []
        if sem is not None:
            sem.n += 1
            inc = [(sem, 1)]
        self.emit(engine, fn, wait=wait, inc=inc)
        return sem.n if sem is not None else None

    def replay(self, block):
        def section(eng_name):
            def body(eng):
                for engine, fn, waits, incs in self.ops:
                    if engine != eng_name:
                        continue
                    for (s, v) in waits:
                        eng.wait_ge(s.h, v)
                    if fn is not None:
                        inst = fn(eng)
                        for (s, a) in incs:
                            assert inst is not None, "inc attached to op that returned None"
                            inst = inst.then_inc(s.h, a)
            return body
        block.sync(section("sync"))
        block.scalar(section("scalar"))
        block.vector(section("vector"))
        block.tensor(section("tensor"))
        block.gpsimd(section("gpsimd"))


def emit_ln_stats(E, bn6, bnag, xt, width, mean, rstd, dv, ac, eps_col, wait=()):
    n_sub = width // 512

    def f_bn(eng, n_sub=n_sub, xt=xt):
        i = None
        for s0 in range(n_sub):
            i = eng.bn_stats(out=bn6[:, 6 * s0:6 * s0 + 6], in_=xt[:, 512 * s0:512 * (s0 + 1)])
        return i
    E.step("vector", f_bn, wait=list(wait) + [(ac, ac.n)])

    def f_ag(eng, n_sub=n_sub):
        return eng.bn_aggr(out=bnag[:, :], in_=bn6[:, 0:6 * n_sub])
    E.step("vector", f_ag)

    def f_mean(eng, mean=mean):
        return eng.tensor_copy(out=mean[:, :], in_=bnag[:, 0:1])
    E.step("vector", f_mean, sem=dv)

    def f_ln(eng, rstd=rstd):
        return eng.activation(out=rstd[:, :], in_=bnag[:, 1:2], func=AF.Ln, bias=eps_col, scale=1.0)
    E.step("scalar", f_ln, wait=[(dv, dv.n)])

    def f_exp(eng, rstd=rstd):
        return eng.activation(out=rstd[:, :], in_=rstd[:, :], func=AF.Exp, bias=0.0, scale=-0.5)
    E.step("scalar", f_exp, sem=ac)


def emit_transpose_tile(E, src, dst_fn, ps_tr, ident, pe, dv, wait=(), n_chunks=8):
    for c in range(n_chunks):
        def f_tr(eng, c=c):
            return eng.transpose(out=ps_tr[:, :], in_=src[:, 128 * c:128 * (c + 1)], identity=ident[:, :])
        E.step("tensor", f_tr, sem=pe, wait=list(wait) + [(dv, dv.n)])

        def f_ev(eng, c=c):
            return eng.tensor_copy(out=dst_fn(c), in_=ps_tr[:, :])
        E.step("vector", f_ev, sem=dv, wait=[(pe, pe.n)])



def emit_attn(nc, E, sems, io, layer):
    x = io["x"]
    identb = io["identb"]
    if layer == 1:
        wq, wk, wv, wo = io["wq"], io["wk"], io["wv"], io["wo"]
        gamma, beta = io["gamma"], io["beta"]
    else:
        wqkvT, woT = io["wqkvT"], io["woT"]
        freqs = io["freqs"]
        repmat, repsign, swapm = io["repmat"], io["repsign"], io["swapm"]
        pa_rows = io["pa_rows"]
    xo = io["xo"]

    NHEAD = 16
    NSTRIPE = S // 512
    NKC = S // 128
    SCALE = 0.125

    from contextlib import ExitStack
    with ExitStack() as ctx:
        xaT = ctx.enter_context(nc.sbuf_tensor("xaT", [128, 8, S], BF16))
        kT = ctx.enter_context(nc.sbuf_tensor("kT", [128, 8, S], BF16))
        vtm = ctx.enter_context(nc.sbuf_tensor("vtm", [128, NKC, D], BF16))
        qT = ctx.enter_context(nc.sbuf_tensor("qT", [128, 8, 512], BF16))
        expT = ctx.enter_context(nc.sbuf_tensor("expT", [128, NKC, 512], BF16))
        aoT = ctx.enter_context(nc.sbuf_tensor("aoT", [128, 8, 512], BF16))
        wbuf = ctx.enter_context(nc.sbuf_tensor("wbuf", [128, 8, 2 * D], BF16))
        xt = ctx.enter_context(nc.sbuf_tensor("xt", [128, D], F32))
        xab = ctx.enter_context(nc.sbuf_tensor("xab", [128, D], BF16))
        bn6 = ctx.enter_context(nc.sbuf_tensor("bn6", [128, 12], F32))
        bnag = ctx.enter_context(nc.sbuf_tensor("bnag", [128, 2], F32))
        stats = ctx.enter_context(nc.sbuf_tensor("stats", [128, 4], F32))
        cc = ctx.enter_context(nc.sbuf_tensor("cc", [128, 2], F32))
        if layer == 1:
            gb = ctx.enter_context(nc.sbuf_tensor("gb", [128, 2 * D], F32))
        onesb = ctx.enter_context(nc.sbuf_tensor("onesb", [128, 1], BF16))
        onesf = ctx.enter_context(nc.sbuf_tensor("onesf", [128, 1], F32))
        ones1 = ctx.enter_context(nc.sbuf_tensor("ones1", [1, 128], BF16))
        ones1f = ctx.enter_context(nc.sbuf_tensor("ones1f", [1, 128], F32))
        scr = ctx.enter_context(nc.sbuf_tensor("scr", [128, 512], F32))
        scr2 = ctx.enter_context(nc.sbuf_tensor("scr2", [128, 512], F32))
        tmp1 = ctx.enter_context(nc.sbuf_tensor("tmp1", [1, 4], F32))
        bsq = ctx.enter_context(nc.sbuf_tensor("bsq", [1, 16], F32))
        crow = ctx.enter_context(nc.sbuf_tensor("crow", [1, 512], F32))
        crow_bf = ctx.enter_context(nc.sbuf_tensor("crow_bf", [1, 512], BF16))
        srow = ctx.enter_context(nc.sbuf_tensor("srow", [1, 512], F32))
        rrow = ctx.enter_context(nc.sbuf_tensor("rrow", [1, 512], F32))
        rrow_bf = ctx.enter_context(nc.sbuf_tensor("rrow_bf", [1, 512], BF16))
        rb = ctx.enter_context(nc.sbuf_tensor("rb", [64, 512], F32))
        xres = ctx.enter_context(nc.sbuf_tensor("xres", [128, D], F32))
        xout = ctx.enter_context(nc.sbuf_tensor("xout", [128, D], F32))
        if layer == 0:
            ctabq = ctx.enter_context(nc.sbuf_tensor("ctabq", [128, S], BF16))
        if layer == 0:
            stabq = ctx.enter_context(nc.sbuf_tensor("stabq", [128, S], BF16))
        if layer == 0:
            freq_s = ctx.enter_context(nc.sbuf_tensor("freq_s", [1, 32], F32))
        if layer == 0:
            rep_s = ctx.enter_context(nc.sbuf_tensor("rep_s", [32, 128], BF16))
        if layer == 0:
            repsg_s = ctx.enter_context(nc.sbuf_tensor("repsg_s", [32, 128], BF16))
        if layer == 0:
            swap_s = ctx.enter_context(nc.sbuf_tensor("swap_s", [128, 128], BF16))
        if layer == 0:
            pa_s = ctx.enter_context(nc.sbuf_tensor("pa_s", [1, 32], F32))
        if layer == 0:
            pacol = ctx.enter_context(nc.sbuf_tensor("pacol", [128, 32], F32))
        if layer == 0:
            dist = ctx.enter_context(nc.sbuf_tensor("dist", [128, 2, 512], F16))
        if layer == 0:
            lnt = ctx.enter_context(nc.sbuf_tensor("lnt", [128, 2, 512], F16))
            scrP = ctx.enter_context(nc.sbuf_tensor("scrP", [128, 2, 512], F32))
            tb_bf = ctx.enter_context(nc.sbuf_tensor("tb_bf", [32, 512], BF16))
        identb_s = ctx.enter_context(nc.sbuf_tensor("identb_s", [128, 128], BF16))
        ps_a = ctx.enter_context(nc.psum_tensor("ps_a", [128, 512], F32))
        ps_a2 = ctx.enter_context(nc.psum_tensor("ps_a2", [128, 512], F32))
        ps_b = ctx.enter_context(nc.psum_tensor("ps_b", [128, 512], F32))
        ps_tr = ctx.enter_context(nc.psum_tensor("ps_tr", [128, 128], BF16))
        ps_pv = ctx.enter_context(nc.psum_tensor("ps_pv", [64, 512], F32))
        ps_s = ctx.enter_context(nc.psum_tensor("ps_s", [1, 512], F32))
        ps_r = ctx.enter_context(nc.psum_tensor("ps_r", [64, 512], F32))
        ps_c = ctx.enter_context(nc.psum_tensor("ps_c", [128, 512], F32))
        ld, pe, dv, ac, go, st, lg = sems

        E.step("vector", lambda eng: eng.memset(onesb[:, :], 1.0))
        E.step("vector", lambda eng: eng.memset(onesf[:, :], 1.0))
        E.step("vector", lambda eng: eng.memset(ones1[:, :], 1.0))
        E.step("vector", lambda eng: eng.memset(ones1f[:, :], 1.0))
        E.step("vector", lambda eng: eng.memset(cc[:, 0:1], EPS))
        E.step("vector", lambda eng: eng.memset(cc[:, 1:2], 1.0), sem=dv)
        E.dma("sync", identb_s[:, :], identb[:, :], ld)

        if layer == 1:
            E.dma("sync", gb[0:1, 0:D], gamma[:, :], ld)
            c_b = E.dma("sync", gb[0:1, D:2 * D], beta[:, :], ld)
            for half in range(4):
                def f_mm(eng, half=half):
                    return eng.matmul(out=ps_c[:, :], lhsT=ones1f[:, :],
                                      rhs=gb[0:1, 512 * half:512 * (half + 1)], start=True, stop=True)
                E.step("tensor", f_mm, sem=pe, wait=[(ld, c_b), (dv, dv.n)])

                def f_ev(eng, half=half):
                    return eng.tensor_copy(out=gb[:, 512 * half:512 * (half + 1)], in_=ps_c[:, :])
                E.step("vector", f_ev, sem=dv, wait=[(pe, pe.n)])
        else:
            E.dma("sync", freq_s[:, :], freqs[:, :], ld)
            E.dma("sync", rep_s[:, :], repmat[:, :], ld)
            E.dma("sync", repsg_s[:, :], repsign[:, :], ld)
            E.dma("sync", swap_s[:, :], swapm[:, :], ld)
            E.dma("sync", pa_s[0:1, 0:16], pa_rows[0:1, :], ld)
            c_pa = E.dma("sync", pa_s[0:1, 16:32], pa_rows[1:2, :], ld)
            E.step("vector", lambda eng: eng.tensor_scalar_max(pa_s[:, :], pa_s[:, :], 0.01),
                   sem=dv, wait=[(ld, c_pa)])

            def f_acol(eng):
                return eng.matmul(out=ps_c[:, 0:16], lhsT=ones1f[:, :], rhs=pa_s[0:1, 16:32],
                                  start=True, stop=True)
            E.step("tensor", f_acol, wait=[(dv, dv.n)])

            def f_pcol(eng):
                return eng.matmul(out=ps_c[:, 16:32], lhsT=ones1f[:, :], rhs=pa_s[0:1, 0:16],
                                  start=True, stop=True)
            E.step("tensor", f_pcol, sem=pe)

            def f_evpa(eng):
                return eng.tensor_copy(out=pacol[:, 0:16], in_=ps_c[:, 0:16])
            E.step("vector", f_evpa, wait=[(pe, pe.n)])

            def f_evpa2(eng):
                return eng.tensor_scalar_mul(pacol[:, 16:32], ps_c[:, 16:32], -1.0)
            E.step("vector", f_evpa2, sem=dv)

            for ch in range(S // 512):
                csl = slice(512 * ch, 512 * (ch + 1))

                def f_pos(eng, ch=ch):
                    return eng.iota(out=srow[:, :], pattern=[[1, 512]], base=512 * ch,
                                    channel_multiplier=0, allow_small_or_imprecise_dtypes=True)
                E.step("gpsimd", f_pos, sem=go, wait=[(pe, pe.n)])

                def f_ang(eng, csl=csl):
                    return eng.matmul(out=ps_a[0:32, 0:512], lhsT=freq_s[:, :],
                                      rhs=srow[:, :], start=True, stop=True)
                E.step("tensor", f_ang, sem=pe, wait=[(ld, ld.n), (dv, dv.n), (ac, ac.n), (go, go.n)])

                def f_evang(eng):
                    return eng.tensor_copy(out=scr[0:32, :], in_=ps_a[0:32, 0:512])
                E.step("vector", f_evang, sem=dv, wait=[(pe, pe.n)])
                MAGIC = 12582912.0
                INV2PI = 0.15915494309189535
                for is_cos in (True, False):
                    if is_cos:
                        E.step("vector", lambda eng: eng.tensor_scalar_add(scr2[0:32, :], scr[0:32, :],
                               HALF_PI), wait=[(ac, ac.n)])
                    else:
                        E.step("vector", lambda eng: eng.tensor_copy(out=scr2[0:32, :], in_=scr[0:32, :]),
                               wait=[(ac, ac.n)])
                    E.step("vector", lambda eng: eng.tensor_scalar(out=xt[0:32, 0:512], in0=scr2[0:32, :],
                           scalar1=INV2PI, scalar2=MAGIC, op0=ALU.mult, op1=ALU.add))
                    E.step("vector", lambda eng: eng.tensor_scalar_sub(xt[0:32, 0:512], xt[0:32, 0:512], MAGIC))
                    E.step("vector", lambda eng: eng.scalar_tensor_tensor(out=scr2[0:32, :], in0=xt[0:32, 0:512],
                           scalar=-TWO_PI, in1=scr2[0:32, :], op0=ALU.mult, op1=ALU.add), sem=dv)
                    E.step("scalar", lambda eng: eng.activation(out=scr2[0:32, :], in_=scr2[0:32, :], func=AF.Sin),
                           sem=ac, wait=[(dv, dv.n)])
                    E.step("vector", lambda eng: eng.tensor_copy(out=tb_bf[:, :], in_=scr2[0:32, :]),
                           sem=dv, wait=[(ac, ac.n)])
                    dstq, lhs = (ctabq, rep_s) if is_cos else (stabq, repsg_s)

                    def f_rep(eng, lhs=lhs):
                        return eng.matmul(out=ps_b[:, 0:512], lhsT=lhs[:, :], rhs=tb_bf[:, :], start=True, stop=True)
                    E.step("tensor", f_rep, sem=pe, wait=[(dv, dv.n)])

                    def f_evq(eng, dstq=dstq, csl=csl):
                        return eng.activation(out=dstq[:, csl], in_=ps_b[:, 0:512], func=AF.Copy, scale=SCALE)
                    E.step("scalar", f_evq, sem=ac, wait=[(pe, pe.n)])

        for sq in range(SEQ_PER_CORE):
            xseq = x[sq * S:(sq + 1) * S, :]
            xoseq = xo[sq * S:(sq + 1) * S, :]

            wprev = lg.n
            if layer == 1:
                for c in range(8):
                    E.dma("gpsimd", wbuf[:, c, 0:D], wk[128 * c:128 * (c + 1), :], lg, wait=[(dv, dv.n), (lg, wprev)])
                    E.dma("gpsimd", wbuf[:, c, D:2 * D], wv[128 * c:128 * (c + 1), :], lg, wait=[(dv, dv.n), (lg, wprev)])
            else:
                for c in range(8):
                    E.dma("gpsimd", wbuf[:, c, 0:D], wqkvT[128 * c:128 * (c + 1), D:2 * D], lg, wait=[(dv, dv.n), (lg, wprev)])
                    E.dma("gpsimd", wbuf[:, c, D:2 * D], wqkvT[128 * c:128 * (c + 1), 2 * D:3 * D], lg, wait=[(dv, dv.n), (lg, wprev)])
            ld_wkv = lg.n

            for i in range(S // 128):
                c_x = E.dma("sync", xt[:, :], xseq[128 * i:128 * (i + 1), :], ld, wait=[(dv, dv.n)])
                emit_ln_stats(E, bn6, bnag, xt, D, stats[:, 0:1], stats[:, 1:2], dv, ac, cc[:, 0:1],
                              wait=[(ld, c_x)])

                def f_norm(eng):
                    return eng.tensor_scalar(out=xt[:, :], in0=xt[:, :], scalar1=stats[:, 0:1],
                                             scalar2=stats[:, 1:2], op0=ALU.subtract, op1=ALU.mult)
                E.step("vector", f_norm, wait=[(ac, ac.n)])
                if layer == 1:
                    emit_ln_stats(E, bn6, bnag, xt, D, stats[:, 2:3], stats[:, 3:4], dv, ac, cc[:, 0:1])

                    def f_norm2(eng):
                        return eng.tensor_scalar(out=xt[:, :], in0=xt[:, :], scalar1=stats[:, 2:3],
                                                 scalar2=stats[:, 3:4], op0=ALU.subtract, op1=ALU.mult)
                    E.step("vector", f_norm2, wait=[(ac, ac.n)])

                    def f_gam(eng):
                        return eng.tensor_mul(out=xt[:, :], in0=xt[:, :], in1=gb[:, 0:D])
                    E.step("vector", f_gam)

                    def f_bet(eng):
                        return eng.tensor_add(out=xt[:, :], in0=xt[:, :], in1=gb[:, D:2 * D])
                    E.step("vector", f_bet)

                def f_cast(eng):
                    return eng.tensor_copy(out=xab[:, :], in_=xt[:, :])
                E.step("vector", f_cast, sem=dv)
                emit_transpose_tile(E, xab, lambda c, i=i: xaT[:, c, 128 * i:128 * (i + 1)],
                                    ps_tr, identb_s, pe, dv)

            for fch in range(8):
                for stp in range(NSTRIPE):
                    def f_mm(eng, fch=fch, stp=stp):
                        inst = None
                        for dch in range(8):
                            inst = eng.matmul(out=ps_a[:, :], lhsT=wbuf[:, dch, 128 * fch:128 * (fch + 1)],
                                              rhs=xaT[:, dch, 512 * stp:512 * (stp + 1)],
                                              start=(dch == 0), stop=(dch == 7))
                        return inst
                    E.step("tensor", f_mm, sem=pe, wait=[(lg, ld_wkv), (dv, dv.n)])

                    def f_ev(eng, fch=fch, stp=stp):
                        return eng.tensor_copy(out=kT[:, fch, 512 * stp:512 * (stp + 1)], in_=ps_a[:, :])
                    E.step("vector", f_ev, sem=dv, wait=[(pe, pe.n)])
            if layer == 0:
                for fch in range(8):
                    for stp in range(NSTRIPE):
                        sl = slice(512 * stp, 512 * (stp + 1))

                        def f_swapmm(eng, fch=fch, sl=sl):
                            return eng.matmul(out=ps_b[:, :], lhsT=swap_s[:, :], rhs=kT[:, fch, sl],
                                              start=True, stop=True)
                        E.step("tensor", f_swapmm, sem=pe, wait=[(dv, dv.n)])

                        def f_t1(eng, fch=fch, sl=sl):
                            return eng.scalar_tensor_tensor(out=scr[:, :], in0=kT[:, fch, sl], scalar=8.0,
                                                            in1=ctabq[:, sl], op0=ALU.mult, op1=ALU.mult)
                        E.step("vector", f_t1, wait=[(pe, pe.n)])

                        def f_t2(eng, sl=sl):
                            return eng.scalar_tensor_tensor(out=scr2[:, :], in0=ps_b[:, :], scalar=8.0,
                                                            in1=stabq[:, sl], op0=ALU.mult, op1=ALU.mult)
                        E.step("vector", f_t2)

                        def f_t3(eng, fch=fch, sl=sl):
                            return eng.tensor_add(out=kT[:, fch, sl], in0=scr[:, :], in1=scr2[:, :])
                        E.step("vector", f_t3, sem=dv)
            for i in range(NKC):
                for vh in range(2):
                    def f_mm(eng, i=i, vh=vh):
                        inst = None
                        for dch in range(8):
                            inst = eng.matmul(out=ps_a[:, :], lhsT=xaT[:, dch, 128 * i:128 * (i + 1)],
                                              rhs=wbuf[:, dch, D + 512 * vh:D + 512 * (vh + 1)],
                                              start=(dch == 0), stop=(dch == 7))
                        return inst
                    E.step("tensor", f_mm, sem=pe, wait=[(dv, dv.n), (lg, ld_wkv)])

                    def f_ev(eng, i=i, vh=vh):
                        return eng.tensor_copy(out=vtm[:, i, 512 * vh:512 * (vh + 1)], in_=ps_a[:, :])
                    E.step("vector", f_ev, sem=dv, wait=[(pe, pe.n)])

            wprev = lg.n
            if layer == 1:
                for c in range(8):
                    E.dma("gpsimd", wbuf[:, c, 0:D], wq[128 * c:128 * (c + 1), :], lg, wait=[(dv, dv.n), (lg, wprev)])
                    E.dma("gpsimd", wbuf[:, c, D:2 * D], wo[128 * c:128 * (c + 1), :], lg, wait=[(dv, dv.n), (lg, wprev)])
            else:
                for c in range(8):
                    E.dma("gpsimd", wbuf[:, c, 0:D], wqkvT[128 * c:128 * (c + 1), 0:D], lg, wait=[(dv, dv.n), (lg, wprev)])
                    E.dma("gpsimd", wbuf[:, c, D:2 * D], woT[128 * c:128 * (c + 1), :], lg, wait=[(dv, dv.n), (lg, wprev)])
            ld_qo = lg.n

            E.step("vector", lambda eng: eng.memset(bsq[:, :], 0.0), sem=dv, wait=[(ac, ac.n)])
            for fch in range(8):
                for kc in range(NSTRIPE):
                    def f_sq(eng, fch=fch, kc=kc):
                        return eng.activation(out=scr[:, :], in_=kT[:, fch, 512 * kc:512 * (kc + 1)],
                                              func=AF.Square)
                    E.step("scalar", f_sq, sem=ac, wait=[(dv, dv.n)])
                    for half in range(2):
                        h = 2 * fch + half

                        def f_sum(eng, half=half):
                            rh = scr[0:64, :] if half == 0 else scr[64:128, :]
                            on = onesf[0:64, :] if half == 0 else onesf[64:128, :]
                            return eng.matmul(out=ps_s[:, :], lhsT=on, rhs=rh,
                                              start=True, stop=True)
                        E.step("tensor", f_sum, sem=pe, wait=[(ac, ac.n), (dv, dv.n)])

                        def f_evk(eng):
                            return eng.tensor_copy(out=srow[:, :], in_=ps_s[:, :])
                        E.step("vector", f_evk, wait=[(pe, pe.n)])

                        def f_m1(eng):
                            return eng.reduce_max(out=tmp1[:, 0:1], in_=srow[:, :], axis=AX)
                        E.step("vector", f_m1)

                        def f_m2(eng, h=h):
                            return eng.tensor_max(out=bsq[:, h:h + 1], in0=bsq[:, h:h + 1], in1=tmp1[:, 0:1])
                        E.step("vector", f_m2, sem=dv)
            for h in range(NHEAD):
                def f_sqrt(eng, h=h):
                    return eng.activation(out=bsq[:, h:h + 1], in_=bsq[:, h:h + 1], func=AF.Sqrt)
                E.step("scalar", f_sqrt, sem=ac, wait=[(dv, dv.n)])

            for stp in range(NSTRIPE):
                sl = slice(512 * stp, 512 * (stp + 1))
                for fch in range(8):
                    def f_mm(eng, fch=fch, stp=stp):
                        inst = None
                        for dch in range(8):
                            inst = eng.matmul(out=ps_a[:, :], lhsT=wbuf[:, dch, 128 * fch:128 * (fch + 1)],
                                              rhs=xaT[:, dch, 512 * stp:512 * (stp + 1)],
                                              start=(dch == 0), stop=(dch == 7))
                        return inst
                    E.step("tensor", f_mm, sem=pe, wait=[(lg, ld_qo), (dv, dv.n), (ac, ac.n)])
                    if layer == 1:
                        def f_ev(eng, fch=fch):
                            return eng.tensor_scalar_mul(qT[:, fch, :], ps_a[:, :], SCALE)
                        E.step("vector", f_ev, sem=dv, wait=[(pe, pe.n)])
                    else:
                        def f_ev(eng, fch=fch):
                            return eng.tensor_copy(out=qT[:, fch, :], in_=ps_a[:, :])
                        E.step("vector", f_ev, sem=dv, wait=[(pe, pe.n)])
                if layer == 0:
                    for fch in range(8):
                        def f_swapmm(eng, fch=fch):
                            return eng.matmul(out=ps_b[:, :], lhsT=swap_s[:, :], rhs=qT[:, fch, :],
                                              start=True, stop=True)
                        E.step("tensor", f_swapmm, sem=pe, wait=[(dv, dv.n)])

                        def f_t1(eng, fch=fch, sl=sl):
                            return eng.tensor_mul(out=scr[:, :], in0=qT[:, fch, :], in1=ctabq[:, sl])
                        E.step("vector", f_t1, wait=[(pe, pe.n)])

                        def f_t2(eng, sl=sl):
                            return eng.tensor_mul(out=scr2[:, :], in0=ps_b[:, :], in1=stabq[:, sl])
                        E.step("vector", f_t2)

                        def f_t3(eng, fch=fch):
                            return eng.tensor_add(out=qT[:, fch, :], in0=scr[:, :], in1=scr2[:, :])
                        E.step("vector", f_t3, sem=dv)

                for h in range(NHEAD):
                    fch, rlo = h // 2, (h % 2) * 64

                    def rsl(t):
                        return t[0:64, :] if rlo == 0 else t[64:128, :]

                    def f_qsq(eng, fch=fch):
                        return eng.activation(out=scr[:, :], in_=qT[:, fch, :], func=AF.Square)
                    E.step("scalar", f_qsq, sem=ac, wait=[(dv, dv.n)])

                    def f_qsum(eng, rlo=rlo):
                        rh = scr[0:64, :] if rlo == 0 else scr[64:128, :]
                        on = onesf[0:64, :] if rlo == 0 else onesf[64:128, :]
                        return eng.matmul(out=ps_s[:, :], lhsT=on, rhs=rh, start=True, stop=True)
                    E.step("tensor", f_qsum, sem=pe, wait=[(ac, ac.n)])

                    def f_qs(eng):
                        return eng.activation(out=crow[:, :], in_=ps_s[:, :], func=AF.Sqrt)
                    E.step("scalar", f_qs, sem=ac, wait=[(pe, pe.n)])

                    def f_cmul(eng, h=h):
                        return eng.tensor_scalar(out=crow[:, :], in0=crow[:, :], scalar1=bsq[:, h:h + 1],
                                                 scalar2=-1.0, op0=ALU.mult, op1=ALU.mult)
                    E.step("vector", f_cmul, wait=[(ac, ac.n)])

                    def f_cbf(eng):
                        return eng.tensor_copy(out=crow_bf[:, :], in_=crow[:, :])
                    E.step("vector", f_cbf, sem=dv)

                    qk_done = {}
                    comb_done = {}
                    for kc in range(NKC):
                        ps = ps_a if kc % 2 == 0 else ps_a2
                        if layer == 0:
                            war = [(dv, comb_done.get(kc - 2, 0))]
                        else:
                            war = [(ac, ac.n if kc < 2 else qk_done.get(("exp", kc - 2), 0))]

                        def f_qk(eng, kc=kc, fch=fch, rlo=rlo, ps=ps):
                            lhs = kT[:, fch, 128 * kc:128 * (kc + 1)]
                            q_ = qT[:, fch, :]
                            lhs = lhs[0:64, :] if rlo == 0 else lhs[64:128, :]
                            q_ = q_[0:64, :] if rlo == 0 else q_[64:128, :]
                            eng.matmul(out=ps[:, :], lhsT=lhs, rhs=q_, start=True, stop=False)
                            return eng.matmul(out=ps[:, :], lhsT=ones1[:, :], rhs=crow_bf[:, :],
                                              start=False, stop=True)
                        qk_done[kc] = E.step("tensor", f_qk, sem=pe, wait=[(dv, dv.n)] + war)
                        if layer == 0:
                            def f_iota(eng, kc=kc, stp=stp):
                                return eng.iota(out=dist[:, kc % 2, :], pattern=[[-1, 512]],
                                                base=128 * kc - 512 * stp,
                                                channel_multiplier=1, allow_small_or_imprecise_dtypes=True)
                            E.step("gpsimd", f_iota, sem=go, wait=[(ac, ac.n)])

                            def f_abs(eng, kc=kc):
                                return eng.activation(out=dist[:, kc % 2, :], in_=dist[:, kc % 2, :], func=AF.Abs)
                            E.step("scalar", f_abs, sem=ac, wait=[(go, go.n)])

                            def f_ln(eng, h=h, kc=kc):
                                return eng.activation(out=lnt[:, kc % 2, :], in_=dist[:, kc % 2, :], func=AF.Ln,
                                                      bias=cc[:, 1:2], scale=pacol[:, h:h + 1])
                            E.step("scalar", f_ln, sem=ac)

                            def f_comb(eng, h=h, kc=kc, ps=ps):
                                return eng.scalar_tensor_tensor(out=scrP[:, kc % 2, :], in0=lnt[:, kc % 2, :],
                                                                scalar=pacol[:, 16 + h:17 + h], in1=ps[:, :],
                                                                op0=ALU.mult, op1=ALU.add)
                            comb_done[kc] = E.step("vector", f_comb, sem=dv,
                                                   wait=[(ac, ac.n), (pe, qk_done[kc])])
                            if kc % 2 == 1:
                                def f_exp(eng, kc=kc):
                                    return eng.activation(out=expT[:, kc - 1:kc + 1, :],
                                                          in_=scrP[:, 0:2, :], func=AF.Exp)
                                qk_done[("exp", kc)] = E.step("scalar", f_exp, sem=ac, wait=[(dv, dv.n)])
                        else:
                            def f_exp(eng, kc=kc, ps=ps):
                                return eng.activation(out=expT[:, kc, :], in_=ps[:, :], func=AF.Exp)
                            qk_done[("exp", kc)] = E.step("scalar", f_exp, sem=ac, wait=[(pe, qk_done[kc])])

                    def f_sums(eng):
                        inst = None
                        for kc in range(NKC):
                            inst = eng.matmul(out=ps_s[:, :], lhsT=onesb[:, :], rhs=expT[:, kc, :],
                                              start=(kc == 0), stop=(kc == NKC - 1))
                        return inst
                    E.step("tensor", f_sums, sem=pe, wait=[(ac, ac.n)])

                    def f_evs(eng):
                        return eng.tensor_copy(out=srow[:, :], in_=ps_s[:, :])
                    E.step("vector", f_evs, wait=[(pe, pe.n)])

                    def f_rec(eng):
                        return eng.reciprocal(out=rrow[:, :], in_=srow[:, :])
                    E.step("vector", f_rec)

                    def f_rbf(eng):
                        return eng.tensor_copy(out=rrow_bf[:, :], in_=rrow[:, :])
                    E.step("vector", f_rbf, sem=dv)

                    def f_pv(eng, h=h):
                        inst = None
                        for kc in range(NKC):
                            inst = eng.matmul(out=ps_pv[:, :], lhsT=vtm[:, kc, 64 * h:64 * (h + 1)],
                                              rhs=expT[:, kc, :], start=(kc == 0), stop=(kc == NKC - 1))
                        return inst
                    E.step("tensor", f_pv, wait=[(ac, ac.n)])

                    def f_rb(eng):
                        return eng.matmul(out=ps_r[:, :], lhsT=ones1[0:1, 0:64], rhs=rrow_bf[:, :],
                                          start=True, stop=True)
                    E.step("tensor", f_rb, sem=pe, wait=[(dv, dv.n)])

                    def f_evrb(eng):
                        return eng.tensor_copy(out=rb[:, :], in_=ps_r[:, :])
                    E.step("vector", f_evrb, wait=[(pe, pe.n)])

                    def f_ao(eng, fch=fch, rlo=rlo):
                        dst = aoT[:, fch, :]
                        dst = dst[0:64, :] if rlo == 0 else dst[64:128, :]
                        return eng.tensor_mul(out=dst, in0=ps_pv[:, :], in1=rb[:, :])
                    E.step("vector", f_ao, sem=dv)

                for tt in range(4):
                    t0 = 512 * stp + 128 * tt
                    c_r = E.dma("sync", xres[:, :], xseq[t0:t0 + 128, :], ld, wait=[(dv, dv.n), (st, st.n)])
                    for fh in range(2):
                        def f_mm(eng, tt=tt, fh=fh):
                            inst = None
                            for c in range(8):
                                inst = eng.matmul(out=ps_b[:, :], lhsT=aoT[:, c, 128 * tt:128 * (tt + 1)],
                                                  rhs=wbuf[:, c, D + 512 * fh:D + 512 * (fh + 1)],
                                                  start=(c == 0), stop=(c == 7))
                            return inst
                        E.step("tensor", f_mm, sem=pe, wait=[(dv, dv.n), (ld, c_r)])

                        def f_res(eng, fh=fh):
                            return eng.scalar_tensor_tensor(out=xout[:, 512 * fh:512 * (fh + 1)], in0=ps_b[:, :],
                                                            scalar=1.0, in1=xres[:, 512 * fh:512 * (fh + 1)],
                                                            op0=ALU.mult, op1=ALU.add)
                        E.step("vector", f_res, sem=dv, wait=[(pe, pe.n)])
                    E.dma("scalar", xoseq[t0:t0 + 128, :], xout[:, :], st, wait=[(dv, dv.n)])







class NcPrefix:
    def __init__(self, nc, prefix):
        self._nc = nc
        self._p = prefix

    def sbuf_tensor(self, name, *a, **k):
        return self._nc.sbuf_tensor(self._p + name, *a, **k)

    def psum_tensor(self, name, *a, **k):
        return self._nc.psum_tensor(self._p + name, *a, **k)

    @property
    def gpsimd(self):
        return self._nc.gpsimd


def new_program(races=True):
    nc = bass.Bass("TRN2", target_bir_lowering=False, detect_race_conditions=races)
    return nc


def make_sems(nc, ctx):
    return tuple(Sem(ctx.enter_context(nc.semaphore(nm)))
                 for nm in ("ld", "pe", "dv", "ac", "go", "st", "lg"))


def full_barrier(E, sems):
    waits = [(s, s.n) for s in sems]
    for e in ("sync", "scalar", "vector", "tensor", "gpsimd"):
        E.emit(e, None, wait=waits)


def finish_program(nc, ctx, E, sems):
    ld, pe, dv, ac, go, st, lg = sems
    E.emit("gpsimd", None, wait=[(st, st.n), (lg, lg.n)])
    block = ctx.enter_context(nc.Block())
    E.replay(block)


def build_attn(layer):
    from contextlib import ExitStack
    nc = new_program()
    io = {"x": nc.dram_tensor("x", [TOK, D], F32, kind="ExternalInput").ap(),
          "identb": nc.dram_tensor("identb", [128, 128], BF16, kind="ExternalInput").ap(),
          "xo": nc.dram_tensor("xo", [TOK, D], F32, kind="ExternalOutput").ap()}
    if layer == 1:
        for n, sh in (("wq", [D, D]), ("wk", [D, D]), ("wv", [D, D]), ("wo", [D, D]),
                      ("gamma", [1, D]), ("beta", [1, D])):
            io[n] = nc.dram_tensor(n, sh, F32, kind="ExternalInput").ap()
    else:
        io["wqkvT"] = nc.dram_tensor("wqkvT", [D, 3 * D], F32, kind="ExternalInput").ap()
        io["woT"] = nc.dram_tensor("woT", [D, D], F32, kind="ExternalInput").ap()
        io["freqs"] = nc.dram_tensor("freqs", [1, 32], F32, kind="ExternalInput").ap()
        io["repmat"] = nc.dram_tensor("repmat", [32, 128], BF16, kind="ExternalInput").ap()
        io["repsign"] = nc.dram_tensor("repsign", [32, 128], BF16, kind="ExternalInput").ap()
        io["swapm"] = nc.dram_tensor("swapm", [128, 128], BF16, kind="ExternalInput").ap()
        io["pa_rows"] = nc.dram_tensor("pa_rows", [2, 16], F32, kind="ExternalInput").ap()
    with ExitStack() as octx:
        sems = make_sems(nc, octx)
        ld, pe, dv, ac, go, st, lg = sems
        E = Emitter(auto={"vector": dv, "scalar": ac, "gpsimd": go})
        emit_attn(nc, E, sems, io, layer)
        finish_program(nc, octx, E, sems)
    return nc


def _bf(a):
    return np.asarray(a).astype(ml_dtypes.bfloat16)

def make_consts():
    identb = _bf(np.eye(128, dtype=np.float32))
    rep = np.zeros((32, 128), np.float32)
    for p in range(128):
        rep[p % 32, p] = 1.0
    repsign = np.zeros((32, 128), np.float32)
    for p in range(128):
        sgn = -1.0 if (p % 64) < 32 else 1.0
        repsign[p % 32, p] = sgn
    swap = np.zeros((128, 128), np.float32)
    for p in range(128):
        q = (p % 64)
        src = (p - p % 64) + ((q + 32) % 64)
        swap[src, p] = 1.0
    pos = np.arange(S, dtype=np.float32)[None, :]
    return identb, _bf(rep), _bf(repsign), _bf(swap), pos


def shard_x(xfull):
    return [np.ascontiguousarray(xfull[CORE_BATCHES[i]].reshape(TOK, D)) for i in range(NCORES)]


def unshard_x(shards):
    out = np.empty((B, S, D), np.float32)
    for i in range(NCORES):
        sh = shards[i].reshape(SEQ_PER_CORE, S, D)
        out[CORE_BATCHES[i][0]] = sh[0]
        out[CORE_BATCHES[i][1]] = sh[1]
    return out


def run_attn(layer, xs, weights):
    return run_attn_cached(layer, xs, weights)


def _run_attn_old(layer, xs, weights):
    nc = build_attn(layer)
    identb, rep, repsign, swap, pos = make_consts()
    in_maps = []
    for i in range(NCORES):
        m = {"x": xs[i], "identb": identb}
        if layer == 1:
            m.update(wq=weights["wq"], wk=weights["wk"], wv=weights["wv"], wo=weights["wo"],
                     gamma=weights["gamma"], beta=weights["beta"])
        else:
            m.update(wqkvT=weights["wqkvT"], woT=weights["woT"],
                     freqs=weights["freqs"], repmat=rep, repsign=repsign, swapm=swap,
                     pa_rows=weights["pa_rows"])
        in_maps.append(m)
    res = run_bass_kernel_spmd(nc, in_maps, core_ids=list(range(NCORES)))
    return [r["xo"] for r in res.results]





def emit_ffn_l1(nc, E, sems, io):
    x = io["x"]
    identb, identf = io["identb"], io["identf"]
    ln_w, ln_b = io["f1_ln_w"], io["f1_ln_b"]
    router_w = io["f1_router_w"]
    gate_w, up_w, down_w = io["f1_gate_w"], io["f1_up_w"], io["f1_down_w"]
    sh_gate, sh_up, sh_down = io["f1_sh_gate"], io["f1_sh_up"], io["f1_sh_down"]
    xo = io["xo"]
    xfT_dram = io["xfT_dram"]
    routed_dram = io["y_dram"]

    NT = TOK // 128
    NST = TOK // 512
    from contextlib import ExitStack
    with ExitStack() as ctx:
        wg = ctx.enter_context(nc.sbuf_tensor("wg", [128, 8, 2048], BF16))
        wu = ctx.enter_context(nc.sbuf_tensor("wu", [128, 8, 2048], BF16))
        wd = ctx.enter_context(nc.sbuf_tensor("wd", [128, 16, D], BF16))
        xfTs = ctx.enter_context(nc.sbuf_tensor("xfTs", [128, 8, 512], BF16))
        xeTs = ctx.enter_context(nc.sbuf_tensor("xeTs", [128, 8, 512], BF16))
        hT = ctx.enter_context(nc.sbuf_tensor("hT", [128, 16, 512], BF16))
        xt = ctx.enter_context(nc.sbuf_tensor("xt", [128, D], F32))
        xa = ctx.enter_context(nc.sbuf_tensor("xa", [128, D], F32))
        rt = ctx.enter_context(nc.sbuf_tensor("rt", [128, D], F32))
        xout = ctx.enter_context(nc.sbuf_tensor("xout", [128, D], F32))
        xfTc = ctx.enter_context(nc.sbuf_tensor("xfTc", [128, 128], F32))
        rw_s = ctx.enter_context(nc.sbuf_tensor("rw_s", [128, 8, 8], F32))
        bn6 = ctx.enter_context(nc.sbuf_tensor("bn6", [128, 12], F32))
        bnag = ctx.enter_context(nc.sbuf_tensor("bnag", [128, 2], F32))
        stats_all = ctx.enter_context(nc.sbuf_tensor("stats_all", [128, NT, 2], F32))
        stats2 = ctx.enter_context(nc.sbuf_tensor("stats2", [128, 2], F32))
        gb = ctx.enter_context(nc.sbuf_tensor("gb", [128, 2 * D], F32))
        cc = ctx.enter_context(nc.sbuf_tensor("cc", [128, 2], F32))
        lrow = ctx.enter_context(nc.sbuf_tensor("lrow", [128, 8], F32))
        m1c = ctx.enter_context(nc.sbuf_tensor("m1c", [128, 1], F32))
        wtok = ctx.enter_context(nc.sbuf_tensor("wtok", [128, 1], F32))
        s_store = ctx.enter_context(nc.sbuf_tensor("s_store", [128, NT, 8], F32))
        srowf = ctx.enter_context(nc.sbuf_tensor("srowf", [1, 512], F32))
        srowb = ctx.enter_context(nc.sbuf_tensor("srowb", [1, 512], BF16))
        sbcast = ctx.enter_context(nc.sbuf_tensor("sbcast", [128, 512], BF16))
        scr = ctx.enter_context(nc.sbuf_tensor("scr", [128, 512], F32))
        scrB = ctx.enter_context(nc.sbuf_tensor("scrB", [128, 512], F32))
        yt = ctx.enter_context(nc.sbuf_tensor("yt", [128, 512], F32))
        ytB = ctx.enter_context(nc.sbuf_tensor("ytB", [128, 512], F32))
        onesb = ctx.enter_context(nc.sbuf_tensor("onesb", [128, 1], BF16))
        ones1 = ctx.enter_context(nc.sbuf_tensor("ones1", [1, 128], BF16))
        ones1f = ctx.enter_context(nc.sbuf_tensor("ones1f", [1, 128], F32))
        identb_s = ctx.enter_context(nc.sbuf_tensor("identb_s", [128, 128], BF16))
        identf_s = ctx.enter_context(nc.sbuf_tensor("identf_s", [128, 128], F32))
        ps_a = ctx.enter_context(nc.psum_tensor("ps_a", [128, 512], F32))
        ps_a2 = ctx.enter_context(nc.psum_tensor("ps_a2", [128, 512], F32))
        ps_b = ctx.enter_context(nc.psum_tensor("ps_b", [128, 512], F32))
        ps_b2 = ctx.enter_context(nc.psum_tensor("ps_b2", [128, 512], F32))
        ps_l = ctx.enter_context(nc.psum_tensor("ps_l", [128, 8], F32))
        ps_trf = ctx.enter_context(nc.psum_tensor("ps_trf", [128, 128], F32))
        ps_d = ctx.enter_context(nc.psum_tensor("ps_d", [128, 512], F32))
        ps_d2 = ctx.enter_context(nc.psum_tensor("ps_d2", [128, 512], F32))
        ld, pe, dv, ac, go, st, lg = sems

        E.step("vector", lambda eng: eng.memset(onesb[:, :], 1.0))
        E.step("vector", lambda eng: eng.memset(ones1[:, :], 1.0))
        E.step("vector", lambda eng: eng.memset(ones1f[:, :], 1.0))
        E.step("vector", lambda eng: eng.memset(cc[:, 0:1], EPS))
        E.step("vector", lambda eng: eng.memset(cc[:, 1:2], 1.0), sem=dv)
        E.dma("sync", identb_s[:, :], identb[:, :], ld)
        E.dma("sync", identf_s[:, :], identf[:, :], ld)
        for c in range(8):
            E.dma("sync", rw_s[:, c, :], router_w[128 * c:128 * (c + 1), :], ld)
        E.dma("sync", gb[0:1, 0:D], ln_w[:, :], ld)
        c_b = E.dma("sync", gb[0:1, D:2 * D], ln_b[:, :], ld)
        for half in range(4):
            def f_mm(eng, half=half):
                return eng.matmul(out=ps_a[:, :], lhsT=ones1f[:, :],
                                  rhs=gb[0:1, 512 * half:512 * (half + 1)], start=True, stop=True)
            E.step("tensor", f_mm, sem=pe, wait=[(ld, c_b), (dv, dv.n)])

            def f_ev(eng, half=half):
                return eng.tensor_copy(out=gb[:, 512 * half:512 * (half + 1)], in_=ps_a[:, :])
            E.step("vector", f_ev, sem=dv, wait=[(pe, pe.n)])

        for i in range(NT):
            c_x = E.dma("sync", xt[:, :], x[128 * i:128 * (i + 1), :], ld, wait=[(dv, dv.n)])
            emit_ln_stats(E, bn6, bnag, xt, D, stats_all[:, i, 0:1], stats_all[:, i, 1:2], dv, ac, cc[:, 0:1],
                          wait=[(ld, c_x)])

            def f_norm(eng, i=i):
                return eng.tensor_scalar(out=xa[:, :], in0=xt[:, :], scalar1=stats_all[:, i, 0:1],
                                         scalar2=stats_all[:, i, 1:2], op0=ALU.subtract, op1=ALU.mult)
            E.step("vector", f_norm, wait=[(ac, ac.n)])
            emit_ln_stats(E, bn6, bnag, xa, D, stats2[:, 0:1], stats2[:, 1:2], dv, ac, cc[:, 0:1])

            def f_norm2(eng):
                return eng.tensor_scalar(out=xa[:, :], in0=xa[:, :], scalar1=stats2[:, 0:1],
                                         scalar2=stats2[:, 1:2], op0=ALU.subtract, op1=ALU.mult)
            E.step("vector", f_norm2, wait=[(ac, ac.n)])

            def f_gam(eng):
                return eng.tensor_mul(out=xa[:, :], in0=xa[:, :], in1=gb[:, 0:D])
            E.step("vector", f_gam)

            def f_bet(eng):
                return eng.tensor_add(out=xa[:, :], in0=xa[:, :], in1=gb[:, D:2 * D])
            E.step("vector", f_bet, sem=dv)
            for c in range(8):
                def f_tr(eng, c=c):
                    return eng.transpose(out=ps_trf[:, :], in_=xa[:, 128 * c:128 * (c + 1)],
                                         identity=identf_s[:, :])
                E.step("tensor", f_tr, sem=pe, wait=[(dv, dv.n), (go, go.n)])

                def f_evt(eng):
                    return eng.tensor_copy(out=xfTc[:, :], in_=ps_trf[:, :])
                E.step("vector", f_evt, wait=[(pe, pe.n)])

                def f_cast(eng, c=c):
                    return eng.tensor_copy(out=xfTs[:, c, 0:128], in_=xfTc[:, :])
                E.step("vector", f_cast, sem=dv, wait=[(lg, lg.n)])

                def f_rmm(eng, c=c):
                    return eng.matmul(out=ps_l[:, :], lhsT=xfTc[:, :], rhs=rw_s[:, c, :],
                                      start=(c == 0), stop=(c == 7))
                E.step("tensor", f_rmm, sem=pe, wait=[(dv, dv.n)])
                E.dma("gpsimd", xfT_dram[c, :, 128 * i:128 * (i + 1)], xfTs[:, c, 0:128], lg,
                      wait=[(dv, dv.n)])
            def f_evl(eng):
                return eng.tensor_copy(out=lrow[:, :], in_=ps_l[:, :])
            E.step("vector", f_evl, wait=[(pe, pe.n)])

            def f_m1(eng):
                return eng.reduce_max(out=m1c[:, :], in_=lrow[:, :], axis=AX)
            E.step("vector", f_m1, sem=dv)

            def f_sig(eng):
                return eng.activation(out=wtok[:, :], in_=m1c[:, :], func=AF.Sigmoid)
            E.step("scalar", f_sig, sem=ac, wait=[(dv, dv.n)])

            def f_oh(eng, i=i):
                return eng.tensor_scalar(out=s_store[:, i, :], in0=lrow[:, :], scalar1=m1c[:, :],
                                         scalar2=None, op0=ALU.is_equal)
            E.step("vector", f_oh, wait=[(ac, ac.n)])

            def f_sc(eng, i=i):
                return eng.tensor_scalar(out=s_store[:, i, :], in0=s_store[:, i, :], scalar1=wtok[:, :],
                                         scalar2=None, op0=ALU.mult)
            E.step("vector", f_sc, sem=dv)

        for e in range(9):
            wprev = lg.n
            if e < 9:
                gsrc = gate_w[e, :, :] if e < 8 else sh_gate[:, :]
                usrc = up_w[e, :, :] if e < 8 else sh_up[:, :]
                dsrc = down_w[e, :, :] if e < 8 else sh_down[:, :]
            for c in range(8):
                E.dma("gpsimd", wg[:, c, :], gsrc[128 * c:128 * (c + 1), :], lg,
                      wait=[(pe, pe.n), (lg, wprev)])
                E.dma("gpsimd", wu[:, c, :], usrc[128 * c:128 * (c + 1), :], lg,
                      wait=[(pe, pe.n), (lg, wprev)])
            for c in range(16):
                E.dma("gpsimd", wd[:, c, :], dsrc[128 * c:128 * (c + 1), :], lg,
                      wait=[(pe, pe.n), (lg, wprev)])
            ld_w = lg.n
            for stp in range(NST):
                c_xf = 0
                for c in range(8):
                    c_xf = E.dma("sync", xfTs[:, c, 0:512], xfT_dram[c, :, 512 * stp:512 * (stp + 1)], ld,
                                 wait=[(pe, pe.n), (dv, dv.n), (lg, lg.n)])
                if e < 8:
                    for q in range(4):
                        tile = 4 * stp + q

                        def f_trs(eng, tile=tile, e=e):
                            return eng.transpose(out=ps_trf[0:1, 0:128], in_=s_store[:, tile, e:e + 1],
                                                 identity=identf_s[:, :])
                        E.step("tensor", f_trs, sem=pe, wait=[(dv, dv.n)])

                        def f_evs(eng, q=q):
                            return eng.tensor_copy(out=srowf[:, 128 * q:128 * (q + 1)], in_=ps_trf[0:1, 0:128])
                        E.step("vector", f_evs, sem=dv, wait=[(pe, pe.n)])

                    def f_sbf(eng):
                        return eng.tensor_copy(out=srowb[:, :], in_=srowf[:, :])
                    E.step("vector", f_sbf, sem=dv)

                    def f_sb(eng):
                        return eng.matmul(out=ps_b[:, :], lhsT=ones1[:, :], rhs=srowb[:, :],
                                          start=True, stop=True)
                    E.step("tensor", f_sb, sem=pe, wait=[(dv, dv.n)])

                    def f_evsb(eng):
                        return eng.tensor_copy(out=sbcast[:, :], in_=ps_b[:, :])
                    E.step("vector", f_evsb, sem=dv, wait=[(pe, pe.n), (ld, c_xf)])
                    for c in range(8):
                        def f_xe(eng, c=c):
                            return eng.tensor_mul(out=xeTs[:, c, :], in0=xfTs[:, c, :], in1=sbcast[:, :])
                        E.step("vector", f_xe, sem=dv)
                    xin = xeTs
                    xe_done = dv.n
                else:
                    E.emit("vector", None, wait=[(ld, c_xf)])
                    xin = xfTs
                    xe_done = dv.n
                silu_done = {}
                h_done = {}
                for ich in range(16):
                    pg = ps_a if ich % 2 == 0 else ps_a2
                    pu = ps_b if ich % 2 == 0 else ps_b2
                    sc = scr if ich % 2 == 0 else scrB

                    def f_zg(eng, ich=ich, xin=xin, pg=pg):
                        inst = None
                        for dch in range(8):
                            inst = eng.matmul(out=pg[:, :], lhsT=wg[:, dch, 128 * ich:128 * (ich + 1)],
                                              rhs=xin[:, dch, :], start=(dch == 0), stop=(dch == 7))
                        return inst
                    E.step("tensor", f_zg, sem=pe,
                           wait=[(lg, ld_w), (ld, c_xf), (dv, xe_done),
                                 (ac, silu_done.get(ich - 2, ac.n if ich < 2 else 0))])

                    def f_silu(eng, sc=sc, pg=pg):
                        return eng.activation(out=sc[:, :], in_=pg[:, :], func=AF.Silu)
                    silu_done[ich] = E.step("scalar", f_silu, sem=ac,
                                            wait=[(pe, pe.n), (dv, h_done.get(ich - 2, dv.n if ich < 2 else 0))])

                    def f_zu(eng, ich=ich, xin=xin, pu=pu):
                        inst = None
                        for dch in range(8):
                            inst = eng.matmul(out=pu[:, :], lhsT=wu[:, dch, 128 * ich:128 * (ich + 1)],
                                              rhs=xin[:, dch, :], start=(dch == 0), stop=(dch == 7))
                        return inst
                    E.step("tensor", f_zu, sem=pe,
                           wait=[(dv, h_done.get(ich - 2, dv.n if ich < 2 else 0))])

                    def f_h(eng, ich=ich, sc=sc, pu=pu):
                        return eng.tensor_mul(out=hT[:, ich, :], in0=sc[:, :], in1=pu[:, :])
                    h_done[ich] = E.step("vector", f_h, sem=dv, wait=[(pe, pe.n), (ac, silu_done[ich])])
                evy_done = {}
                acc_done = {}
                for k8, (tt, dh) in enumerate([(a, b) for a in range(4) for b in range(2)]):
                    pd = ps_d if k8 % 2 == 0 else ps_d2
                    yb = yt if k8 % 2 == 0 else ytB

                    def f_dn(eng, tt=tt, dh=dh, pd=pd):
                        inst = None
                        for ich in range(16):
                            inst = eng.matmul(out=pd[:, :], lhsT=hT[:, ich, 128 * tt:128 * (tt + 1)],
                                              rhs=wd[:, ich, 512 * dh:512 * (dh + 1)],
                                              start=(ich == 0), stop=(ich == 15))
                        return inst
                    E.step("tensor", f_dn, sem=pe,
                           wait=[(dv, dv.n if k8 < 2 else evy_done.get(k8 - 2, 0)),
                                 (go, go.n if k8 == 0 else 0)])

                    def f_evy(eng, pd=pd, yb=yb):
                        return eng.tensor_copy(out=yb[:, :], in_=pd[:, :])
                    evy_done[k8] = E.step("vector", f_evy, sem=dv,
                                          wait=[(pe, pe.n), (lg, lg.n)])
                    t0 = 512 * stp + 128 * tt

                    def f_acc(eng, t0=t0, dh=dh, e=e, yb=yb):
                        return eng.dma_start(out=routed_dram[t0:t0 + 128, 512 * dh:512 * (dh + 1)],
                                             in_=yb[:, :],
                                             accum_op=(ALU.bypass if e == 0 else ALU.add))
                    lg.n += 16
                    acc_done[k8] = lg.n
                    E.emit("gpsimd", f_acc, wait=[(dv, dv.n)], inc=[(lg, 16)])

        acc_done = lg.n
        for i in range(NT):
            c_x = E.dma("sync", xt[:, :], x[128 * i:128 * (i + 1), :], ld,
                        wait=[(dv, dv.n), (lg, acc_done)])
            c_r = E.dma("sync", rt[:, :], routed_dram[128 * i:128 * (i + 1), :], ld,
                        wait=[(st, st.n)])

            def f_h2(eng, i=i):
                return eng.tensor_scalar(out=xa[:, :], in0=xt[:, :], scalar1=stats_all[:, i, 0:1],
                                         scalar2=stats_all[:, i, 1:2], op0=ALU.subtract, op1=ALU.mult)
            E.step("vector", f_h2, wait=[(ld, c_r)])

            def f_add(eng):
                return eng.tensor_add(out=xa[:, :], in0=xa[:, :], in1=rt[:, :])
            E.step("vector", f_add)

            def f_out(eng):
                return eng.scalar_tensor_tensor(out=xout[:, :], in0=xa[:, :], scalar=0.5, in1=xt[:, :],
                                                op0=ALU.mult, op1=ALU.add)
            E.step("vector", f_out, sem=dv)
            E.dma("scalar", xo[128 * i:128 * (i + 1), :], xout[:, :], st, wait=[(dv, dv.n)])





def ffn1_io(nc, io):
    io["f1_ln_w"] = nc.dram_tensor("f1_ln_w", [1, D], F32, kind="ExternalInput").ap()
    io["f1_ln_b"] = nc.dram_tensor("f1_ln_b", [1, D], F32, kind="ExternalInput").ap()
    io["f1_router_w"] = nc.dram_tensor("f1_router_w", [D, 8], F32, kind="ExternalInput").ap()
    io["f1_gate_w"] = nc.dram_tensor("f1_gate_w", [8, D, 2048], F32, kind="ExternalInput").ap()
    io["f1_up_w"] = nc.dram_tensor("f1_up_w", [8, D, 2048], F32, kind="ExternalInput").ap()
    io["f1_down_w"] = nc.dram_tensor("f1_down_w", [8, 2048, D], F32, kind="ExternalInput").ap()
    io["f1_sh_gate"] = nc.dram_tensor("f1_sh_gate", [D, 2048], F32, kind="ExternalInput").ap()
    io["f1_sh_up"] = nc.dram_tensor("f1_sh_up", [D, 2048], F32, kind="ExternalInput").ap()
    io["f1_sh_down"] = nc.dram_tensor("f1_sh_down", [2048, D], F32, kind="ExternalInput").ap()


def build_ffn_l1():
    from contextlib import ExitStack
    nc = new_program()
    io = {"x": nc.dram_tensor("x", [TOK, D], F32, kind="ExternalInput").ap(),
          "identb": nc.dram_tensor("identb", [128, 128], BF16, kind="ExternalInput").ap(),
          "identf": nc.dram_tensor("identf", [128, 128], F32, kind="ExternalInput").ap(),
          "xo": nc.dram_tensor("xo", [TOK, D], F32, kind="ExternalOutput").ap(),
          "xfT_dram": nc.dram_tensor("xfT_dram", [8, 128, TOK], BF16, kind="Internal").ap(),
          "y_dram": nc.dram_tensor("y_dram", [TOK, D], F32, kind="Internal").ap()}
    ffn1_io(nc, io)
    with ExitStack() as octx:
        sems = make_sems(nc, octx)
        ld, pe, dv, ac, go, st, lg = sems
        E = Emitter(auto={"vector": dv, "scalar": ac, "gpsimd": go})
        emit_ffn_l1(nc, E, sems, io)
        finish_program(nc, octx, E, sems)
    return nc

def run_ffn_l1(xs, weights):
    return run_ffn_l1_cached(xs, weights)


def _run_ffn_l1_old(xs, weights):
    nc = build_ffn_l1()
    identb, rep, repsign, swap, pos = make_consts()
    identf = np.eye(128, dtype=np.float32)
    in_maps = []
    for i in range(NCORES):
        m = {"x": xs[i], "identb": identb, "identf": identf}
        m.update(weights)
        in_maps.append(m)
    res = run_bass_kernel_spmd(nc, in_maps, core_ids=list(range(NCORES)))
    return [r["xo"] for r in res.results]





def emit_ffn_l0(nc, E, sems, io, counts_only, sparse=True):
    x = io["x"]
    identf = io["identf"]
    router_w = io["f0_router_w"]
    ltri = io["ltri"]
    if not counts_only:
        identb = io["identb"]
        w1, b1, w2, b2 = io["f0_w1"], io["f0_b1"], io["f0_w2"], io["f0_b2"]
        cnt_all = io["cnt_all"]
        bmask = io["bmask"]
        xo = io["xo"]
        xfT_dram = io["xfT_dram"]
        y_dram = io["y_dram"]
        if sparse:
            disp_dram = io["disp_dram"]
            eout_dram = io["eout_dram"]
    else:
        cnt_out = io["cnt_out"]

    NT = TOK // 128
    NST = TOK // 512
    TPB = S // 128
    CAP = float(CAP0)
    from contextlib import ExitStack
    with ExitStack() as ctx:
        if not counts_only:
            wA = ctx.enter_context(nc.sbuf_tensor("wA", [128, 8, 4096], BF16))
            wB = ctx.enter_context(nc.sbuf_tensor("wB", [128, 32, D], BF16))
            hT = ctx.enter_context(nc.sbuf_tensor("hT", [128, 32, 512], BF16))
            xfTs = ctx.enter_context(nc.sbuf_tensor("xfTs", [128, 8, 512], BF16))
            b1c = ctx.enter_context(nc.sbuf_tensor("b1c", [128, 32], F32))
            b1r = ctx.enter_context(nc.sbuf_tensor("b1r", [1, 4096], F32))
            b2r = ctx.enter_context(nc.sbuf_tensor("b2r", [1, D], BF16))
            if not sparse:
                geff_store = ctx.enter_context(nc.sbuf_tensor("geff_store", [128, NT, 8], F32))
            cnt_s = ctx.enter_context(nc.sbuf_tensor("cnt_s", [16, 8], F32))
            bmask_s = ctx.enter_context(nc.sbuf_tensor("bmask_s", [16, 2], F32))
            offr = ctx.enter_context(nc.sbuf_tensor("offr", [2, 8], F32))
            offr1 = ctx.enter_context(nc.sbuf_tensor("offr1", [1, 8], F32))
            obc = ctx.enter_context(nc.sbuf_tensor("obc", [128, 16], F32))
            if not sparse:
                yt = ctx.enter_context(nc.sbuf_tensor("yt", [128, 512], F32))
                ytB = ctx.enter_context(nc.sbuf_tensor("ytB", [128, 512], F32))
            else:
                yt = ytB = None
            if sparse:
                ybf = ctx.enter_context(nc.sbuf_tensor("ybf", [128, 512], BF16))
                ybfB = ctx.enter_context(nc.sbuf_tensor("ybfB", [128, 512], BF16))
                idx_store = ctx.enter_context(nc.sbuf_tensor("idx_store", [128, NT, 2], U32))
                gk_store = ctx.enter_context(nc.sbuf_tensor("gk_store", [128, NT, 2], F32))
                cb0 = ctx.enter_context(nc.sbuf_tensor("cb0", [128, 8], F32))
                io8 = ctx.enter_context(nc.sbuf_tensor("io8", [128, 8], F32))
                io8r = ctx.enter_context(nc.sbuf_tensor("io8r", [1, 8], F32))
                dtile = ctx.enter_context(nc.sbuf_tensor("dtile", [128, D], BF16))
                gbuf1 = ctx.enter_context(nc.sbuf_tensor("gbuf1", [128, D], BF16))
                gbuf2 = ctx.enter_context(nc.sbuf_tensor("gbuf2", [128, D], BF16))
            identb_s = ctx.enter_context(nc.sbuf_tensor("identb_s", [128, 128], BF16))
            if not sparse:
                rt = ctx.enter_context(nc.sbuf_tensor("rt", [128, D], F32))
        xt = ctx.enter_context(nc.sbuf_tensor("xt", [128, D], F32))
        xa = ctx.enter_context(nc.sbuf_tensor("xa", [128, D], F32))
        xfTc = ctx.enter_context(nc.sbuf_tensor("xfTc", [128, 128], F32))
        if counts_only or not sparse:
            xfb = ctx.enter_context(nc.sbuf_tensor("xfb", [128, 128], BF16))
        rw_s = ctx.enter_context(nc.sbuf_tensor("rw_s", [128, 8, 8], F32))
        ltri_s = ctx.enter_context(nc.sbuf_tensor("ltri_s", [128, 128], F32))
        bn6 = ctx.enter_context(nc.sbuf_tensor("bn6", [128, 12], F32))
        bnag = ctx.enter_context(nc.sbuf_tensor("bnag", [128, 2], F32))
        stats = ctx.enter_context(nc.sbuf_tensor("stats", [128, 2], F32))
        cc = ctx.enter_context(nc.sbuf_tensor("cc", [128, 2], F32))
        lpad = ctx.enter_context(nc.sbuf_tensor("lpad", [128, 16], F32))
        m8 = ctx.enter_context(nc.sbuf_tensor("m8", [128, 8], F32))
        oh1 = ctx.enter_context(nc.sbuf_tensor("oh1", [128, 8], F32))
        oh2 = ctx.enter_context(nc.sbuf_tensor("oh2", [128, 8], F32))
        ohb = ctx.enter_context(nc.sbuf_tensor("ohb", [128, 8], F32))
        cum = ctx.enter_context(nc.sbuf_tensor("cum", [128, 8], F32))
        carry = ctx.enter_context(nc.sbuf_tensor("carry", [1, 8], F32))
        tsc = ctx.enter_context(nc.sbuf_tensor("tsc", [128, 8], F32))
        col = ctx.enter_context(nc.sbuf_tensor("col", [128, 8], F32))
        onesf = ctx.enter_context(nc.sbuf_tensor("onesf", [128, 1], F32))
        ones1 = ctx.enter_context(nc.sbuf_tensor("ones1", [1, 128], BF16))
        ones1f = ctx.enter_context(nc.sbuf_tensor("ones1f", [1, 128], F32))
        identf_s = ctx.enter_context(nc.sbuf_tensor("identf_s", [128, 128], F32))
        ps_a = ctx.enter_context(nc.psum_tensor("ps_a", [128, 512], F32))
        ps_a2 = ctx.enter_context(nc.psum_tensor("ps_a2", [128, 512], F32))
        ps_d = ctx.enter_context(nc.psum_tensor("ps_d", [128, 512], F32))
        ps_d2 = ctx.enter_context(nc.psum_tensor("ps_d2", [128, 512], F32))
        ps_small = ctx.enter_context(nc.psum_tensor("ps_small", [128, 24], F32))
        ps_l = ps_small[:, 0:8]
        ps_cum = ps_small[:, 8:16]
        ps_cnt = ps_small[0:1, 16:24]
        ps_trf = ctx.enter_context(nc.psum_tensor("ps_trf", [128, 128], F32))
        if not counts_only and sparse:
            ps_trb = ctx.enter_context(nc.psum_tensor("ps_trb", [128, 128], BF16))
        ld, pe, dv, ac, go, st, lg = sems

        E.step("vector", lambda eng: eng.memset(onesf[:, :], 1.0))
        E.step("vector", lambda eng: eng.memset(ones1[:, :], 1.0))
        E.step("vector", lambda eng: eng.memset(ones1f[:, :], 1.0))
        E.step("vector", lambda eng: eng.memset(cc[:, 0:1], EPS))
        E.step("vector", lambda eng: eng.memset(cc[:, 1:2], 1.0))
        E.step("vector", lambda eng: eng.memset(lpad[:, 8:16], -1e30))
        E.step("vector", lambda eng: eng.memset(carry[:, :], 0.0), sem=dv)
        E.dma("sync", identf_s[:, :], identf[:, :], ld)
        E.dma("sync", ltri_s[:, :], ltri[:, :], ld)
        for c in range(8):
            E.dma("sync", rw_s[:, c, :], router_w[128 * c:128 * (c + 1), :], ld)
        if not counts_only:
            E.dma("sync", identb_s[:, :], identb[:, :], ld)
            E.dma("sync", cnt_s[:, :], cnt_all[:, :], ld)
            c_bm = E.dma("sync", bmask_s[:, :], bmask[:, :], ld)
            def f_off(eng):
                return eng.matmul(out=ps_cnt[0:1, 0:8] if False else ps_l[0:8, 0:2], lhsT=cnt_s[:, :],
                                  rhs=bmask_s[:, :], start=True, stop=True)
            E.step("tensor", f_off, sem=pe, wait=[(ld, c_bm)])

            def f_evoff(eng):
                return eng.tensor_copy(out=offr[:, :] if False else col[0:8, 0:2], in_=ps_l[0:8, 0:2])
            E.step("vector", f_evoff, sem=dv, wait=[(pe, pe.n)])

            def f_troff(eng):
                return eng.transpose(out=ps_trf[0:2, 0:8], in_=col[0:8, 0:2], identity=identf_s[0:8, 0:8])
            E.step("tensor", f_troff, sem=pe, wait=[(dv, dv.n)])

            def f_evoff2(eng):
                return eng.tensor_copy(out=offr[:, :], in_=ps_trf[0:2, 0:8])
            E.step("vector", f_evoff2, sem=dv, wait=[(pe, pe.n)])
            E.dma("gpsimd", offr1[:, :], offr[1:2, :], lg, wait=[(dv, dv.n)])
            def f_ob0(eng):
                return eng.matmul(out=ps_l[:, 0:8], lhsT=ones1f[:, :], rhs=offr[0:1, :], start=True, stop=True)
            E.step("tensor", f_ob0, sem=pe, wait=[(lg, lg.n), (dv, dv.n)])

            def f_evob0(eng):
                return eng.tensor_copy(out=obc[:, 0:8], in_=ps_l[:, 0:8])
            E.step("vector", f_evob0, sem=dv, wait=[(pe, pe.n)])

            def f_ob1(eng):
                return eng.matmul(out=ps_l[:, 0:8], lhsT=ones1f[:, :], rhs=offr1[0:1, :], start=True, stop=True)
            E.step("tensor", f_ob1, sem=pe, wait=[(dv, dv.n)])

            def f_evob1(eng):
                return eng.tensor_copy(out=obc[:, 8:16], in_=ps_l[:, 0:8])
            E.step("vector", f_evob1, sem=dv, wait=[(pe, pe.n)])
            if sparse:
                def f_io8(eng):
                    return eng.iota(out=io8r[:, :], pattern=[[1, 8]], base=0,
                                    channel_multiplier=0, allow_small_or_imprecise_dtypes=True)
                E.step("gpsimd", f_io8, sem=go)

                def f_io8b(eng):
                    return eng.matmul(out=ps_l[:, 0:8], lhsT=ones1f[:, :], rhs=io8r[:, :],
                                      start=True, stop=True)
                E.step("tensor", f_io8b, sem=pe, wait=[(go, go.n), (dv, dv.n)])

                def f_evio8(eng):
                    return eng.tensor_copy(out=io8[:, :], in_=ps_l[:, 0:8])
                E.step("vector", f_evio8, sem=dv, wait=[(pe, pe.n)])
                rbound = nc.gpsimd.alloc_register()
                E.emit("gpsimd", lambda eng: eng.reg_mov(rbound, 8 * CLOC - 1))
                E.step("vector", lambda eng: eng.memset(gbuf1[:, :], 0.0), sem=dv)
                for r in range(8 * CLOC // 128):
                    E.dma("gpsimd", disp_dram[128 * r:128 * (r + 1), :], gbuf1[:, :], lg,
                          wait=[(dv, dv.n)])
                zero_done = lg.n

        for i in range(NT):
            b_loc = i // TPB
            ib = i % TPB
            c_x = E.dma("sync", xt[:, :], x[128 * i:128 * (i + 1), :], ld, wait=[(dv, dv.n)])
            emit_ln_stats(E, bn6, bnag, xt, D, stats[:, 0:1], stats[:, 1:2], dv, ac, cc[:, 0:1],
                          wait=[(ld, c_x)])

            def f_norm(eng):
                return eng.tensor_scalar(out=xa[:, :], in0=xt[:, :], scalar1=stats[:, 0:1],
                                         scalar2=stats[:, 1:2], op0=ALU.subtract, op1=ALU.mult)
            E.step("vector", f_norm, sem=dv, wait=[(ac, ac.n), (lg, lg.n)])
            for c in range(8):
                def f_tr(eng, c=c):
                    return eng.transpose(out=ps_trf[:, :], in_=xa[:, 128 * c:128 * (c + 1)],
                                         identity=identf_s[:, :])
                E.step("tensor", f_tr, sem=pe, wait=[(dv, dv.n), (go, go.n)])

                def f_evt(eng):
                    return eng.tensor_copy(out=xfTc[:, :], in_=ps_trf[:, :])
                E.step("vector", f_evt, wait=[(pe, pe.n)])
                if not counts_only and not sparse:
                    def f_cast(eng, c=c):
                        return eng.tensor_copy(out=xfb[:, :], in_=xfTc[:, :])
                    E.step("vector", f_cast, sem=dv, wait=[(lg, lg.n)])
                    E.dma("gpsimd", xfT_dram[c, :, 128 * i:128 * (i + 1)], xfb[:, :], lg,
                          wait=[(dv, dv.n)])
                else:
                    E.step("vector", lambda eng: eng.engine_nop(), sem=dv)

                def f_rmm(eng, c=c):
                    return eng.matmul(out=ps_l[:, 0:8], lhsT=xfTc[:, :], rhs=rw_s[:, c, :],
                                      start=(c == 0), stop=(c == 7))
                E.step("tensor", f_rmm, sem=pe, wait=[(dv, dv.n)])

            def f_evl(eng):
                return eng.tensor_copy(out=lpad[:, 0:8], in_=ps_l[:, 0:8])
            E.step("vector", f_evl, wait=[(pe, pe.n)])

            def f_max8(eng):
                return eng.max(out=m8[:, :], in_=lpad[:, :])
            E.step("vector", f_max8)

            def f_oh1(eng):
                return eng.tensor_scalar(out=oh1[:, :], in0=lpad[:, 0:8], scalar1=m8[:, 0:1],
                                         scalar2=None, op0=ALU.is_equal)
            E.step("vector", f_oh1)

            def f_ohm(eng):
                return eng.tensor_scalar(out=oh2[:, :], in0=lpad[:, 0:8], scalar1=m8[:, 1:2],
                                         scalar2=None, op0=ALU.is_equal)
            E.step("vector", f_ohm)

            def f_not1(eng):
                return eng.tensor_scalar(out=tsc[:, :], in0=oh1[:, :], scalar1=-1.0, scalar2=1.0,
                                         op0=ALU.mult, op1=ALU.add)
            E.step("vector", f_not1)

            def f_oh2(eng):
                return eng.tensor_mul(out=oh2[:, :], in0=oh2[:, :], in1=tsc[:, :])
            E.step("vector", f_oh2)

            def f_ohb(eng):
                return eng.tensor_add(out=ohb[:, :], in0=oh1[:, :], in1=oh2[:, :])
            E.step("vector", f_ohb, sem=dv)

            def f_cum(eng, ib=ib):
                inst = eng.matmul(out=ps_cum[:, 0:8], lhsT=ltri_s[:, :], rhs=ohb[:, :],
                                  start=True, stop=(ib == 0))
                if ib != 0:
                    inst = eng.matmul(out=ps_cum[:, 0:8], lhsT=ones1f[:, :], rhs=carry[:, :],
                                      start=False, stop=True)
                return inst
            E.step("tensor", f_cum, sem=pe, wait=[(dv, dv.n)])

            def f_ccnt(eng):
                return eng.matmul(out=ps_cnt[:, 0:8], lhsT=onesf[:, :], rhs=ohb[:, :],
                                  start=True, stop=True)
            E.step("tensor", f_ccnt, sem=pe)

            def f_evcum(eng):
                return eng.tensor_copy(out=cum[:, :], in_=ps_cum[:, 0:8])
            E.step("vector", f_evcum, wait=[(pe, pe.n)])

            if ib == 0:
                def f_carry(eng):
                    return eng.tensor_copy(out=carry[:, :], in_=ps_cnt[:, 0:8])
                E.step("vector", f_carry, sem=dv, wait=[(lg, lg.n)])
            else:
                def f_carry(eng):
                    return eng.tensor_add(out=carry[:, :], in0=carry[:, :], in1=ps_cnt[:, 0:8])
                E.step("vector", f_carry, sem=dv)

            if counts_only:
                if ib == TPB - 1:
                    E.dma("gpsimd", cnt_out[b_loc:b_loc + 1, :], carry[:, :], lg, wait=[(dv, dv.n)])
                continue
            if sparse and ib == TPB - 1 and b_loc == 0:
                def f_cb0(eng):
                    return eng.matmul(out=ps_l[:, 0:8], lhsT=ones1f[:, :], rhs=carry[:, :],
                                      start=True, stop=True)
                E.step("tensor", f_cb0, sem=pe, wait=[(dv, dv.n)])

                def f_evcb0(eng):
                    return eng.tensor_copy(out=cb0[:, :], in_=ps_l[:, 0:8])
                E.step("vector", f_evcb0, sem=dv, wait=[(pe, pe.n)])

            def f_p1(eng):
                return eng.tensor_mul(out=tsc[:, :], in0=oh1[:, :], in1=cum[:, :])
            E.step("vector", f_p1)

            def f_p1s(eng):
                return eng.reduce_sum(out=col[:, 0:1], in_=tsc[:, :], axis=AX)
            E.step("vector", f_p1s)

            def f_o1(eng, b_loc=b_loc):
                return eng.tensor_mul(out=tsc[:, :], in0=oh1[:, :], in1=obc[:, 8 * b_loc:8 * (b_loc + 1)])
            E.step("vector", f_o1)

            def f_o1s(eng):
                return eng.reduce_sum(out=col[:, 1:2], in_=tsc[:, :], axis=AX)
            E.step("vector", f_o1s)

            def f_p2(eng):
                return eng.tensor_mul(out=tsc[:, :], in0=oh2[:, :], in1=cum[:, :])
            E.step("vector", f_p2)

            def f_p2s(eng):
                return eng.reduce_sum(out=col[:, 2:3], in_=tsc[:, :], axis=AX)
            E.step("vector", f_p2s)

            def f_o2(eng, b_loc=b_loc):
                return eng.tensor_mul(out=tsc[:, :], in0=oh2[:, :], in1=obc[:, 8 * b_loc:8 * (b_loc + 1)])
            E.step("vector", f_o2)

            def f_o2s(eng):
                return eng.reduce_sum(out=col[:, 3:4], in_=tsc[:, :], axis=AX)
            E.step("vector", f_o2s)

            def f_k1(eng):
                return eng.tensor_add(out=col[:, 4:5], in0=col[:, 0:1], in1=col[:, 1:2])
            E.step("vector", f_k1)

            def f_k1b(eng):
                return eng.tensor_scalar(out=col[:, 4:5], in0=col[:, 4:5], scalar1=-1.0, scalar2=CAP,
                                         op0=ALU.mult, op1=ALU.add)
            E.step("vector", f_k1b)

            def f_k1c(eng):
                return eng.tensor_scalar_max(col[:, 4:5], col[:, 4:5], 0.0)
            E.step("vector", f_k1c)

            def f_k1d(eng):
                return eng.tensor_scalar_min(col[:, 4:5], col[:, 4:5], 1.0)
            E.step("vector", f_k1d)

            def f_k2(eng):
                return eng.tensor_add(out=col[:, 5:6], in0=col[:, 2:3], in1=col[:, 3:4])
            E.step("vector", f_k2)

            def f_k2b(eng):
                return eng.tensor_scalar(out=col[:, 5:6], in0=col[:, 5:6], scalar1=-1.0, scalar2=CAP,
                                         op0=ALU.mult, op1=ALU.add)
            E.step("vector", f_k2b)

            def f_k2c(eng):
                return eng.tensor_scalar_max(col[:, 5:6], col[:, 5:6], 0.0)
            E.step("vector", f_k2c)

            def f_k2d(eng):
                return eng.tensor_scalar_min(col[:, 5:6], col[:, 5:6], 1.0)
            E.step("vector", f_k2d)

            def f_d(eng):
                return eng.tensor_sub(out=col[:, 6:7], in0=m8[:, 1:2], in1=m8[:, 0:1])
            E.step("vector", f_d, sem=dv)

            def f_e(eng):
                return eng.activation(out=col[:, 6:7], in_=col[:, 6:7], func=AF.Exp)
            E.step("scalar", f_e, sem=ac, wait=[(dv, dv.n)])

            def f_g1(eng):
                return eng.tensor_scalar_add(col[:, 7:8], col[:, 6:7], 1.0)
            E.step("vector", f_g1, wait=[(ac, ac.n)])

            def f_g1b(eng):
                return eng.reciprocal(out=col[:, 6:7], in_=col[:, 7:8])
            E.step("vector", f_g1b)

            def f_g2(eng):
                return eng.tensor_scalar(out=col[:, 7:8], in0=col[:, 6:7], scalar1=-1.0, scalar2=1.0,
                                         op0=ALU.mult, op1=ALU.add)
            E.step("vector", f_g2)

            def f_gk1(eng):
                return eng.tensor_mul(out=col[:, 4:5], in0=col[:, 4:5], in1=col[:, 6:7])
            E.step("vector", f_gk1)

            def f_gk2(eng):
                return eng.tensor_mul(out=col[:, 5:6], in0=col[:, 5:6], in1=col[:, 7:8])
            E.step("vector", f_gk2)

            if not sparse:
                def f_ge1(eng, i=i):
                    return eng.tensor_scalar(out=geff_store[:, i, :], in0=oh1[:, :], scalar1=col[:, 4:5],
                                             scalar2=None, op0=ALU.mult)
                E.step("vector", f_ge1)

                def f_ge2(eng):
                    return eng.tensor_scalar(out=tsc[:, :], in0=oh2[:, :], scalar1=col[:, 5:6],
                                             scalar2=None, op0=ALU.mult)
                E.step("vector", f_ge2)

                def f_ge(eng, i=i):
                    return eng.tensor_add(out=geff_store[:, i, :], in0=geff_store[:, i, :], in1=tsc[:, :])
                E.step("vector", f_ge, sem=dv)
            else:
                for k_ in (0, 1):
                    ohk = oh1 if k_ == 0 else oh2
                    pslice = col[:, 0:1] if k_ == 0 else col[:, 2:3]
                    gslice = col[:, 4:5] if k_ == 0 else col[:, 5:6]

                    def f_ev_(eng, ohk=ohk):
                        return eng.tensor_mul(out=tsc[:, :], in0=ohk[:, :], in1=io8[:, :])
                    E.step("vector", f_ev_)

                    def f_evs_(eng):
                        return eng.reduce_sum(out=col[:, 6:7], in_=tsc[:, :], axis=AX)
                    E.step("vector", f_evs_)

                    def f_sb_(eng):
                        return eng.tensor_scalar(out=col[:, 6:7], in0=col[:, 6:7], scalar1=float(CLOC),
                                                 scalar2=None, op0=ALU.mult)
                    E.step("vector", f_sb_)
                    if b_loc == 1:
                        def f_lv_(eng, ohk=ohk):
                            return eng.tensor_mul(out=tsc[:, :], in0=ohk[:, :], in1=cb0[:, :])
                        E.step("vector", f_lv_)

                        def f_lvs_(eng):
                            return eng.reduce_sum(out=col[:, 7:8], in_=tsc[:, :], axis=AX)
                        E.step("vector", f_lvs_)

                        def f_lp_(eng, pslice=pslice):
                            return eng.tensor_add(out=col[:, 7:8], in0=col[:, 7:8], in1=pslice)
                        E.step("vector", f_lp_)
                    else:
                        def f_lp_(eng, pslice=pslice):
                            return eng.tensor_copy(out=col[:, 7:8], in_=pslice)
                        E.step("vector", f_lp_)

                    def f_slot_(eng):
                        return eng.tensor_add(out=col[:, 6:7], in0=col[:, 6:7], in1=col[:, 7:8])
                    E.step("vector", f_slot_)

                    def f_v1_(eng):
                        return eng.tensor_scalar(out=col[:, 7:8], in0=col[:, 7:8], scalar1=-1.0,
                                                 scalar2=float(CLOC), op0=ALU.mult, op1=ALU.add)
                    E.step("vector", f_v1_)

                    def f_v2_(eng):
                        return eng.tensor_scalar_max(col[:, 7:8], col[:, 7:8], 0.0)
                    E.step("vector", f_v2_)

                    def f_v3_(eng):
                        return eng.tensor_scalar_min(col[:, 7:8], col[:, 7:8], 1.0)
                    E.step("vector", f_v3_)

                    def f_gk_(eng, i=i, k_=k_, gslice=gslice):
                        return eng.tensor_mul(out=gk_store[:, i, k_:k_ + 1], in0=gslice, in1=col[:, 7:8])
                    E.step("vector", f_gk_)

                    def f_inv_(eng):
                        return eng.tensor_scalar(out=col[:, 7:8], in0=col[:, 7:8], scalar1=-1.0,
                                                 scalar2=1.0, op0=ALU.mult, op1=ALU.add)
                    E.step("vector", f_inv_)

                    def f_idx_(eng):
                        return eng.scalar_tensor_tensor(out=col[:, 6:7], in0=col[:, 7:8], scalar=BIGIDX,
                                                        in1=col[:, 6:7], op0=ALU.mult, op1=ALU.add)
                    E.step("vector", f_idx_)

                    def f_u32_(eng, i=i, k_=k_):
                        return eng.tensor_copy(out=idx_store[:, i, k_:k_ + 1], in_=col[:, 6:7])
                    E.step("vector", f_u32_, sem=dv)

                    def f_sc_(eng, i=i, k_=k_):
                        return eng.indirect_dma_start(
                            out=disp_dram[:, :],
                            out_offset=IndirectOffsetOnAxis(idx_store[:, i, k_:k_ + 1], 0),
                            in_=xa[:, :], in_offset=None,
                            bounds_check=rbound, oob_is_err=False)
                    lg.n += 16
                    E.emit("gpsimd", f_sc_, wait=[(dv, dv.n), (lg, zero_done)], inc=[(lg, 16)])
                E.step("vector", lambda eng: eng.engine_nop(), sem=dv)

        if counts_only:
            return

        for e in range(8):
            wprev = lg.n
            for c in range(8):
                E.dma("gpsimd", wA[:, c, :], w1[e, 128 * c:128 * (c + 1), :], lg,
                      wait=[(pe, pe.n), (lg, wprev)])
            for c in range(32):
                E.dma("gpsimd", wB[:, c, :], w2[e, 128 * c:128 * (c + 1), :], lg,
                      wait=[(pe, pe.n), (lg, wprev)])
            E.dma("sync", b1r[:, :], b1[e:e + 1, :], ld, wait=[(pe, pe.n)])
            c_b2 = E.dma("gpsimd", b2r[:, :], b2[e:e + 1, :], lg, wait=[(pe, pe.n), (lg, wprev)])
            ld_w = lg.n
            for c in range(32):
                def f_trb(eng, c=c):
                    return eng.transpose(out=ps_trf[0:128, 0:1], in_=b1r[:, 128 * c:128 * (c + 1)],
                                         identity=identf_s[0:1, 0:1])
                E.step("tensor", f_trb, sem=pe, wait=[(ld, ld.n), (ac, ac.n), (dv, dv.n)])

                def f_evb(eng, c=c):
                    return eng.tensor_copy(out=b1c[:, c:c + 1], in_=ps_trf[0:128, 0:1])
                E.step("vector", f_evb, sem=dv, wait=[(pe, pe.n)])
            for stp in range(CLOC // 512 if sparse else NST):
                if sparse:
                    for q in range(4):
                        r0 = e * CLOC + 512 * stp + 128 * q
                        c_d = E.dma("sync", dtile[:, :], disp_dram[r0:r0 + 128, :], ld,
                                    wait=[(pe, pe.n), (dv, dv.n), (lg, lg.n)])
                        for c in range(8):
                            def f_trd(eng, c=c):
                                return eng.transpose(out=ps_trb[:, :], in_=dtile[:, 128 * c:128 * (c + 1)],
                                                     identity=identb_s[:, :])
                            E.step("tensor", f_trd, sem=pe, wait=[(ld, c_d), (dv, dv.n)])

                            def f_evd(eng, c=c, q=q):
                                return eng.tensor_copy(out=xfTs[:, c, 128 * q:128 * (q + 1)], in_=ps_trb[:, :])
                            E.step("vector", f_evd, sem=dv, wait=[(pe, pe.n)])
                    c_xf = ld.n
                else:
                    c_xf = 0
                    for c in range(8):
                        c_xf = E.dma("sync", xfTs[:, c, 0:512], xfT_dram[c, :, 512 * stp:512 * (stp + 1)], ld,
                                     wait=[(pe, pe.n), (dv, dv.n), (lg, lg.n)])
                xfts_dv = dv.n
                relu_done = {}
                for ich in range(32):
                    pz = ps_a if ich % 2 == 0 else ps_a2

                    def f_z(eng, ich=ich, pz=pz):
                        inst = None
                        for dch in range(8):
                            inst = eng.matmul(out=pz[:, :], lhsT=wA[:, dch, 128 * ich:128 * (ich + 1)],
                                              rhs=xfTs[:, dch, :], start=(dch == 0), stop=(dch == 7))
                        return inst
                    E.step("tensor", f_z, sem=pe,
                           wait=[(lg, ld_w), (ld, c_xf),
                                 (ac, relu_done.get(ich - 2, ac.n if ich < 2 else 0)),
                                 (dv, xfts_dv if ich == 0 else 0)])

                    def f_relu(eng, ich=ich, pz=pz):
                        return eng.activation(out=hT[:, ich, :], in_=pz[:, :], func=AF.Relu,
                                              bias=b1c[:, ich:ich + 1], scale=1.0)
                    relu_done[ich] = E.step("scalar", f_relu, sem=ac,
                                            wait=[(pe, pe.n), (dv, dv.n if ich == 0 else 0)])
                evy_done = {}
                for k8, (tt, dh) in enumerate([(a, b) for a in range(4) for b in range(2)]):
                    tile = 4 * stp + tt
                    pd = ps_d if k8 % 2 == 0 else ps_d2
                    yb = yt if k8 % 2 == 0 else ytB

                    def f_dn(eng, tt=tt, dh=dh, pd=pd):
                        inst = None
                        for ich in range(32):
                            inst = eng.matmul(out=pd[:, :], lhsT=hT[:, ich, 128 * tt:128 * (tt + 1)],
                                              rhs=wB[:, ich, 512 * dh:512 * (dh + 1)],
                                              start=(ich == 0), stop=False)
                        return eng.matmul(out=pd[:, :], lhsT=ones1[:, :],
                                          rhs=b2r[:, 512 * dh:512 * (dh + 1)], start=False, stop=True)
                    E.step("tensor", f_dn, sem=pe,
                           wait=[(ac, ac.n if k8 == 0 else 0), (lg, c_b2),
                                 (dv, dv.n if k8 < 2 else evy_done.get(k8 - 2, 0))])

                    if sparse:
                        ybs = ybf if k8 % 2 == 0 else ybfB

                        def f_evy(eng, pd=pd, ybs=ybs):
                            return eng.tensor_copy(out=ybs[:, :], in_=pd[:, :])
                        evy_done[k8] = E.step("vector", f_evy, sem=dv, wait=[(pe, pe.n), (st, st.n)])
                        r0 = e * CLOC + 512 * stp + 128 * tt
                        E.dma("scalar", eout_dram[r0:r0 + 128, 512 * dh:512 * (dh + 1)], ybs[:, :],
                              st, wait=[(dv, dv.n)])
                    else:
                        def f_evy(eng, tile=tile, e=e, pd=pd, yb=yb):
                            return eng.tensor_scalar(out=yb[:, :], in0=pd[:, :],
                                                     scalar1=geff_store[:, tile, e:e + 1],
                                                     scalar2=None, op0=ALU.mult)
                        evy_done[k8] = E.step("vector", f_evy, sem=dv, wait=[(pe, pe.n), (lg, lg.n)])
                        t0 = 512 * stp + 128 * tt

                        def f_acc(eng, t0=t0, dh=dh, e=e, yb=yb):
                            return eng.dma_start(out=y_dram[t0:t0 + 128, 512 * dh:512 * (dh + 1)],
                                                 in_=yb[:, :],
                                                 accum_op=(ALU.bypass if e == 0 else ALU.add))
                        lg.n += 16
                        E.emit("gpsimd", f_acc, wait=[(dv, dv.n)], inc=[(lg, 16)])

        if sparse:
            eout_done = st.n
            for i in range(NT):
                c_x = E.dma("sync", xt[:, :], x[128 * i:128 * (i + 1), :], ld,
                            wait=[(dv, dv.n), (st, st.n)])
                E.step("vector", lambda eng: eng.memset(gbuf1[:, :], 0.0), wait=[(lg, lg.n)])
                E.step("vector", lambda eng: eng.memset(gbuf2[:, :], 0.0), sem=dv)
                for k_, gb_ in ((0, gbuf1), (1, gbuf2)):
                    def f_ga(eng, i=i, k_=k_, gb_=gb_):
                        return eng.indirect_dma_start(
                            out=gb_[:, :], out_offset=None,
                            in_=eout_dram[:, :],
                            in_offset=IndirectOffsetOnAxis(idx_store[:, i, k_:k_ + 1], 0),
                            bounds_check=rbound, oob_is_err=False)
                    lg.n += 16
                    E.emit("gpsimd", f_ga, wait=[(dv, dv.n), (st, eout_done)], inc=[(lg, 16)])
                gath = lg.n

                def f_c1(eng, i=i):
                    return eng.tensor_scalar(out=xa[:, :], in0=gbuf1[:, :],
                                             scalar1=gk_store[:, i, 0:1], scalar2=None, op0=ALU.mult)
                E.step("vector", f_c1, wait=[(lg, gath), (st, st.n)])

                def f_c2(eng, i=i):
                    return eng.scalar_tensor_tensor(out=xa[:, :], in0=gbuf2[:, :],
                                                    scalar=gk_store[:, i, 1:2], in1=xa[:, :],
                                                    op0=ALU.mult, op1=ALU.add)
                E.step("vector", f_c2)

                def f_out(eng):
                    return eng.scalar_tensor_tensor(out=xa[:, :], in0=xa[:, :], scalar=0.5, in1=xt[:, :],
                                                    op0=ALU.mult, op1=ALU.add)
                E.step("vector", f_out, sem=dv, wait=[(ld, c_x)])
                E.dma("scalar", xo[128 * i:128 * (i + 1), :], xa[:, :], st, wait=[(dv, dv.n)])
        else:
            acc_done = lg.n
            for i in range(NT):
                c_x = E.dma("sync", xt[:, :], x[128 * i:128 * (i + 1), :], ld,
                            wait=[(dv, dv.n), (lg, acc_done)])
                c_r = E.dma("sync", rt[:, :], y_dram[128 * i:128 * (i + 1), :], ld,
                            wait=[(st, st.n)])

                def f_out(eng):
                    return eng.scalar_tensor_tensor(out=xa[:, :], in0=rt[:, :], scalar=0.5, in1=xt[:, :],
                                                    op0=ALU.mult, op1=ALU.add)
                E.step("vector", f_out, sem=dv, wait=[(ld, c_r)])
                E.dma("scalar", xo[128 * i:128 * (i + 1), :], xa[:, :], st, wait=[(dv, dv.n)])





def ffn0_io(nc, io):
    io["f0_router_w"] = nc.dram_tensor("f0_router_w", [D, 8], F32, kind="ExternalInput").ap()
    io["f0_w1"] = nc.dram_tensor("f0_w1", [8, D, 4096], F32, kind="ExternalInput").ap()
    io["f0_b1"] = nc.dram_tensor("f0_b1", [8, 4096], F32, kind="ExternalInput").ap()
    io["f0_w2"] = nc.dram_tensor("f0_w2", [8, 4096, D], F32, kind="ExternalInput").ap()
    io["f0_b2"] = nc.dram_tensor("f0_b2", [8, D], F32, kind="ExternalInput").ap()


def build_ffn_l0(counts_only):
    from contextlib import ExitStack
    nc = new_program()
    io = {"x": nc.dram_tensor("x", [TOK, D], F32, kind="ExternalInput").ap(),
          "identf": nc.dram_tensor("identf", [128, 128], F32, kind="ExternalInput").ap(),
          "f0_router_w": nc.dram_tensor("f0_router_w", [D, 8], F32, kind="ExternalInput").ap(),
          "ltri": nc.dram_tensor("ltri", [128, 128], F32, kind="ExternalInput").ap()}
    if counts_only:
        io["cnt_out"] = nc.dram_tensor("cnt_out", [2, 8], F32, kind="ExternalOutput").ap()
    else:
        io["identb"] = nc.dram_tensor("identb", [128, 128], BF16, kind="ExternalInput").ap()
        for n, sh in (("f0_w1", [8, D, 4096]), ("f0_b1", [8, 4096]), ("f0_w2", [8, 4096, D]),
                      ("f0_b2", [8, D])):
            io[n] = nc.dram_tensor(n, sh, F32, kind="ExternalInput").ap()
        io["cnt_all"] = nc.dram_tensor("cnt_all", [16, 8], F32, kind="ExternalInput").ap()
        io["bmask"] = nc.dram_tensor("bmask", [16, 2], F32, kind="ExternalInput").ap()
        io["xo"] = nc.dram_tensor("xo", [TOK, D], F32, kind="ExternalOutput").ap()
        io["xfT_dram"] = nc.dram_tensor("xfT_dram", [8, 128, TOK], BF16, kind="Internal").ap()
        io["y_dram"] = nc.dram_tensor("y_dram", [TOK, D], F32, kind="Internal").ap()
        io["disp_dram"] = nc.dram_tensor("disp_dram", [8 * CLOC, D], BF16, kind="Internal").ap()
        io["eout_dram"] = nc.dram_tensor("eout_dram", [8 * CLOC, D], BF16, kind="Internal").ap()
    with ExitStack() as octx:
        sems = make_sems(nc, octx)
        ld, pe, dv, ac, go, st, lg = sems
        E = Emitter(auto={"vector": dv, "scalar": ac, "gpsimd": go})
        emit_ffn_l0(nc, E, sems, io, counts_only)
        finish_program(nc, octx, E, sems)
    return nc

def make_l0_consts():
    identf = np.eye(128, dtype=np.float32)
    ltri = np.tril(np.ones((128, 128), np.float32), -1).T
    return identf, ltri


def make_bmask(core):
    bm = np.zeros((16, 2), np.float32)
    for b_loc, gb_ in enumerate(CORE_BATCHES[core]):
        bm[:gb_, b_loc] = 1.0
    return bm


_NC_CACHE = {}

def _get_nc(key, builder, *args):
    if key not in _NC_CACHE:
        _NC_CACHE[key] = builder(*args)
    return _NC_CACHE[key]


def _run_l0_ffn(xs, l0w, identf, ltri, identb):
    nc = _get_nc("l0cnt", build_ffn_l0, True)
    in_maps = [{"x": xs[i], "identf": identf, "ltri": ltri, "f0_router_w": l0w["router_w"]}
               for i in range(NCORES)]
    res = run_bass_kernel_spmd(nc, in_maps, core_ids=list(range(NCORES)))
    cnt_all = np.zeros((16, 8), np.float32)
    for i in range(NCORES):
        c = res.results[i]["cnt_out"]
        cnt_all[CORE_BATCHES[i][0]] = c[0]
        cnt_all[CORE_BATCHES[i][1]] = c[1]
    nc = _get_nc("l0ffn", build_ffn_l0, False)
    in_maps = []
    for i in range(NCORES):
        m = {"x": xs[i], "identf": identf, "identb": identb, "ltri": ltri,
             "cnt_all": cnt_all, "bmask": make_bmask(i)}
        m.update({"f0_" + k: v for k, v in l0w.items()})
        in_maps.append(m)
    res = run_bass_kernel_spmd(nc, in_maps, core_ids=list(range(NCORES)))
    return [r["xo"] for r in res.results]





def attn_io(nc, io, layer):
    if layer == 1:
        for n, sh in (("wq", [D, D]), ("wk", [D, D]), ("wv", [D, D]), ("wo", [D, D]),
                      ("gamma", [1, D]), ("beta", [1, D])):
            io[n] = nc.dram_tensor(n, sh, F32, kind="ExternalInput").ap()
    else:
        io["wqkvT"] = nc.dram_tensor("wqkvT", [D, 3 * D], F32, kind="ExternalInput").ap()
        io["woT"] = nc.dram_tensor("woT", [D, D], F32, kind="ExternalInput").ap()
        io["freqs"] = nc.dram_tensor("freqs", [1, 32], F32, kind="ExternalInput").ap()
        io["repmat"] = nc.dram_tensor("repmat", [32, 128], BF16, kind="ExternalInput").ap()
        io["repsign"] = nc.dram_tensor("repsign", [32, 128], BF16, kind="ExternalInput").ap()
        io["swapm"] = nc.dram_tensor("swapm", [128, 128], BF16, kind="ExternalInput").ap()
        io["pa_rows"] = nc.dram_tensor("pa_rows", [2, 16], F32, kind="ExternalInput").ap()


def build_part1():
    from contextlib import ExitStack
    nc = new_program()
    io = {"x": nc.dram_tensor("x", [TOK, D], F32, kind="ExternalInput").ap(),
          "identb": nc.dram_tensor("identb", [128, 128], BF16, kind="ExternalInput").ap(),
          "identf": nc.dram_tensor("identf", [128, 128], F32, kind="ExternalInput").ap(),
          "ltri": nc.dram_tensor("ltri", [128, 128], F32, kind="ExternalInput").ap(),
          "cnt_all": nc.dram_tensor("cnt_all", [16, 8], F32, kind="ExternalInput").ap(),
          "bmask": nc.dram_tensor("bmask", [16, 2], F32, kind="ExternalInput").ap(),
          "xo": nc.dram_tensor("xo", [TOK, D], F32, kind="ExternalOutput").ap(),
          "cnt_out": nc.dram_tensor("cnt_out", [2, 8], F32, kind="ExternalOutput").ap(),
          "xfT_dram": nc.dram_tensor("xfT_dram", [8, 128, TOK], BF16, kind="Internal").ap(),
          "y_dram": nc.dram_tensor("y_dram", [TOK, D], F32, kind="Internal").ap(),
          "disp_dram": nc.dram_tensor("disp_dram", [8 * CLOC, D], BF16, kind="Internal").ap(),
          "eout_dram": nc.dram_tensor("eout_dram", [8 * CLOC, D], BF16, kind="Internal").ap(),
          "xmid": nc.dram_tensor("xmid", [TOK, D], F32, kind="Internal").ap()}
    ffn0_io(nc, io)
    attn_io(nc, io, 0)
    with ExitStack() as octx:
        sems = make_sems(nc, octx)
        ld, pe, dv, ac, go, st, lg = sems
        E = Emitter(auto={"vector": dv, "scalar": ac, "gpsimd": go})
        io1 = dict(io); io1["xo"] = io["xmid"]
        emit_ffn_l0(NcPrefix(nc, "s1_"), E, sems, io1, False)
        full_barrier(E, sems)
        io2 = dict(io); io2["x"] = io["xmid"]
        emit_attn(NcPrefix(nc, "s2_"), E, sems, io2, 0)
        full_barrier(E, sems)
        io3 = dict(io); io3["x"] = io["xo"]
        emit_ffn_l0(NcPrefix(nc, "s3_"), E, sems, io3, True)
        finish_program(nc, octx, E, sems)
    return nc


def build_part2():
    from contextlib import ExitStack
    nc = new_program()
    io = {"x": nc.dram_tensor("x", [TOK, D], F32, kind="ExternalInput").ap(),
          "identb": nc.dram_tensor("identb", [128, 128], BF16, kind="ExternalInput").ap(),
          "identf": nc.dram_tensor("identf", [128, 128], F32, kind="ExternalInput").ap(),
          "ltri": nc.dram_tensor("ltri", [128, 128], F32, kind="ExternalInput").ap(),
          "cnt_all": nc.dram_tensor("cnt_all", [16, 8], F32, kind="ExternalInput").ap(),
          "bmask": nc.dram_tensor("bmask", [16, 2], F32, kind="ExternalInput").ap(),
          "xo": nc.dram_tensor("xo", [TOK, D], F32, kind="ExternalOutput").ap(),
          "xfT_dram": nc.dram_tensor("xfT_dram", [8, 128, TOK], BF16, kind="Internal").ap(),
          "y_dram": nc.dram_tensor("y_dram", [TOK, D], F32, kind="Internal").ap(),
          "disp_dram": nc.dram_tensor("disp_dram", [8 * CLOC, D], BF16, kind="Internal").ap(),
          "eout_dram": nc.dram_tensor("eout_dram", [8 * CLOC, D], BF16, kind="Internal").ap(),
          "xmid": nc.dram_tensor("xmid", [TOK, D], F32, kind="Internal").ap(),
          "xmid2": nc.dram_tensor("xmid2", [TOK, D], F32, kind="Internal").ap()}
    ffn0_io(nc, io)
    ffn1_io(nc, io)
    attn_io(nc, io, 1)
    with ExitStack() as octx:
        sems = make_sems(nc, octx)
        ld, pe, dv, ac, go, st, lg = sems
        E = Emitter(auto={"vector": dv, "scalar": ac, "gpsimd": go})
        io1 = dict(io); io1["xo"] = io["xmid"]
        emit_ffn_l0(NcPrefix(nc, "s1_"), E, sems, io1, False)
        full_barrier(E, sems)
        io2 = dict(io); io2["x"] = io["xmid"]; io2["xo"] = io["xmid2"]
        emit_ffn_l1(NcPrefix(nc, "s2_"), E, sems, io2)
        full_barrier(E, sems)
        io3 = dict(io); io3["x"] = io["xmid2"]; io3["xo"] = io["xmid"]
        emit_attn(NcPrefix(nc, "s3_"), E, sems, io3, 1)
        full_barrier(E, sems)
        io4 = dict(io); io4["x"] = io["xmid"]
        emit_ffn_l1(NcPrefix(nc, "s4_"), E, sems, io4)
        finish_program(nc, octx, E, sems)
    return nc


def kernel(**inputs):
    f32 = lambda a: np.ascontiguousarray(np.asarray(a), dtype=np.float32)
    x = f32(inputs["L0_attn_x"])
    identb, rep, repsign, swap, pos = make_consts()
    identf, ltri = make_l0_consts()
    common = {"identb": identb, "identf": identf, "ltri": ltri}
    f0 = dict(f0_router_w=f32(inputs["L0_ffn_w_router"]), f0_w1=f32(inputs["L0_ffn_w1"]),
              f0_b1=f32(inputs["L0_ffn_b1"]), f0_w2=f32(inputs["L0_ffn_w2"]),
              f0_b2=f32(inputs["L0_ffn_b2"]))
    a0 = dict(wqkvT=f32(inputs["L0_attn_qkv_w"]).T.copy(),
              woT=f32(inputs["L0_attn_out_w"]).T.copy(),
              freqs=f32(inputs["L0_attn_rope_freqs"]).reshape(1, 32),
              repmat=rep, repsign=repsign, swapm=swap,
              pa_rows=np.stack([f32(inputs["L0_attn_bias_p"]).reshape(16),
                                f32(inputs["L0_attn_bias_a"]).reshape(16)]).astype(np.float32))
    a1 = dict(wq=f32(inputs["L1_attn_Wq"]), wk=f32(inputs["L1_attn_Wk"]),
              wv=f32(inputs["L1_attn_Wv"]), wo=f32(inputs["L1_attn_Wo"]),
              gamma=f32(inputs["L1_attn_ln_gamma"]).reshape(1, D),
              beta=f32(inputs["L1_attn_ln_beta"]).reshape(1, D))
    f1 = dict(f1_ln_w=f32(inputs["L1_ffn_ln_w"]).reshape(1, D),
              f1_ln_b=f32(inputs["L1_ffn_ln_b"]).reshape(1, D),
              f1_router_w=f32(inputs["L1_ffn_router_w"]), f1_gate_w=f32(inputs["L1_ffn_gate_w"]),
              f1_up_w=f32(inputs["L1_ffn_up_w"]), f1_down_w=f32(inputs["L1_ffn_down_w"]),
              f1_sh_gate=f32(inputs["L1_ffn_sh_gate"]), f1_sh_up=f32(inputs["L1_ffn_sh_up"]),
              f1_sh_down=f32(inputs["L1_ffn_sh_down"]))
    xs = shard_x(x)

    nc = _get_nc("l0cnt", build_ffn_l0, True)
    in_maps = [{"x": xs[i], "identf": identf, "ltri": ltri, "f0_router_w": f0["f0_router_w"]}
               for i in range(NCORES)]
    res = run_bass_kernel_spmd(nc, in_maps, core_ids=list(range(NCORES)))
    cnt1 = np.zeros((16, 8), np.float32)
    for i in range(NCORES):
        c = res.results[i]["cnt_out"]
        cnt1[CORE_BATCHES[i][0]] = c[0]
        cnt1[CORE_BATCHES[i][1]] = c[1]

    nc = _get_nc("part1", build_part1)
    in_maps = []
    for i in range(NCORES):
        m = {"x": xs[i], "cnt_all": cnt1, "bmask": make_bmask(i)}
        m.update(common); m.update(f0); m.update(a0)
        in_maps.append(m)
    res = run_bass_kernel_spmd(nc, in_maps, core_ids=list(range(NCORES)))
    xs = [r["xo"] for r in res.results]
    cnt2 = np.zeros((16, 8), np.float32)
    for i in range(NCORES):
        c = res.results[i]["cnt_out"]
        cnt2[CORE_BATCHES[i][0]] = c[0]
        cnt2[CORE_BATCHES[i][1]] = c[1]

    nc = _get_nc("part2", build_part2)
    in_maps = []
    for i in range(NCORES):
        m = {"x": xs[i], "cnt_all": cnt2, "bmask": make_bmask(i)}
        m.update(common); m.update(f0); m.update(f1); m.update(a1)
        in_maps.append(m)
    res = run_bass_kernel_spmd(nc, in_maps, core_ids=list(range(NCORES)))
    return unshard_x([r["xo"] for r in res.results])


def kernel_8launch(**inputs):
    f32 = lambda a: np.ascontiguousarray(np.asarray(a), dtype=np.float32)
    x = f32(inputs["L0_attn_x"])
    identb, rep, repsign, swap, pos = make_consts()
    identf, ltri = make_l0_consts()

    l0w = dict(router_w=f32(inputs["L0_ffn_w_router"]), w1=f32(inputs["L0_ffn_w1"]),
               b1=f32(inputs["L0_ffn_b1"]), w2=f32(inputs["L0_ffn_w2"]), b2=f32(inputs["L0_ffn_b2"]))
    a0w = dict(wqkvT=f32(inputs["L0_attn_qkv_w"]).T.copy(),
               woT=f32(inputs["L0_attn_out_w"]).T.copy(),
               freqs=f32(inputs["L0_attn_rope_freqs"]).reshape(1, 32),
               pa_rows=np.stack([f32(inputs["L0_attn_bias_p"]).reshape(16),
                                 f32(inputs["L0_attn_bias_a"]).reshape(16)]).astype(np.float32))
    a1w = dict(wq=f32(inputs["L1_attn_Wq"]), wk=f32(inputs["L1_attn_Wk"]),
               wv=f32(inputs["L1_attn_Wv"]), wo=f32(inputs["L1_attn_Wo"]),
               gamma=f32(inputs["L1_attn_ln_gamma"]).reshape(1, D),
               beta=f32(inputs["L1_attn_ln_beta"]).reshape(1, D))
    f1w = dict(ln_w=f32(inputs["L1_ffn_ln_w"]).reshape(1, D),
               ln_b=f32(inputs["L1_ffn_ln_b"]).reshape(1, D),
               router_w=f32(inputs["L1_ffn_router_w"]), gate_w=f32(inputs["L1_ffn_gate_w"]),
               up_w=f32(inputs["L1_ffn_up_w"]), down_w=f32(inputs["L1_ffn_down_w"]),
               sh_gate=f32(inputs["L1_ffn_sh_gate"]), sh_up=f32(inputs["L1_ffn_sh_up"]),
               sh_down=f32(inputs["L1_ffn_sh_down"]))

    xs = shard_x(x)
    xs = _run_l0_ffn(xs, l0w, identf, ltri, identb)
    xs = run_attn(0, xs, a0w)
    xs = _run_l0_ffn(xs, l0w, identf, ltri, identb)
    xs = run_ffn_l1_cached(xs, f1w)
    xs = run_attn_cached(1, xs, a1w)
    xs = run_ffn_l1_cached(xs, f1w)
    return unshard_x(xs)


def run_ffn_l1_cached(xs, weights):
    nc = _get_nc("l1ffn", build_ffn_l1)
    identb, rep, repsign, swap, pos = make_consts()
    identf = np.eye(128, dtype=np.float32)
    in_maps = []
    for i in range(NCORES):
        m = {"x": xs[i], "identb": identb, "identf": identf}
        m.update({"f1_" + k: v for k, v in weights.items()})
        in_maps.append(m)
    res = run_bass_kernel_spmd(nc, in_maps, core_ids=list(range(NCORES)))
    return [r["xo"] for r in res.results]


def run_attn_cached(layer, xs, weights):
    nc = _get_nc(("attn", layer), build_attn, layer)
    identb, rep, repsign, swap, pos = make_consts()
    in_maps = []
    for i in range(NCORES):
        m = {"x": xs[i], "identb": identb}
        if layer == 1:
            m.update(weights)
        else:
            m.update(weights, repmat=rep, repsign=repsign, swapm=swap)
        in_maps.append(m)
    res = run_bass_kernel_spmd(nc, in_maps, core_ids=list(range(NCORES)))
    return [r["xo"] for r in res.results]



# version 08
# speedup vs baseline: 1.6288x; 1.1399x over previous
import numpy as np
import ml_dtypes
import concourse.bass as bass
import concourse.mybir as mybir
from concourse.bass_core_rust import IndirectOffsetOnAxis
from concourse.bass_utils import run_bass_kernel_spmd

F32 = mybir.dt.float32
BF16 = mybir.dt.bfloat16
U32 = mybir.dt.uint32
F16 = mybir.dt.float16
AF = mybir.ActivationFunctionType
ALU = mybir.AluOpType
AX = mybir.AxisListType.X

NCORES = 8
B, S, D = 16, 2048, 1024
SEQ_PER_CORE = 2
TOK = SEQ_PER_CORE * S
CORE_BATCHES = [[i, 15 - i] for i in range(NCORES)]
CAP0 = 3072
CLOC = 1024
C1LOC = 2048
BIGIDX = 4000000.0
EPS = 1e-5
TWO_PI = 6.283185307179586
PI = 3.141592653589793
HALF_PI = 1.5707963267948966


class Sem:
    def __init__(self, handle):
        self.h = handle
        self.n = 0

class Emitter:
    def __init__(self, auto=None):
        self.ops = []
        self.auto = auto or {}

    def emit(self, engine, fn=None, wait=(), inc=()):
        self.ops.append((engine, fn, list(wait), list(inc)))

    def dma(self, engine, out, in_, sem, wait=()):
        def f(eng, out=out, in_=in_):
            return eng.dma_start(out=out, in_=in_)
        sem.n += 16
        self.emit(engine, f, wait=wait, inc=[(sem, 16)])
        return sem.n

    def step(self, engine, fn, sem=None, wait=()):
        if engine in self.auto:
            a = self.auto[engine]
            assert sem is None or sem is a
            w = [(a, a.n)] + [x for x in wait if not (x[0] is a)]
            a.n += 1
            self.emit(engine, fn, wait=w, inc=[(a, 1)])
            return a.n
        inc = []
        if sem is not None:
            sem.n += 1
            inc = [(sem, 1)]
        self.emit(engine, fn, wait=wait, inc=inc)
        return sem.n if sem is not None else None

    def replay(self, block):
        def section(eng_name):
            def body(eng):
                for engine, fn, waits, incs in self.ops:
                    if engine != eng_name:
                        continue
                    for (s, v) in waits:
                        eng.wait_ge(s.h, v)
                    if fn is not None:
                        inst = fn(eng)
                        for (s, a) in incs:
                            assert inst is not None, "inc attached to op that returned None"
                            inst = inst.then_inc(s.h, a)
            return body
        block.sync(section("sync"))
        block.scalar(section("scalar"))
        block.vector(section("vector"))
        block.tensor(section("tensor"))
        block.gpsimd(section("gpsimd"))


def emit_ln_stats(E, bn6, bnag, xt, width, mean, rstd, dv, ac, eps_col, wait=()):
    n_sub = width // 512

    def f_bn(eng, n_sub=n_sub, xt=xt):
        i = None
        for s0 in range(n_sub):
            i = eng.bn_stats(out=bn6[:, 6 * s0:6 * s0 + 6], in_=xt[:, 512 * s0:512 * (s0 + 1)])
        return i
    E.step("vector", f_bn, wait=list(wait) + [(ac, ac.n)])

    def f_ag(eng, n_sub=n_sub):
        return eng.bn_aggr(out=bnag[:, :], in_=bn6[:, 0:6 * n_sub])
    E.step("vector", f_ag)

    def f_mean(eng, mean=mean):
        return eng.tensor_copy(out=mean[:, :], in_=bnag[:, 0:1])
    E.step("vector", f_mean, sem=dv)

    def f_ln(eng, rstd=rstd):
        return eng.activation(out=rstd[:, :], in_=bnag[:, 1:2], func=AF.Ln, bias=eps_col, scale=1.0)
    E.step("scalar", f_ln, wait=[(dv, dv.n)])

    def f_exp(eng, rstd=rstd):
        return eng.activation(out=rstd[:, :], in_=rstd[:, :], func=AF.Exp, bias=0.0, scale=-0.5)
    E.step("scalar", f_exp, sem=ac)


def emit_transpose_tile(E, src, dst_fn, ps_tr, ident, pe, dv, wait=(), n_chunks=8):
    for c in range(n_chunks):
        def f_tr(eng, c=c):
            return eng.transpose(out=ps_tr[:, :], in_=src[:, 128 * c:128 * (c + 1)], identity=ident[:, :])
        E.step("tensor", f_tr, sem=pe, wait=list(wait) + [(dv, dv.n)])

        def f_ev(eng, c=c):
            return eng.tensor_copy(out=dst_fn(c), in_=ps_tr[:, :])
        E.step("vector", f_ev, sem=dv, wait=[(pe, pe.n)])



def emit_attn(nc, E, sems, io, layer):
    x = io["x"]
    identb = io["identb"]
    if layer == 1:
        wq, wk, wv, wo = io["wq"], io["wk"], io["wv"], io["wo"]
        gamma, beta = io["gamma"], io["beta"]
    else:
        wqkvT, woT = io["wqkvT"], io["woT"]
        freqs = io["freqs"]
        repmat, repsign, swapm = io["repmat"], io["repsign"], io["swapm"]
        pa_rows = io["pa_rows"]
    xo = io["xo"]

    NHEAD = 16
    NSTRIPE = S // 512
    NKC = S // 128
    SCALE = 0.125

    from contextlib import ExitStack
    with ExitStack() as ctx:
        xaT = ctx.enter_context(nc.sbuf_tensor("xaT", [128, 8, S], BF16))
        kT = ctx.enter_context(nc.sbuf_tensor("kT", [128, 8, S], BF16))
        vtm = ctx.enter_context(nc.sbuf_tensor("vtm", [128, NKC, D], BF16))
        qT = ctx.enter_context(nc.sbuf_tensor("qT", [128, 8, 512], BF16))
        expT = ctx.enter_context(nc.sbuf_tensor("expT", [128, NKC, 512], BF16))
        aoT = ctx.enter_context(nc.sbuf_tensor("aoT", [128, 8, 512], BF16))
        wbuf = ctx.enter_context(nc.sbuf_tensor("wbuf", [128, 8, 2 * D], BF16))
        xt = ctx.enter_context(nc.sbuf_tensor("xt", [128, D], F32))
        xab = ctx.enter_context(nc.sbuf_tensor("xab", [128, D], BF16))
        bn6 = ctx.enter_context(nc.sbuf_tensor("bn6", [128, 12], F32))
        bnag = ctx.enter_context(nc.sbuf_tensor("bnag", [128, 2], F32))
        stats = ctx.enter_context(nc.sbuf_tensor("stats", [128, 4], F32))
        cc = ctx.enter_context(nc.sbuf_tensor("cc", [128, 2], F32))
        if layer == 1:
            gb = ctx.enter_context(nc.sbuf_tensor("gb", [128, 2 * D], F32))
        onesb = ctx.enter_context(nc.sbuf_tensor("onesb", [128, 1], BF16))
        onesf = ctx.enter_context(nc.sbuf_tensor("onesf", [128, 1], F32))
        ones1 = ctx.enter_context(nc.sbuf_tensor("ones1", [1, 128], BF16))
        ones1f = ctx.enter_context(nc.sbuf_tensor("ones1f", [1, 128], F32))
        scr = ctx.enter_context(nc.sbuf_tensor("scr", [128, 512], F32))
        scr2 = ctx.enter_context(nc.sbuf_tensor("scr2", [128, 512], F32))
        tmp1 = ctx.enter_context(nc.sbuf_tensor("tmp1", [1, 4], F32))
        bsq = ctx.enter_context(nc.sbuf_tensor("bsq", [1, 16], F32))
        crow = ctx.enter_context(nc.sbuf_tensor("crow", [1, 512], F32))
        crow_bf = ctx.enter_context(nc.sbuf_tensor("crow_bf", [1, 512], BF16))
        srow = ctx.enter_context(nc.sbuf_tensor("srow", [1, 512], F32))
        rrow = ctx.enter_context(nc.sbuf_tensor("rrow", [1, 512], F32))
        rrow_bf = ctx.enter_context(nc.sbuf_tensor("rrow_bf", [1, 512], BF16))
        rb = ctx.enter_context(nc.sbuf_tensor("rb", [64, 512], F32))
        xres = ctx.enter_context(nc.sbuf_tensor("xres", [128, D], F32))
        xout = ctx.enter_context(nc.sbuf_tensor("xout", [128, D], F32))
        if layer == 0:
            ctabq = ctx.enter_context(nc.sbuf_tensor("ctabq", [128, S], BF16))
        if layer == 0:
            stabq = ctx.enter_context(nc.sbuf_tensor("stabq", [128, S], BF16))
        if layer == 0:
            freq_s = ctx.enter_context(nc.sbuf_tensor("freq_s", [1, 32], F32))
        if layer == 0:
            rep_s = ctx.enter_context(nc.sbuf_tensor("rep_s", [32, 128], BF16))
        if layer == 0:
            repsg_s = ctx.enter_context(nc.sbuf_tensor("repsg_s", [32, 128], BF16))
        if layer == 0:
            swap_s = ctx.enter_context(nc.sbuf_tensor("swap_s", [128, 128], BF16))
        if layer == 0:
            pa_s = ctx.enter_context(nc.sbuf_tensor("pa_s", [1, 32], F32))
        if layer == 0:
            pacol = ctx.enter_context(nc.sbuf_tensor("pacol", [128, 32], F32))
        if layer == 0:
            dist = ctx.enter_context(nc.sbuf_tensor("dist", [128, 2, 512], F16))
        if layer == 0:
            lnt = ctx.enter_context(nc.sbuf_tensor("lnt", [128, 2, 512], F16))
            scrP = ctx.enter_context(nc.sbuf_tensor("scrP", [128, 2, 512], F32))
            tb_bf = ctx.enter_context(nc.sbuf_tensor("tb_bf", [32, 512], BF16))
        identb_s = ctx.enter_context(nc.sbuf_tensor("identb_s", [128, 128], BF16))
        ps_a = ctx.enter_context(nc.psum_tensor("ps_a", [128, 512], F32))
        ps_a2 = ctx.enter_context(nc.psum_tensor("ps_a2", [128, 512], F32))
        ps_b = ctx.enter_context(nc.psum_tensor("ps_b", [128, 512], F32))
        ps_tr = ctx.enter_context(nc.psum_tensor("ps_tr", [128, 128], BF16))
        ps_pv = ctx.enter_context(nc.psum_tensor("ps_pv", [64, 512], F32))
        ps_s = ctx.enter_context(nc.psum_tensor("ps_s", [1, 512], F32))
        ps_r = ctx.enter_context(nc.psum_tensor("ps_r", [64, 512], F32))
        ps_c = ctx.enter_context(nc.psum_tensor("ps_c", [128, 512], F32))
        ld, pe, dv, ac, go, st, lg = sems

        E.step("vector", lambda eng: eng.memset(onesb[:, :], 1.0))
        E.step("vector", lambda eng: eng.memset(onesf[:, :], 1.0))
        E.step("vector", lambda eng: eng.memset(ones1[:, :], 1.0))
        E.step("vector", lambda eng: eng.memset(ones1f[:, :], 1.0))
        E.step("vector", lambda eng: eng.memset(cc[:, 0:1], EPS))
        E.step("vector", lambda eng: eng.memset(cc[:, 1:2], 1.0), sem=dv)
        E.dma("sync", identb_s[:, :], identb[:, :], ld)

        if layer == 1:
            E.dma("sync", gb[0:1, 0:D], gamma[:, :], ld)
            c_b = E.dma("sync", gb[0:1, D:2 * D], beta[:, :], ld)
            for half in range(4):
                def f_mm(eng, half=half):
                    return eng.matmul(out=ps_c[:, :], lhsT=ones1f[:, :],
                                      rhs=gb[0:1, 512 * half:512 * (half + 1)], start=True, stop=True)
                E.step("tensor", f_mm, sem=pe, wait=[(ld, c_b), (dv, dv.n)])

                def f_ev(eng, half=half):
                    return eng.tensor_copy(out=gb[:, 512 * half:512 * (half + 1)], in_=ps_c[:, :])
                E.step("vector", f_ev, sem=dv, wait=[(pe, pe.n)])
        else:
            E.dma("sync", freq_s[:, :], freqs[:, :], ld)
            E.dma("sync", rep_s[:, :], repmat[:, :], ld)
            E.dma("sync", repsg_s[:, :], repsign[:, :], ld)
            E.dma("sync", swap_s[:, :], swapm[:, :], ld)
            E.dma("sync", pa_s[0:1, 0:16], pa_rows[0:1, :], ld)
            c_pa = E.dma("sync", pa_s[0:1, 16:32], pa_rows[1:2, :], ld)
            E.step("vector", lambda eng: eng.tensor_scalar_max(pa_s[:, :], pa_s[:, :], 0.01),
                   sem=dv, wait=[(ld, c_pa)])

            def f_acol(eng):
                return eng.matmul(out=ps_c[:, 0:16], lhsT=ones1f[:, :], rhs=pa_s[0:1, 16:32],
                                  start=True, stop=True)
            E.step("tensor", f_acol, wait=[(dv, dv.n)])

            def f_pcol(eng):
                return eng.matmul(out=ps_c[:, 16:32], lhsT=ones1f[:, :], rhs=pa_s[0:1, 0:16],
                                  start=True, stop=True)
            E.step("tensor", f_pcol, sem=pe)

            def f_evpa(eng):
                return eng.tensor_copy(out=pacol[:, 0:16], in_=ps_c[:, 0:16])
            E.step("vector", f_evpa, wait=[(pe, pe.n)])

            def f_evpa2(eng):
                return eng.tensor_scalar_mul(pacol[:, 16:32], ps_c[:, 16:32], -1.0)
            E.step("vector", f_evpa2, sem=dv)

            for ch in range(S // 512):
                csl = slice(512 * ch, 512 * (ch + 1))

                def f_pos(eng, ch=ch):
                    return eng.iota(out=srow[:, :], pattern=[[1, 512]], base=512 * ch,
                                    channel_multiplier=0, allow_small_or_imprecise_dtypes=True)
                E.step("gpsimd", f_pos, sem=go, wait=[(pe, pe.n)])

                def f_ang(eng, csl=csl):
                    return eng.matmul(out=ps_a[0:32, 0:512], lhsT=freq_s[:, :],
                                      rhs=srow[:, :], start=True, stop=True)
                E.step("tensor", f_ang, sem=pe, wait=[(ld, ld.n), (dv, dv.n), (ac, ac.n), (go, go.n)])

                def f_evang(eng):
                    return eng.tensor_copy(out=scr[0:32, :], in_=ps_a[0:32, 0:512])
                E.step("vector", f_evang, sem=dv, wait=[(pe, pe.n)])
                MAGIC = 12582912.0
                INV2PI = 0.15915494309189535
                for is_cos in (True, False):
                    if is_cos:
                        E.step("vector", lambda eng: eng.tensor_scalar_add(scr2[0:32, :], scr[0:32, :],
                               HALF_PI), wait=[(ac, ac.n)])
                    else:
                        E.step("vector", lambda eng: eng.tensor_copy(out=scr2[0:32, :], in_=scr[0:32, :]),
                               wait=[(ac, ac.n)])
                    E.step("vector", lambda eng: eng.tensor_scalar(out=xt[0:32, 0:512], in0=scr2[0:32, :],
                           scalar1=INV2PI, scalar2=MAGIC, op0=ALU.mult, op1=ALU.add))
                    E.step("vector", lambda eng: eng.tensor_scalar_sub(xt[0:32, 0:512], xt[0:32, 0:512], MAGIC))
                    E.step("vector", lambda eng: eng.scalar_tensor_tensor(out=scr2[0:32, :], in0=xt[0:32, 0:512],
                           scalar=-TWO_PI, in1=scr2[0:32, :], op0=ALU.mult, op1=ALU.add), sem=dv)
                    E.step("scalar", lambda eng: eng.activation(out=scr2[0:32, :], in_=scr2[0:32, :], func=AF.Sin),
                           sem=ac, wait=[(dv, dv.n)])
                    E.step("vector", lambda eng: eng.tensor_copy(out=tb_bf[:, :], in_=scr2[0:32, :]),
                           sem=dv, wait=[(ac, ac.n)])
                    dstq, lhs = (ctabq, rep_s) if is_cos else (stabq, repsg_s)

                    def f_rep(eng, lhs=lhs):
                        return eng.matmul(out=ps_b[:, 0:512], lhsT=lhs[:, :], rhs=tb_bf[:, :], start=True, stop=True)
                    E.step("tensor", f_rep, sem=pe, wait=[(dv, dv.n)])

                    def f_evq(eng, dstq=dstq, csl=csl):
                        return eng.activation(out=dstq[:, csl], in_=ps_b[:, 0:512], func=AF.Copy, scale=SCALE)
                    E.step("scalar", f_evq, sem=ac, wait=[(pe, pe.n)])

        for sq in range(SEQ_PER_CORE):
            xseq = x[sq * S:(sq + 1) * S, :]
            xoseq = xo[sq * S:(sq + 1) * S, :]

            wprev = lg.n
            if layer == 1:
                for c in range(8):
                    E.dma("gpsimd", wbuf[:, c, 0:D], wk[128 * c:128 * (c + 1), :], lg, wait=[(dv, dv.n), (lg, wprev)])
                    E.dma("gpsimd", wbuf[:, c, D:2 * D], wv[128 * c:128 * (c + 1), :], lg, wait=[(dv, dv.n), (lg, wprev)])
            else:
                for c in range(8):
                    E.dma("gpsimd", wbuf[:, c, 0:D], wqkvT[128 * c:128 * (c + 1), D:2 * D], lg, wait=[(dv, dv.n), (lg, wprev)])
                    E.dma("gpsimd", wbuf[:, c, D:2 * D], wqkvT[128 * c:128 * (c + 1), 2 * D:3 * D], lg, wait=[(dv, dv.n), (lg, wprev)])
            ld_wkv = lg.n

            for i in range(S // 128):
                c_x = E.dma("sync", xt[:, :], xseq[128 * i:128 * (i + 1), :], ld, wait=[(dv, dv.n)])
                emit_ln_stats(E, bn6, bnag, xt, D, stats[:, 0:1], stats[:, 1:2], dv, ac, cc[:, 0:1],
                              wait=[(ld, c_x)])

                def f_norm(eng):
                    return eng.tensor_scalar(out=xt[:, :], in0=xt[:, :], scalar1=stats[:, 0:1],
                                             scalar2=stats[:, 1:2], op0=ALU.subtract, op1=ALU.mult)
                E.step("vector", f_norm, wait=[(ac, ac.n)])
                if layer == 1:
                    emit_ln_stats(E, bn6, bnag, xt, D, stats[:, 2:3], stats[:, 3:4], dv, ac, cc[:, 0:1])

                    def f_norm2(eng):
                        return eng.tensor_scalar(out=xt[:, :], in0=xt[:, :], scalar1=stats[:, 2:3],
                                                 scalar2=stats[:, 3:4], op0=ALU.subtract, op1=ALU.mult)
                    E.step("vector", f_norm2, wait=[(ac, ac.n)])

                    def f_gam(eng):
                        return eng.tensor_mul(out=xt[:, :], in0=xt[:, :], in1=gb[:, 0:D])
                    E.step("vector", f_gam)

                    def f_bet(eng):
                        return eng.tensor_add(out=xt[:, :], in0=xt[:, :], in1=gb[:, D:2 * D])
                    E.step("vector", f_bet)

                def f_cast(eng):
                    return eng.tensor_copy(out=xab[:, :], in_=xt[:, :])
                E.step("vector", f_cast, sem=dv)
                emit_transpose_tile(E, xab, lambda c, i=i: xaT[:, c, 128 * i:128 * (i + 1)],
                                    ps_tr, identb_s, pe, dv)

            for fch in range(8):
                for stp in range(NSTRIPE):
                    def f_mm(eng, fch=fch, stp=stp):
                        inst = None
                        for dch in range(8):
                            inst = eng.matmul(out=ps_a[:, :], lhsT=wbuf[:, dch, 128 * fch:128 * (fch + 1)],
                                              rhs=xaT[:, dch, 512 * stp:512 * (stp + 1)],
                                              start=(dch == 0), stop=(dch == 7))
                        return inst
                    E.step("tensor", f_mm, sem=pe, wait=[(lg, ld_wkv), (dv, dv.n)])

                    def f_ev(eng, fch=fch, stp=stp):
                        return eng.tensor_copy(out=kT[:, fch, 512 * stp:512 * (stp + 1)], in_=ps_a[:, :])
                    E.step("vector", f_ev, sem=dv, wait=[(pe, pe.n)])
            if layer == 0:
                for fch in range(8):
                    for stp in range(NSTRIPE):
                        sl = slice(512 * stp, 512 * (stp + 1))

                        def f_swapmm(eng, fch=fch, sl=sl):
                            return eng.matmul(out=ps_b[:, :], lhsT=swap_s[:, :], rhs=kT[:, fch, sl],
                                              start=True, stop=True)
                        E.step("tensor", f_swapmm, sem=pe, wait=[(dv, dv.n)])

                        def f_t1(eng, fch=fch, sl=sl):
                            return eng.scalar_tensor_tensor(out=scr[:, :], in0=kT[:, fch, sl], scalar=8.0,
                                                            in1=ctabq[:, sl], op0=ALU.mult, op1=ALU.mult)
                        E.step("vector", f_t1, wait=[(pe, pe.n)])

                        def f_t2(eng, sl=sl):
                            return eng.scalar_tensor_tensor(out=scr2[:, :], in0=ps_b[:, :], scalar=8.0,
                                                            in1=stabq[:, sl], op0=ALU.mult, op1=ALU.mult)
                        E.step("vector", f_t2)

                        def f_t3(eng, fch=fch, sl=sl):
                            return eng.tensor_add(out=kT[:, fch, sl], in0=scr[:, :], in1=scr2[:, :])
                        E.step("vector", f_t3, sem=dv)
            for i in range(NKC):
                for vh in range(2):
                    def f_mm(eng, i=i, vh=vh):
                        inst = None
                        for dch in range(8):
                            inst = eng.matmul(out=ps_a[:, :], lhsT=xaT[:, dch, 128 * i:128 * (i + 1)],
                                              rhs=wbuf[:, dch, D + 512 * vh:D + 512 * (vh + 1)],
                                              start=(dch == 0), stop=(dch == 7))
                        return inst
                    E.step("tensor", f_mm, sem=pe, wait=[(dv, dv.n), (lg, ld_wkv)])

                    def f_ev(eng, i=i, vh=vh):
                        return eng.tensor_copy(out=vtm[:, i, 512 * vh:512 * (vh + 1)], in_=ps_a[:, :])
                    E.step("vector", f_ev, sem=dv, wait=[(pe, pe.n)])

            wprev = lg.n
            if layer == 1:
                for c in range(8):
                    E.dma("gpsimd", wbuf[:, c, 0:D], wq[128 * c:128 * (c + 1), :], lg, wait=[(dv, dv.n), (lg, wprev)])
                    E.dma("gpsimd", wbuf[:, c, D:2 * D], wo[128 * c:128 * (c + 1), :], lg, wait=[(dv, dv.n), (lg, wprev)])
            else:
                for c in range(8):
                    E.dma("gpsimd", wbuf[:, c, 0:D], wqkvT[128 * c:128 * (c + 1), 0:D], lg, wait=[(dv, dv.n), (lg, wprev)])
                    E.dma("gpsimd", wbuf[:, c, D:2 * D], woT[128 * c:128 * (c + 1), :], lg, wait=[(dv, dv.n), (lg, wprev)])
            ld_qo = lg.n

            E.step("vector", lambda eng: eng.memset(bsq[:, :], 0.0), sem=dv, wait=[(ac, ac.n)])
            for fch in range(8):
                for kc in range(NSTRIPE):
                    def f_sq(eng, fch=fch, kc=kc):
                        return eng.activation(out=scr[:, :], in_=kT[:, fch, 512 * kc:512 * (kc + 1)],
                                              func=AF.Square)
                    E.step("scalar", f_sq, sem=ac, wait=[(dv, dv.n)])
                    for half in range(2):
                        h = 2 * fch + half

                        def f_sum(eng, half=half):
                            rh = scr[0:64, :] if half == 0 else scr[64:128, :]
                            on = onesf[0:64, :] if half == 0 else onesf[64:128, :]
                            return eng.matmul(out=ps_s[:, :], lhsT=on, rhs=rh,
                                              start=True, stop=True)
                        E.step("tensor", f_sum, sem=pe, wait=[(ac, ac.n), (dv, dv.n)])

                        def f_evk(eng):
                            return eng.tensor_copy(out=srow[:, :], in_=ps_s[:, :])
                        E.step("vector", f_evk, wait=[(pe, pe.n)])

                        def f_m1(eng):
                            return eng.reduce_max(out=tmp1[:, 0:1], in_=srow[:, :], axis=AX)
                        E.step("vector", f_m1)

                        def f_m2(eng, h=h):
                            return eng.tensor_max(out=bsq[:, h:h + 1], in0=bsq[:, h:h + 1], in1=tmp1[:, 0:1])
                        E.step("vector", f_m2, sem=dv)
            for h in range(NHEAD):
                def f_sqrt(eng, h=h):
                    return eng.activation(out=bsq[:, h:h + 1], in_=bsq[:, h:h + 1], func=AF.Sqrt)
                E.step("scalar", f_sqrt, sem=ac, wait=[(dv, dv.n)])

            for stp in range(NSTRIPE):
                sl = slice(512 * stp, 512 * (stp + 1))
                for fch in range(8):
                    def f_mm(eng, fch=fch, stp=stp):
                        inst = None
                        for dch in range(8):
                            inst = eng.matmul(out=ps_a[:, :], lhsT=wbuf[:, dch, 128 * fch:128 * (fch + 1)],
                                              rhs=xaT[:, dch, 512 * stp:512 * (stp + 1)],
                                              start=(dch == 0), stop=(dch == 7))
                        return inst
                    E.step("tensor", f_mm, sem=pe, wait=[(lg, ld_qo), (dv, dv.n), (ac, ac.n)])
                    if layer == 1:
                        def f_ev(eng, fch=fch):
                            return eng.tensor_scalar_mul(qT[:, fch, :], ps_a[:, :], SCALE)
                        E.step("vector", f_ev, sem=dv, wait=[(pe, pe.n)])
                    else:
                        def f_ev(eng, fch=fch):
                            return eng.tensor_copy(out=qT[:, fch, :], in_=ps_a[:, :])
                        E.step("vector", f_ev, sem=dv, wait=[(pe, pe.n)])
                if layer == 0:
                    for fch in range(8):
                        def f_swapmm(eng, fch=fch):
                            return eng.matmul(out=ps_b[:, :], lhsT=swap_s[:, :], rhs=qT[:, fch, :],
                                              start=True, stop=True)
                        E.step("tensor", f_swapmm, sem=pe, wait=[(dv, dv.n)])

                        def f_t1(eng, fch=fch, sl=sl):
                            return eng.tensor_mul(out=scr[:, :], in0=qT[:, fch, :], in1=ctabq[:, sl])
                        E.step("vector", f_t1, wait=[(pe, pe.n)])

                        def f_t2(eng, sl=sl):
                            return eng.tensor_mul(out=scr2[:, :], in0=ps_b[:, :], in1=stabq[:, sl])
                        E.step("vector", f_t2)

                        def f_t3(eng, fch=fch):
                            return eng.tensor_add(out=qT[:, fch, :], in0=scr[:, :], in1=scr2[:, :])
                        E.step("vector", f_t3, sem=dv)

                for h in range(NHEAD):
                    fch, rlo = h // 2, (h % 2) * 64

                    def rsl(t):
                        return t[0:64, :] if rlo == 0 else t[64:128, :]

                    def f_qsq(eng, fch=fch):
                        return eng.activation(out=scr[:, :], in_=qT[:, fch, :], func=AF.Square)
                    E.step("scalar", f_qsq, sem=ac, wait=[(dv, dv.n)])

                    def f_qsum(eng, rlo=rlo):
                        rh = scr[0:64, :] if rlo == 0 else scr[64:128, :]
                        on = onesf[0:64, :] if rlo == 0 else onesf[64:128, :]
                        return eng.matmul(out=ps_s[:, :], lhsT=on, rhs=rh, start=True, stop=True)
                    E.step("tensor", f_qsum, sem=pe, wait=[(ac, ac.n)])

                    def f_qs(eng):
                        return eng.activation(out=crow[:, :], in_=ps_s[:, :], func=AF.Sqrt)
                    E.step("scalar", f_qs, sem=ac, wait=[(pe, pe.n)])

                    def f_cmul(eng, h=h):
                        return eng.tensor_scalar(out=crow[:, :], in0=crow[:, :], scalar1=bsq[:, h:h + 1],
                                                 scalar2=-1.0, op0=ALU.mult, op1=ALU.mult)
                    E.step("vector", f_cmul, wait=[(ac, ac.n)])

                    def f_cbf(eng):
                        return eng.tensor_copy(out=crow_bf[:, :], in_=crow[:, :])
                    E.step("vector", f_cbf, sem=dv)

                    qk_done = {}
                    comb_done = {}
                    for kc in range(NKC):
                        ps = ps_a if kc % 2 == 0 else ps_a2
                        if layer == 0:
                            war = [(dv, comb_done.get(kc - 2, 0))]
                        else:
                            war = [(ac, ac.n if kc < 2 else qk_done.get(("exp", kc - 2), 0))]

                        def f_qk(eng, kc=kc, fch=fch, rlo=rlo, ps=ps):
                            lhs = kT[:, fch, 128 * kc:128 * (kc + 1)]
                            q_ = qT[:, fch, :]
                            lhs = lhs[0:64, :] if rlo == 0 else lhs[64:128, :]
                            q_ = q_[0:64, :] if rlo == 0 else q_[64:128, :]
                            eng.matmul(out=ps[:, :], lhsT=lhs, rhs=q_, start=True, stop=False)
                            return eng.matmul(out=ps[:, :], lhsT=ones1[:, :], rhs=crow_bf[:, :],
                                              start=False, stop=True)
                        qk_done[kc] = E.step("tensor", f_qk, sem=pe, wait=[(dv, dv.n)] + war)
                        if layer == 0:
                            def f_iota(eng, kc=kc, stp=stp):
                                return eng.iota(out=dist[:, kc % 2, :], pattern=[[-1, 512]],
                                                base=128 * kc - 512 * stp,
                                                channel_multiplier=1, allow_small_or_imprecise_dtypes=True)
                            E.step("gpsimd", f_iota, sem=go, wait=[(ac, ac.n)])

                            def f_abs(eng, kc=kc):
                                return eng.activation(out=dist[:, kc % 2, :], in_=dist[:, kc % 2, :], func=AF.Abs)
                            E.step("scalar", f_abs, sem=ac, wait=[(go, go.n)])

                            def f_ln(eng, h=h, kc=kc):
                                return eng.activation(out=lnt[:, kc % 2, :], in_=dist[:, kc % 2, :], func=AF.Ln,
                                                      bias=cc[:, 1:2], scale=pacol[:, h:h + 1])
                            E.step("scalar", f_ln, sem=ac)

                            def f_comb(eng, h=h, kc=kc, ps=ps):
                                return eng.scalar_tensor_tensor(out=scrP[:, kc % 2, :], in0=lnt[:, kc % 2, :],
                                                                scalar=pacol[:, 16 + h:17 + h], in1=ps[:, :],
                                                                op0=ALU.mult, op1=ALU.add)
                            comb_done[kc] = E.step("vector", f_comb, sem=dv,
                                                   wait=[(ac, ac.n), (pe, qk_done[kc])])
                            if kc % 2 == 1:
                                def f_exp(eng, kc=kc):
                                    return eng.activation(out=expT[:, kc - 1:kc + 1, :],
                                                          in_=scrP[:, 0:2, :], func=AF.Exp)
                                qk_done[("exp", kc)] = E.step("scalar", f_exp, sem=ac, wait=[(dv, dv.n)])
                        else:
                            def f_exp(eng, kc=kc, ps=ps):
                                return eng.activation(out=expT[:, kc, :], in_=ps[:, :], func=AF.Exp)
                            qk_done[("exp", kc)] = E.step("scalar", f_exp, sem=ac, wait=[(pe, qk_done[kc])])

                    def f_sums(eng):
                        inst = None
                        for kc in range(NKC):
                            inst = eng.matmul(out=ps_s[:, :], lhsT=onesb[:, :], rhs=expT[:, kc, :],
                                              start=(kc == 0), stop=(kc == NKC - 1))
                        return inst
                    E.step("tensor", f_sums, sem=pe, wait=[(ac, ac.n)])

                    def f_evs(eng):
                        return eng.tensor_copy(out=srow[:, :], in_=ps_s[:, :])
                    E.step("vector", f_evs, wait=[(pe, pe.n)])

                    def f_rec(eng):
                        return eng.reciprocal(out=rrow[:, :], in_=srow[:, :])
                    E.step("vector", f_rec)

                    def f_rbf(eng):
                        return eng.tensor_copy(out=rrow_bf[:, :], in_=rrow[:, :])
                    E.step("vector", f_rbf, sem=dv)

                    def f_pv(eng, h=h):
                        inst = None
                        for kc in range(NKC):
                            inst = eng.matmul(out=ps_pv[:, :], lhsT=vtm[:, kc, 64 * h:64 * (h + 1)],
                                              rhs=expT[:, kc, :], start=(kc == 0), stop=(kc == NKC - 1))
                        return inst
                    E.step("tensor", f_pv, wait=[(ac, ac.n)])

                    def f_rb(eng):
                        return eng.matmul(out=ps_r[:, :], lhsT=ones1[0:1, 0:64], rhs=rrow_bf[:, :],
                                          start=True, stop=True)
                    E.step("tensor", f_rb, sem=pe, wait=[(dv, dv.n)])

                    def f_evrb(eng):
                        return eng.tensor_copy(out=rb[:, :], in_=ps_r[:, :])
                    E.step("vector", f_evrb, wait=[(pe, pe.n)])

                    def f_ao(eng, fch=fch, rlo=rlo):
                        dst = aoT[:, fch, :]
                        dst = dst[0:64, :] if rlo == 0 else dst[64:128, :]
                        return eng.tensor_mul(out=dst, in0=ps_pv[:, :], in1=rb[:, :])
                    E.step("vector", f_ao, sem=dv)

                for tt in range(4):
                    t0 = 512 * stp + 128 * tt
                    c_r = E.dma("sync", xres[:, :], xseq[t0:t0 + 128, :], ld, wait=[(dv, dv.n), (st, st.n)])
                    for fh in range(2):
                        def f_mm(eng, tt=tt, fh=fh):
                            inst = None
                            for c in range(8):
                                inst = eng.matmul(out=ps_b[:, :], lhsT=aoT[:, c, 128 * tt:128 * (tt + 1)],
                                                  rhs=wbuf[:, c, D + 512 * fh:D + 512 * (fh + 1)],
                                                  start=(c == 0), stop=(c == 7))
                            return inst
                        E.step("tensor", f_mm, sem=pe, wait=[(dv, dv.n), (ld, c_r)])

                        def f_res(eng, fh=fh):
                            return eng.scalar_tensor_tensor(out=xout[:, 512 * fh:512 * (fh + 1)], in0=ps_b[:, :],
                                                            scalar=1.0, in1=xres[:, 512 * fh:512 * (fh + 1)],
                                                            op0=ALU.mult, op1=ALU.add)
                        E.step("vector", f_res, sem=dv, wait=[(pe, pe.n)])
                    E.dma("scalar", xoseq[t0:t0 + 128, :], xout[:, :], st, wait=[(dv, dv.n)])







class NcPrefix:
    def __init__(self, nc, prefix):
        self._nc = nc
        self._p = prefix

    def sbuf_tensor(self, name, *a, **k):
        return self._nc.sbuf_tensor(self._p + name, *a, **k)

    def psum_tensor(self, name, *a, **k):
        return self._nc.psum_tensor(self._p + name, *a, **k)

    @property
    def gpsimd(self):
        return self._nc.gpsimd


def new_program(races=True):
    nc = bass.Bass("TRN2", target_bir_lowering=False, detect_race_conditions=races)
    return nc


def make_sems(nc, ctx):
    return tuple(Sem(ctx.enter_context(nc.semaphore(nm)))
                 for nm in ("ld", "pe", "dv", "ac", "go", "st", "lg"))


def full_barrier(E, sems):
    waits = [(s, s.n) for s in sems]
    for e in ("sync", "scalar", "vector", "tensor", "gpsimd"):
        E.emit(e, None, wait=waits)


def finish_program(nc, ctx, E, sems):
    ld, pe, dv, ac, go, st, lg = sems
    E.emit("gpsimd", None, wait=[(st, st.n), (lg, lg.n)])
    block = ctx.enter_context(nc.Block())
    E.replay(block)


def build_attn(layer):
    from contextlib import ExitStack
    nc = new_program()
    io = {"x": nc.dram_tensor("x", [TOK, D], F32, kind="ExternalInput").ap(),
          "identb": nc.dram_tensor("identb", [128, 128], BF16, kind="ExternalInput").ap(),
          "xo": nc.dram_tensor("xo", [TOK, D], F32, kind="ExternalOutput").ap()}
    if layer == 1:
        for n, sh in (("wq", [D, D]), ("wk", [D, D]), ("wv", [D, D]), ("wo", [D, D]),
                      ("gamma", [1, D]), ("beta", [1, D])):
            io[n] = nc.dram_tensor(n, sh, F32, kind="ExternalInput").ap()
    else:
        io["wqkvT"] = nc.dram_tensor("wqkvT", [D, 3 * D], F32, kind="ExternalInput").ap()
        io["woT"] = nc.dram_tensor("woT", [D, D], F32, kind="ExternalInput").ap()
        io["freqs"] = nc.dram_tensor("freqs", [1, 32], F32, kind="ExternalInput").ap()
        io["repmat"] = nc.dram_tensor("repmat", [32, 128], BF16, kind="ExternalInput").ap()
        io["repsign"] = nc.dram_tensor("repsign", [32, 128], BF16, kind="ExternalInput").ap()
        io["swapm"] = nc.dram_tensor("swapm", [128, 128], BF16, kind="ExternalInput").ap()
        io["pa_rows"] = nc.dram_tensor("pa_rows", [2, 16], F32, kind="ExternalInput").ap()
    with ExitStack() as octx:
        sems = make_sems(nc, octx)
        ld, pe, dv, ac, go, st, lg = sems
        E = Emitter(auto={"vector": dv, "scalar": ac, "gpsimd": go})
        emit_attn(nc, E, sems, io, layer)
        finish_program(nc, octx, E, sems)
    return nc


def _bf(a):
    return np.asarray(a).astype(ml_dtypes.bfloat16)

def make_consts():
    identb = _bf(np.eye(128, dtype=np.float32))
    rep = np.zeros((32, 128), np.float32)
    for p in range(128):
        rep[p % 32, p] = 1.0
    repsign = np.zeros((32, 128), np.float32)
    for p in range(128):
        sgn = -1.0 if (p % 64) < 32 else 1.0
        repsign[p % 32, p] = sgn
    swap = np.zeros((128, 128), np.float32)
    for p in range(128):
        q = (p % 64)
        src = (p - p % 64) + ((q + 32) % 64)
        swap[src, p] = 1.0
    pos = np.arange(S, dtype=np.float32)[None, :]
    return identb, _bf(rep), _bf(repsign), _bf(swap), pos


def shard_x(xfull):
    return [np.ascontiguousarray(xfull[CORE_BATCHES[i]].reshape(TOK, D)) for i in range(NCORES)]


def unshard_x(shards):
    out = np.empty((B, S, D), np.float32)
    for i in range(NCORES):
        sh = shards[i].reshape(SEQ_PER_CORE, S, D)
        out[CORE_BATCHES[i][0]] = sh[0]
        out[CORE_BATCHES[i][1]] = sh[1]
    return out


def run_attn(layer, xs, weights):
    return run_attn_cached(layer, xs, weights)


def _run_attn_old(layer, xs, weights):
    nc = build_attn(layer)
    identb, rep, repsign, swap, pos = make_consts()
    in_maps = []
    for i in range(NCORES):
        m = {"x": xs[i], "identb": identb}
        if layer == 1:
            m.update(wq=weights["wq"], wk=weights["wk"], wv=weights["wv"], wo=weights["wo"],
                     gamma=weights["gamma"], beta=weights["beta"])
        else:
            m.update(wqkvT=weights["wqkvT"], woT=weights["woT"],
                     freqs=weights["freqs"], repmat=rep, repsign=repsign, swapm=swap,
                     pa_rows=weights["pa_rows"])
        in_maps.append(m)
    res = run_bass_kernel_spmd(nc, in_maps, core_ids=list(range(NCORES)))
    return [r["xo"] for r in res.results]





def emit_ffn_l1(nc, E, sems, io, sparse=True):
    x = io["x"]
    identb, identf = io["identb"], io["identf"]
    ln_w, ln_b = io["f1_ln_w"], io["f1_ln_b"]
    router_w = io["f1_router_w"]
    gate_w, up_w, down_w = io["f1_gate_w"], io["f1_up_w"], io["f1_down_w"]
    sh_gate, sh_up, sh_down = io["f1_sh_gate"], io["f1_sh_up"], io["f1_sh_down"]
    xo = io["xo"]
    xfT_dram = io["xfT_dram"]
    routed_dram = io["y_dram"]
    if sparse:
        ltri = io["ltri"]
        disp_dram = io["disp1_dram"]
        eout_dram = io["eout1_dram"]

    NT = TOK // 128
    NST = TOK // 512
    from contextlib import ExitStack
    with ExitStack() as ctx:
        wg = ctx.enter_context(nc.sbuf_tensor("wg", [128, 8, 2048], BF16))
        wu = ctx.enter_context(nc.sbuf_tensor("wu", [128, 8, 2048], BF16))
        wd = ctx.enter_context(nc.sbuf_tensor("wd", [128, 16, D], BF16))
        xfTs = ctx.enter_context(nc.sbuf_tensor("xfTs", [128, 8, 512], BF16))
        xeTs = ctx.enter_context(nc.sbuf_tensor("xeTs", [128, 8, 512], BF16))
        hT = ctx.enter_context(nc.sbuf_tensor("hT", [128, 16, 512], BF16))
        xt = ctx.enter_context(nc.sbuf_tensor("xt", [128, D], F32))
        xa = ctx.enter_context(nc.sbuf_tensor("xa", [128, D], F32))
        rt = ctx.enter_context(nc.sbuf_tensor("rt", [128, D], F32))
        xout = ctx.enter_context(nc.sbuf_tensor("xout", [128, D], F32))
        xfTc = ctx.enter_context(nc.sbuf_tensor("xfTc", [128, 128], F32))
        rw_s = ctx.enter_context(nc.sbuf_tensor("rw_s", [128, 8, 8], F32))
        if sparse:
            ltri_s = ctx.enter_context(nc.sbuf_tensor("ltri_s", [128, 128], F32))
            carry = ctx.enter_context(nc.sbuf_tensor("carry", [1, 8], F32))
            io8 = ctx.enter_context(nc.sbuf_tensor("io8", [128, 8], F32))
            io8r = ctx.enter_context(nc.sbuf_tensor("io8r", [1, 8], F32))
            oh = ctx.enter_context(nc.sbuf_tensor("oh", [128, 8], F32))
            tsc = ctx.enter_context(nc.sbuf_tensor("tsc", [128, 8], F32))
            cum = ctx.enter_context(nc.sbuf_tensor("cum", [128, 8], F32))
            col = ctx.enter_context(nc.sbuf_tensor("col", [128, 8], F32))
            xasc = ctx.enter_context(nc.sbuf_tensor("xasc", [128, D], F32))
            idx1_store = ctx.enter_context(nc.sbuf_tensor("idx1_store", [128, NT, 1], U32))
            dtile = ctx.enter_context(nc.sbuf_tensor("dtile", [128, D], BF16))
            grb = ctx.enter_context(nc.sbuf_tensor("grb", [128, D], BF16))
            ybf2 = ctx.enter_context(nc.sbuf_tensor("ybf2", [128, 512], BF16))
            ybf2B = ctx.enter_context(nc.sbuf_tensor("ybf2B", [128, 512], BF16))
        bn6 = ctx.enter_context(nc.sbuf_tensor("bn6", [128, 12], F32))
        bnag = ctx.enter_context(nc.sbuf_tensor("bnag", [128, 2], F32))
        stats_all = ctx.enter_context(nc.sbuf_tensor("stats_all", [128, NT, 2], F32))
        stats2 = ctx.enter_context(nc.sbuf_tensor("stats2", [128, 2], F32))
        gb = ctx.enter_context(nc.sbuf_tensor("gb", [128, 2 * D], F32))
        cc = ctx.enter_context(nc.sbuf_tensor("cc", [128, 2], F32))
        lrow = ctx.enter_context(nc.sbuf_tensor("lrow", [128, 8], F32))
        m1c = ctx.enter_context(nc.sbuf_tensor("m1c", [128, 1], F32))
        wtok = ctx.enter_context(nc.sbuf_tensor("wtok", [128, 1], F32))
        s_store = ctx.enter_context(nc.sbuf_tensor("s_store", [128, NT, 8], F32))
        srowf = ctx.enter_context(nc.sbuf_tensor("srowf", [1, 512], F32))
        srowb = ctx.enter_context(nc.sbuf_tensor("srowb", [1, 512], BF16))
        sbcast = ctx.enter_context(nc.sbuf_tensor("sbcast", [128, 512], BF16))
        scr = ctx.enter_context(nc.sbuf_tensor("scr", [128, 512], F32))
        scrB = ctx.enter_context(nc.sbuf_tensor("scrB", [128, 512], F32))
        yt = ctx.enter_context(nc.sbuf_tensor("yt", [128, 512], F32))
        ytB = ctx.enter_context(nc.sbuf_tensor("ytB", [128, 512], F32))
        onesb = ctx.enter_context(nc.sbuf_tensor("onesb", [128, 1], BF16))
        onesfc = ctx.enter_context(nc.sbuf_tensor("onesfc", [128, 1], F32))
        ones1 = ctx.enter_context(nc.sbuf_tensor("ones1", [1, 128], BF16))
        ones1f = ctx.enter_context(nc.sbuf_tensor("ones1f", [1, 128], F32))
        identb_s = ctx.enter_context(nc.sbuf_tensor("identb_s", [128, 128], BF16))
        identf_s = ctx.enter_context(nc.sbuf_tensor("identf_s", [128, 128], F32))
        ps_a = ctx.enter_context(nc.psum_tensor("ps_a", [128, 512], F32))
        ps_a2 = ctx.enter_context(nc.psum_tensor("ps_a2", [128, 512], F32))
        ps_b = ctx.enter_context(nc.psum_tensor("ps_b", [128, 512], F32))
        ps_b2 = ctx.enter_context(nc.psum_tensor("ps_b2", [128, 512], F32))
        ps_trf = ctx.enter_context(nc.psum_tensor("ps_trf", [128, 128], F32))
        if sparse:
            ps_trb = ctx.enter_context(nc.psum_tensor("ps_trb", [128, 128], BF16))
        ps_d = ctx.enter_context(nc.psum_tensor("ps_d", [128, 512], F32))
        ps_d2 = ctx.enter_context(nc.psum_tensor("ps_d2", [128, 512], F32))
        ps_l = ps_d[:, 0:8]
        ps_cum = ps_d[:, 8:16]
        ps_cnt = ps_d[0:1, 16:24]
        ld, pe, dv, ac, go, st, lg = sems

        E.step("vector", lambda eng: eng.memset(onesb[:, :], 1.0))
        E.step("vector", lambda eng: eng.memset(onesfc[:, :], 1.0))
        E.step("vector", lambda eng: eng.memset(ones1[:, :], 1.0))
        E.step("vector", lambda eng: eng.memset(ones1f[:, :], 1.0))
        E.step("vector", lambda eng: eng.memset(cc[:, 0:1], EPS))
        E.step("vector", lambda eng: eng.memset(cc[:, 1:2], 1.0), sem=dv)
        E.dma("sync", identb_s[:, :], identb[:, :], ld)
        E.dma("sync", identf_s[:, :], identf[:, :], ld)
        for c in range(8):
            E.dma("sync", rw_s[:, c, :], router_w[128 * c:128 * (c + 1), :], ld)
        if sparse:
            E.dma("sync", ltri_s[:, :], ltri[:, :], ld)
        E.dma("sync", gb[0:1, 0:D], ln_w[:, :], ld)
        c_b = E.dma("sync", gb[0:1, D:2 * D], ln_b[:, :], ld)
        for half in range(4):
            def f_mm(eng, half=half):
                return eng.matmul(out=ps_a[:, :], lhsT=ones1f[:, :],
                                  rhs=gb[0:1, 512 * half:512 * (half + 1)], start=True, stop=True)
            E.step("tensor", f_mm, sem=pe, wait=[(ld, c_b), (dv, dv.n)])

            def f_ev(eng, half=half):
                return eng.tensor_copy(out=gb[:, 512 * half:512 * (half + 1)], in_=ps_a[:, :])
            E.step("vector", f_ev, sem=dv, wait=[(pe, pe.n)])

        if sparse:
            E.step("vector", lambda eng: eng.memset(carry[:, :], 0.0), sem=dv)
            rbound1 = nc.gpsimd.alloc_register()
            E.emit("gpsimd", lambda eng: eng.reg_mov(rbound1, 8 * C1LOC - 1))

            def f_io8(eng):
                return eng.iota(out=io8r[:, :], pattern=[[1, 8]], base=0,
                                channel_multiplier=0, allow_small_or_imprecise_dtypes=True)
            E.step("gpsimd", f_io8, sem=go)

            def f_io8b(eng):
                return eng.matmul(out=ps_l[:, 0:8], lhsT=ones1f[:, :], rhs=io8r[:, :],
                                  start=True, stop=True)
            E.step("tensor", f_io8b, sem=pe, wait=[(go, go.n), (dv, dv.n), (ld, ld.n)])

            def f_evio8(eng):
                return eng.tensor_copy(out=io8[:, :], in_=ps_l[:, 0:8])
            E.step("vector", f_evio8, sem=dv, wait=[(pe, pe.n)])
            E.step("vector", lambda eng: eng.memset(grb[:, :], 0.0), sem=dv)
            for r in range(8 * C1LOC // 128):
                E.dma("gpsimd", disp_dram[128 * r:128 * (r + 1), :], grb[:, :], lg,
                      wait=[(dv, dv.n)])
            zero1_done = lg.n

        for i in range(NT):
            c_x = E.dma("sync", xt[:, :], x[128 * i:128 * (i + 1), :], ld, wait=[(dv, dv.n)])
            emit_ln_stats(E, bn6, bnag, xt, D, stats_all[:, i, 0:1], stats_all[:, i, 1:2], dv, ac, cc[:, 0:1],
                          wait=[(ld, c_x)])

            def f_norm(eng, i=i):
                return eng.tensor_scalar(out=xa[:, :], in0=xt[:, :], scalar1=stats_all[:, i, 0:1],
                                         scalar2=stats_all[:, i, 1:2], op0=ALU.subtract, op1=ALU.mult)
            E.step("vector", f_norm, wait=[(ac, ac.n)])
            emit_ln_stats(E, bn6, bnag, xa, D, stats2[:, 0:1], stats2[:, 1:2], dv, ac, cc[:, 0:1])

            def f_norm2(eng):
                return eng.tensor_scalar(out=xa[:, :], in0=xa[:, :], scalar1=stats2[:, 0:1],
                                         scalar2=stats2[:, 1:2], op0=ALU.subtract, op1=ALU.mult)
            E.step("vector", f_norm2, wait=[(ac, ac.n)])

            def f_gam(eng):
                return eng.tensor_mul(out=xa[:, :], in0=xa[:, :], in1=gb[:, 0:D])
            E.step("vector", f_gam)

            def f_bet(eng):
                return eng.tensor_add(out=xa[:, :], in0=xa[:, :], in1=gb[:, D:2 * D])
            E.step("vector", f_bet, sem=dv)
            for c in range(8):
                def f_tr(eng, c=c):
                    return eng.transpose(out=ps_trf[:, :], in_=xa[:, 128 * c:128 * (c + 1)],
                                         identity=identf_s[:, :])
                E.step("tensor", f_tr, sem=pe, wait=[(dv, dv.n), (go, go.n)])

                def f_evt(eng):
                    return eng.tensor_copy(out=xfTc[:, :], in_=ps_trf[:, :])
                E.step("vector", f_evt, wait=[(pe, pe.n)])

                def f_cast(eng, c=c):
                    return eng.tensor_copy(out=xfTs[:, c, 0:128], in_=xfTc[:, :])
                E.step("vector", f_cast, sem=dv, wait=[(lg, lg.n)])

                def f_rmm(eng, c=c):
                    return eng.matmul(out=ps_l[:, :], lhsT=xfTc[:, :], rhs=rw_s[:, c, :],
                                      start=(c == 0), stop=(c == 7))
                E.step("tensor", f_rmm, sem=pe, wait=[(dv, dv.n)])
                E.dma("gpsimd", xfT_dram[c, :, 128 * i:128 * (i + 1)], xfTs[:, c, 0:128], lg,
                      wait=[(dv, dv.n)])
            def f_evl(eng):
                return eng.tensor_copy(out=lrow[:, :], in_=ps_l[:, :])
            E.step("vector", f_evl, wait=[(pe, pe.n)])

            def f_m1(eng):
                return eng.reduce_max(out=m1c[:, :], in_=lrow[:, :], axis=AX)
            E.step("vector", f_m1, sem=dv)

            def f_sig(eng):
                return eng.activation(out=wtok[:, :], in_=m1c[:, :], func=AF.Sigmoid)
            E.step("scalar", f_sig, sem=ac, wait=[(dv, dv.n)])

            if not sparse:
                def f_oh(eng, i=i):
                    return eng.tensor_scalar(out=s_store[:, i, :], in0=lrow[:, :], scalar1=m1c[:, :],
                                             scalar2=None, op0=ALU.is_equal)
                E.step("vector", f_oh, wait=[(ac, ac.n)])

                def f_sc(eng, i=i):
                    return eng.tensor_scalar(out=s_store[:, i, :], in0=s_store[:, i, :], scalar1=wtok[:, :],
                                             scalar2=None, op0=ALU.mult)
                E.step("vector", f_sc, sem=dv)
            else:
                def f_oh(eng):
                    return eng.tensor_scalar(out=oh[:, :], in0=lrow[:, :], scalar1=m1c[:, :],
                                             scalar2=None, op0=ALU.is_equal)
                E.step("vector", f_oh, wait=[(ac, ac.n)])
                def f_xsc(eng):
                    return eng.tensor_scalar(out=xasc[:, :], in0=xa[:, :], scalar1=wtok[:, :],
                                             scalar2=None, op0=ALU.mult)
                E.step("vector", f_xsc, sem=dv, wait=[(lg, lg.n)])
                def f_cum(eng):
                    eng.matmul(out=ps_cum[:, 0:8], lhsT=ltri_s[:, :], rhs=oh[:, :],
                               start=True, stop=False)
                    return eng.matmul(out=ps_cum[:, 0:8], lhsT=ones1f[:, :], rhs=carry[:, :],
                                      start=False, stop=True)
                E.step("tensor", f_cum, sem=pe, wait=[(dv, dv.n)])

                def f_ccnt(eng):
                    return eng.matmul(out=ps_cnt[:, 0:8], lhsT=onesfc[:, :], rhs=oh[:, :],
                                      start=True, stop=True)
                E.step("tensor", f_ccnt, sem=pe)

                def f_evcum(eng):
                    return eng.tensor_copy(out=cum[:, :], in_=ps_cum[:, 0:8])
                E.step("vector", f_evcum, wait=[(pe, pe.n)])

                def f_carry(eng):
                    return eng.tensor_add(out=carry[:, :], in0=carry[:, :], in1=ps_cnt[:, 0:8])
                E.step("vector", f_carry)
                def f_l1(eng):
                    return eng.tensor_mul(out=tsc[:, :], in0=oh[:, :], in1=cum[:, :])
                E.step("vector", f_l1)

                def f_l2(eng):
                    return eng.reduce_sum(out=col[:, 0:1], in_=tsc[:, :], axis=AX)
                E.step("vector", f_l2)

                def f_e1(eng):
                    return eng.tensor_mul(out=tsc[:, :], in0=oh[:, :], in1=io8[:, :])
                E.step("vector", f_e1)

                def f_e2(eng):
                    return eng.reduce_sum(out=col[:, 1:2], in_=tsc[:, :], axis=AX)
                E.step("vector", f_e2)

                def f_slot(eng):
                    return eng.scalar_tensor_tensor(out=col[:, 1:2], in0=col[:, 1:2], scalar=float(C1LOC),
                                                    in1=col[:, 0:1], op0=ALU.mult, op1=ALU.add)
                E.step("vector", f_slot)
                def f_v1(eng):
                    return eng.tensor_scalar(out=col[:, 2:3], in0=col[:, 0:1], scalar1=-1.0,
                                             scalar2=float(C1LOC), op0=ALU.mult, op1=ALU.add)
                E.step("vector", f_v1)

                def f_v2(eng):
                    return eng.tensor_scalar_max(col[:, 2:3], col[:, 2:3], 0.0)
                E.step("vector", f_v2)

                def f_v3(eng):
                    return eng.tensor_scalar_min(col[:, 2:3], col[:, 2:3], 1.0)
                E.step("vector", f_v3)

                def f_inv(eng):
                    return eng.tensor_scalar(out=col[:, 2:3], in0=col[:, 2:3], scalar1=-1.0,
                                             scalar2=1.0, op0=ALU.mult, op1=ALU.add)
                E.step("vector", f_inv)

                def f_idx(eng):
                    return eng.scalar_tensor_tensor(out=col[:, 1:2], in0=col[:, 2:3], scalar=BIGIDX,
                                                    in1=col[:, 1:2], op0=ALU.mult, op1=ALU.add)
                E.step("vector", f_idx)

                def f_u32(eng, i=i):
                    return eng.tensor_copy(out=idx1_store[:, i, 0:1], in_=col[:, 1:2])
                E.step("vector", f_u32, sem=dv)

                def f_sct(eng, i=i):
                    return eng.indirect_dma_start(
                        out=disp_dram[:, :],
                        out_offset=IndirectOffsetOnAxis(idx1_store[:, i, 0:1], 0),
                        in_=xasc[:, :], in_offset=None,
                        bounds_check=rbound1, oob_is_err=False)
                lg.n += 16
                E.emit("gpsimd", f_sct, wait=[(dv, dv.n), (lg, zero1_done)], inc=[(lg, 16)])

        for e in range(9):
            wprev = lg.n
            if e < 9:
                gsrc = gate_w[e, :, :] if e < 8 else sh_gate[:, :]
                usrc = up_w[e, :, :] if e < 8 else sh_up[:, :]
                dsrc = down_w[e, :, :] if e < 8 else sh_down[:, :]
            for c in range(8):
                E.dma("gpsimd", wg[:, c, :], gsrc[128 * c:128 * (c + 1), :], lg,
                      wait=[(pe, pe.n), (lg, wprev)])
                E.dma("gpsimd", wu[:, c, :], usrc[128 * c:128 * (c + 1), :], lg,
                      wait=[(pe, pe.n), (lg, wprev)])
            for c in range(16):
                E.dma("gpsimd", wd[:, c, :], dsrc[128 * c:128 * (c + 1), :], lg,
                      wait=[(pe, pe.n), (lg, wprev)])
            ld_w = lg.n
            nstripes = (C1LOC // 512 if (sparse and e < 8) else NST)
            for stp in range(nstripes):
                if sparse and e < 8:
                    for q in range(4):
                        r0 = e * C1LOC + 512 * stp + 128 * q
                        c_d = E.dma("sync", dtile[:, :], disp_dram[r0:r0 + 128, :], ld,
                                    wait=[(pe, pe.n), (dv, dv.n), (lg, lg.n)])
                        for c in range(8):
                            def f_trd(eng, c=c):
                                return eng.transpose(out=ps_trb[:, :], in_=dtile[:, 128 * c:128 * (c + 1)],
                                                     identity=identb_s[:, :])
                            E.step("tensor", f_trd, sem=pe, wait=[(ld, c_d), (dv, dv.n)])

                            def f_evd(eng, c=c, q=q):
                                return eng.tensor_copy(out=xfTs[:, c, 128 * q:128 * (q + 1)], in_=ps_trb[:, :])
                            E.step("vector", f_evd, sem=dv, wait=[(pe, pe.n)])
                    c_xf = ld.n
                else:
                    c_xf = 0
                    for c in range(8):
                        c_xf = E.dma("sync", xfTs[:, c, 0:512], xfT_dram[c, :, 512 * stp:512 * (stp + 1)], ld,
                                     wait=[(pe, pe.n), (dv, dv.n), (lg, lg.n)])
                if e < 8 and not sparse:
                    for q in range(4):
                        tile = 4 * stp + q

                        def f_trs(eng, tile=tile, e=e):
                            return eng.transpose(out=ps_trf[0:1, 0:128], in_=s_store[:, tile, e:e + 1],
                                                 identity=identf_s[:, :])
                        E.step("tensor", f_trs, sem=pe, wait=[(dv, dv.n)])

                        def f_evs(eng, q=q):
                            return eng.tensor_copy(out=srowf[:, 128 * q:128 * (q + 1)], in_=ps_trf[0:1, 0:128])
                        E.step("vector", f_evs, sem=dv, wait=[(pe, pe.n)])

                    def f_sbf(eng):
                        return eng.tensor_copy(out=srowb[:, :], in_=srowf[:, :])
                    E.step("vector", f_sbf, sem=dv)

                    def f_sb(eng):
                        return eng.matmul(out=ps_b[:, :], lhsT=ones1[:, :], rhs=srowb[:, :],
                                          start=True, stop=True)
                    E.step("tensor", f_sb, sem=pe, wait=[(dv, dv.n)])

                    def f_evsb(eng):
                        return eng.tensor_copy(out=sbcast[:, :], in_=ps_b[:, :])
                    E.step("vector", f_evsb, sem=dv, wait=[(pe, pe.n), (ld, c_xf)])
                    for c in range(8):
                        def f_xe(eng, c=c):
                            return eng.tensor_mul(out=xeTs[:, c, :], in0=xfTs[:, c, :], in1=sbcast[:, :])
                        E.step("vector", f_xe, sem=dv)
                    xin = xeTs
                    xe_done = dv.n
                else:
                    E.emit("vector", None, wait=[(ld, c_xf)])
                    xin = xfTs
                    xe_done = dv.n
                silu_done = {}
                h_done = {}
                for ich in range(16):
                    pg = ps_a if ich % 2 == 0 else ps_a2
                    pu = ps_b if ich % 2 == 0 else ps_b2
                    sc = scr if ich % 2 == 0 else scrB

                    def f_zg(eng, ich=ich, xin=xin, pg=pg):
                        inst = None
                        for dch in range(8):
                            inst = eng.matmul(out=pg[:, :], lhsT=wg[:, dch, 128 * ich:128 * (ich + 1)],
                                              rhs=xin[:, dch, :], start=(dch == 0), stop=(dch == 7))
                        return inst
                    E.step("tensor", f_zg, sem=pe,
                           wait=[(lg, ld_w), (ld, c_xf), (dv, xe_done),
                                 (ac, silu_done.get(ich - 2, ac.n if ich < 2 else 0))])

                    def f_silu(eng, sc=sc, pg=pg):
                        return eng.activation(out=sc[:, :], in_=pg[:, :], func=AF.Silu)
                    silu_done[ich] = E.step("scalar", f_silu, sem=ac,
                                            wait=[(pe, pe.n), (dv, h_done.get(ich - 2, dv.n if ich < 2 else 0))])

                    def f_zu(eng, ich=ich, xin=xin, pu=pu):
                        inst = None
                        for dch in range(8):
                            inst = eng.matmul(out=pu[:, :], lhsT=wu[:, dch, 128 * ich:128 * (ich + 1)],
                                              rhs=xin[:, dch, :], start=(dch == 0), stop=(dch == 7))
                        return inst
                    E.step("tensor", f_zu, sem=pe,
                           wait=[(dv, h_done.get(ich - 2, dv.n if ich < 2 else 0))])

                    def f_h(eng, ich=ich, sc=sc, pu=pu):
                        return eng.tensor_mul(out=hT[:, ich, :], in0=sc[:, :], in1=pu[:, :])
                    h_done[ich] = E.step("vector", f_h, sem=dv, wait=[(pe, pe.n), (ac, silu_done[ich])])
                evy_done = {}
                acc_done = {}
                for k8, (tt, dh) in enumerate([(a, b) for a in range(4) for b in range(2)]):
                    pd = ps_d if k8 % 2 == 0 else ps_d2
                    yb = yt if k8 % 2 == 0 else ytB

                    def f_dn(eng, tt=tt, dh=dh, pd=pd):
                        inst = None
                        for ich in range(16):
                            inst = eng.matmul(out=pd[:, :], lhsT=hT[:, ich, 128 * tt:128 * (tt + 1)],
                                              rhs=wd[:, ich, 512 * dh:512 * (dh + 1)],
                                              start=(ich == 0), stop=(ich == 15))
                        return inst
                    E.step("tensor", f_dn, sem=pe,
                           wait=[(dv, dv.n if k8 < 2 else evy_done.get(k8 - 2, 0)),
                                 (go, go.n if k8 == 0 else 0)])

                    if sparse and e < 8:
                        ybs = ybf2 if k8 % 2 == 0 else ybf2B

                        def f_evy(eng, pd=pd, ybs=ybs):
                            return eng.tensor_copy(out=ybs[:, :], in_=pd[:, :])
                        evy_done[k8] = E.step("vector", f_evy, sem=dv,
                                              wait=[(pe, pe.n), (st, st.n)])
                        r0 = e * C1LOC + 512 * stp + 128 * tt
                        E.dma("scalar", eout_dram[r0:r0 + 128, 512 * dh:512 * (dh + 1)], ybs[:, :],
                              st, wait=[(dv, dv.n)])
                    else:
                        def f_evy(eng, pd=pd, yb=yb):
                            return eng.tensor_copy(out=yb[:, :], in_=pd[:, :])
                        evy_done[k8] = E.step("vector", f_evy, sem=dv,
                                              wait=[(pe, pe.n), (lg, lg.n)])
                        t0 = 512 * stp + 128 * tt

                        def f_acc(eng, t0=t0, dh=dh, e=e, yb=yb):
                            return eng.dma_start(out=routed_dram[t0:t0 + 128, 512 * dh:512 * (dh + 1)],
                                                 in_=yb[:, :],
                                                 accum_op=(ALU.bypass if (e == 0 or (sparse and e == 8)) else ALU.add))
                        lg.n += 16
                        acc_done[k8] = lg.n
                        E.emit("gpsimd", f_acc, wait=[(dv, dv.n)], inc=[(lg, 16)])

        acc_done = lg.n
        eout1_done = st.n
        for i in range(NT):
            c_x = E.dma("sync", xt[:, :], x[128 * i:128 * (i + 1), :], ld,
                        wait=[(dv, dv.n), (lg, acc_done)])
            c_r = E.dma("sync", rt[:, :], routed_dram[128 * i:128 * (i + 1), :], ld,
                        wait=[(st, st.n)])
            if sparse:
                E.step("vector", lambda eng: eng.memset(grb[:, :], 0.0), wait=[(lg, lg.n)])
                E.step("vector", lambda eng: eng.engine_nop(), sem=dv)

                def f_ga1(eng, i=i):
                    return eng.indirect_dma_start(
                        out=grb[:, :], out_offset=None,
                        in_=eout_dram[:, :],
                        in_offset=IndirectOffsetOnAxis(idx1_store[:, i, 0:1], 0),
                        bounds_check=rbound1, oob_is_err=False)
                lg.n += 16
                E.emit("gpsimd", f_ga1, wait=[(dv, dv.n), (st, eout1_done)], inc=[(lg, 16)])
                gath = lg.n

            def f_h2(eng, i=i):
                return eng.tensor_scalar(out=xa[:, :], in0=xt[:, :], scalar1=stats_all[:, i, 0:1],
                                         scalar2=stats_all[:, i, 1:2], op0=ALU.subtract, op1=ALU.mult)
            E.step("vector", f_h2, wait=[(ld, c_r), (st, st.n)])

            def f_add(eng):
                return eng.tensor_add(out=xa[:, :], in0=xa[:, :], in1=rt[:, :])
            E.step("vector", f_add)
            if sparse:
                def f_addg(eng):
                    return eng.tensor_add(out=xa[:, :], in0=xa[:, :], in1=grb[:, :])
                E.step("vector", f_addg, wait=[(lg, gath)])

            def f_out(eng):
                return eng.scalar_tensor_tensor(out=xout[:, :], in0=xa[:, :], scalar=0.5, in1=xt[:, :],
                                                op0=ALU.mult, op1=ALU.add)
            E.step("vector", f_out, sem=dv)
            E.dma("scalar", xo[128 * i:128 * (i + 1), :], xout[:, :], st, wait=[(dv, dv.n)])





def ffn1_io(nc, io):
    io["f1_ln_w"] = nc.dram_tensor("f1_ln_w", [1, D], F32, kind="ExternalInput").ap()
    io["f1_ln_b"] = nc.dram_tensor("f1_ln_b", [1, D], F32, kind="ExternalInput").ap()
    io["f1_router_w"] = nc.dram_tensor("f1_router_w", [D, 8], F32, kind="ExternalInput").ap()
    io["f1_gate_w"] = nc.dram_tensor("f1_gate_w", [8, D, 2048], F32, kind="ExternalInput").ap()
    io["f1_up_w"] = nc.dram_tensor("f1_up_w", [8, D, 2048], F32, kind="ExternalInput").ap()
    io["f1_down_w"] = nc.dram_tensor("f1_down_w", [8, 2048, D], F32, kind="ExternalInput").ap()
    io["f1_sh_gate"] = nc.dram_tensor("f1_sh_gate", [D, 2048], F32, kind="ExternalInput").ap()
    io["f1_sh_up"] = nc.dram_tensor("f1_sh_up", [D, 2048], F32, kind="ExternalInput").ap()
    io["f1_sh_down"] = nc.dram_tensor("f1_sh_down", [2048, D], F32, kind="ExternalInput").ap()


def build_ffn_l1():
    from contextlib import ExitStack
    nc = new_program()
    io = {"x": nc.dram_tensor("x", [TOK, D], F32, kind="ExternalInput").ap(),
          "identb": nc.dram_tensor("identb", [128, 128], BF16, kind="ExternalInput").ap(),
          "identf": nc.dram_tensor("identf", [128, 128], F32, kind="ExternalInput").ap(),
          "xo": nc.dram_tensor("xo", [TOK, D], F32, kind="ExternalOutput").ap(),
          "ltri": nc.dram_tensor("ltri", [128, 128], F32, kind="ExternalInput").ap(),
          "xfT_dram": nc.dram_tensor("xfT_dram", [8, 128, TOK], BF16, kind="Internal").ap(),
          "y_dram": nc.dram_tensor("y_dram", [TOK, D], F32, kind="Internal").ap(),
          "disp1_dram": nc.dram_tensor("disp1_dram", [8 * C1LOC, D], BF16, kind="Internal").ap(),
          "eout1_dram": nc.dram_tensor("eout1_dram", [8 * C1LOC, D], BF16, kind="Internal").ap()}
    ffn1_io(nc, io)
    with ExitStack() as octx:
        sems = make_sems(nc, octx)
        ld, pe, dv, ac, go, st, lg = sems
        E = Emitter(auto={"vector": dv, "scalar": ac, "gpsimd": go})
        emit_ffn_l1(nc, E, sems, io)
        finish_program(nc, octx, E, sems)
    return nc

def run_ffn_l1(xs, weights):
    return run_ffn_l1_cached(xs, weights)


def _run_ffn_l1_old(xs, weights):
    nc = build_ffn_l1()
    identb, rep, repsign, swap, pos = make_consts()
    identf = np.eye(128, dtype=np.float32)
    in_maps = []
    for i in range(NCORES):
        m = {"x": xs[i], "identb": identb, "identf": identf}
        m.update(weights)
        in_maps.append(m)
    res = run_bass_kernel_spmd(nc, in_maps, core_ids=list(range(NCORES)))
    return [r["xo"] for r in res.results]





def emit_ffn_l0(nc, E, sems, io, counts_only, sparse=True):
    x = io["x"]
    identf = io["identf"]
    router_w = io["f0_router_w"]
    ltri = io["ltri"]
    if not counts_only:
        identb = io["identb"]
        w1, b1, w2, b2 = io["f0_w1"], io["f0_b1"], io["f0_w2"], io["f0_b2"]
        cnt_all = io["cnt_all"]
        bmask = io["bmask"]
        xo = io["xo"]
        xfT_dram = io["xfT_dram"]
        y_dram = io["y_dram"]
        if sparse:
            disp_dram = io["disp_dram"]
            eout_dram = io["eout_dram"]
    else:
        cnt_out = io["cnt_out"]

    NT = TOK // 128
    NST = TOK // 512
    TPB = S // 128
    CAP = float(CAP0)
    from contextlib import ExitStack
    with ExitStack() as ctx:
        if not counts_only:
            wA = ctx.enter_context(nc.sbuf_tensor("wA", [128, 8, 4096], BF16))
            wB = ctx.enter_context(nc.sbuf_tensor("wB", [128, 32, D], BF16))
            hT = ctx.enter_context(nc.sbuf_tensor("hT", [128, 32, 512], BF16))
            xfTs = ctx.enter_context(nc.sbuf_tensor("xfTs", [128, 8, 512], BF16))
            b1c = ctx.enter_context(nc.sbuf_tensor("b1c", [128, 32], F32))
            b1r = ctx.enter_context(nc.sbuf_tensor("b1r", [1, 4096], F32))
            b2r = ctx.enter_context(nc.sbuf_tensor("b2r", [1, D], BF16))
            if not sparse:
                geff_store = ctx.enter_context(nc.sbuf_tensor("geff_store", [128, NT, 8], F32))
            cnt_s = ctx.enter_context(nc.sbuf_tensor("cnt_s", [16, 8], F32))
            bmask_s = ctx.enter_context(nc.sbuf_tensor("bmask_s", [16, 2], F32))
            offr = ctx.enter_context(nc.sbuf_tensor("offr", [2, 8], F32))
            offr1 = ctx.enter_context(nc.sbuf_tensor("offr1", [1, 8], F32))
            obc = ctx.enter_context(nc.sbuf_tensor("obc", [128, 16], F32))
            if not sparse:
                yt = ctx.enter_context(nc.sbuf_tensor("yt", [128, 512], F32))
                ytB = ctx.enter_context(nc.sbuf_tensor("ytB", [128, 512], F32))
            else:
                yt = ytB = None
            if sparse:
                ybf = ctx.enter_context(nc.sbuf_tensor("ybf", [128, 512], BF16))
                ybfB = ctx.enter_context(nc.sbuf_tensor("ybfB", [128, 512], BF16))
                idx_store = ctx.enter_context(nc.sbuf_tensor("idx_store", [128, NT, 2], U32))
                gk_store = ctx.enter_context(nc.sbuf_tensor("gk_store", [128, NT, 2], F32))
                cb0 = ctx.enter_context(nc.sbuf_tensor("cb0", [128, 8], F32))
                io8 = ctx.enter_context(nc.sbuf_tensor("io8", [128, 8], F32))
                io8r = ctx.enter_context(nc.sbuf_tensor("io8r", [1, 8], F32))
                dtile = ctx.enter_context(nc.sbuf_tensor("dtile", [128, D], BF16))
                gbuf1 = ctx.enter_context(nc.sbuf_tensor("gbuf1", [128, D], BF16))
                gbuf2 = ctx.enter_context(nc.sbuf_tensor("gbuf2", [128, D], BF16))
            identb_s = ctx.enter_context(nc.sbuf_tensor("identb_s", [128, 128], BF16))
            if not sparse:
                rt = ctx.enter_context(nc.sbuf_tensor("rt", [128, D], F32))
        xt = ctx.enter_context(nc.sbuf_tensor("xt", [128, D], F32))
        xa = ctx.enter_context(nc.sbuf_tensor("xa", [128, D], F32))
        xfTc = ctx.enter_context(nc.sbuf_tensor("xfTc", [128, 128], F32))
        if counts_only or not sparse:
            xfb = ctx.enter_context(nc.sbuf_tensor("xfb", [128, 128], BF16))
        rw_s = ctx.enter_context(nc.sbuf_tensor("rw_s", [128, 8, 8], F32))
        ltri_s = ctx.enter_context(nc.sbuf_tensor("ltri_s", [128, 128], F32))
        bn6 = ctx.enter_context(nc.sbuf_tensor("bn6", [128, 12], F32))
        bnag = ctx.enter_context(nc.sbuf_tensor("bnag", [128, 2], F32))
        stats = ctx.enter_context(nc.sbuf_tensor("stats", [128, 2], F32))
        cc = ctx.enter_context(nc.sbuf_tensor("cc", [128, 2], F32))
        lpad = ctx.enter_context(nc.sbuf_tensor("lpad", [128, 16], F32))
        m8 = ctx.enter_context(nc.sbuf_tensor("m8", [128, 8], F32))
        oh1 = ctx.enter_context(nc.sbuf_tensor("oh1", [128, 8], F32))
        oh2 = ctx.enter_context(nc.sbuf_tensor("oh2", [128, 8], F32))
        ohb = ctx.enter_context(nc.sbuf_tensor("ohb", [128, 8], F32))
        cum = ctx.enter_context(nc.sbuf_tensor("cum", [128, 8], F32))
        carry = ctx.enter_context(nc.sbuf_tensor("carry", [1, 8], F32))
        tsc = ctx.enter_context(nc.sbuf_tensor("tsc", [128, 8], F32))
        col = ctx.enter_context(nc.sbuf_tensor("col", [128, 8], F32))
        onesf = ctx.enter_context(nc.sbuf_tensor("onesf", [128, 1], F32))
        ones1 = ctx.enter_context(nc.sbuf_tensor("ones1", [1, 128], BF16))
        ones1f = ctx.enter_context(nc.sbuf_tensor("ones1f", [1, 128], F32))
        identf_s = ctx.enter_context(nc.sbuf_tensor("identf_s", [128, 128], F32))
        ps_a = ctx.enter_context(nc.psum_tensor("ps_a", [128, 512], F32))
        ps_a2 = ctx.enter_context(nc.psum_tensor("ps_a2", [128, 512], F32))
        ps_d = ctx.enter_context(nc.psum_tensor("ps_d", [128, 512], F32))
        ps_d2 = ctx.enter_context(nc.psum_tensor("ps_d2", [128, 512], F32))
        ps_small = ctx.enter_context(nc.psum_tensor("ps_small", [128, 24], F32))
        ps_l = ps_small[:, 0:8]
        ps_cum = ps_small[:, 8:16]
        ps_cnt = ps_small[0:1, 16:24]
        ps_trf = ctx.enter_context(nc.psum_tensor("ps_trf", [128, 128], F32))
        if not counts_only and sparse:
            ps_trb = ctx.enter_context(nc.psum_tensor("ps_trb", [128, 128], BF16))
        ld, pe, dv, ac, go, st, lg = sems

        E.step("vector", lambda eng: eng.memset(onesf[:, :], 1.0))
        E.step("vector", lambda eng: eng.memset(ones1[:, :], 1.0))
        E.step("vector", lambda eng: eng.memset(ones1f[:, :], 1.0))
        E.step("vector", lambda eng: eng.memset(cc[:, 0:1], EPS))
        E.step("vector", lambda eng: eng.memset(cc[:, 1:2], 1.0))
        E.step("vector", lambda eng: eng.memset(lpad[:, 8:16], -1e30))
        E.step("vector", lambda eng: eng.memset(carry[:, :], 0.0), sem=dv)
        E.dma("sync", identf_s[:, :], identf[:, :], ld)
        E.dma("sync", ltri_s[:, :], ltri[:, :], ld)
        for c in range(8):
            E.dma("sync", rw_s[:, c, :], router_w[128 * c:128 * (c + 1), :], ld)
        if not counts_only:
            E.dma("sync", identb_s[:, :], identb[:, :], ld)
            E.dma("sync", cnt_s[:, :], cnt_all[:, :], ld)
            c_bm = E.dma("sync", bmask_s[:, :], bmask[:, :], ld)
            def f_off(eng):
                return eng.matmul(out=ps_cnt[0:1, 0:8] if False else ps_l[0:8, 0:2], lhsT=cnt_s[:, :],
                                  rhs=bmask_s[:, :], start=True, stop=True)
            E.step("tensor", f_off, sem=pe, wait=[(ld, c_bm)])

            def f_evoff(eng):
                return eng.tensor_copy(out=offr[:, :] if False else col[0:8, 0:2], in_=ps_l[0:8, 0:2])
            E.step("vector", f_evoff, sem=dv, wait=[(pe, pe.n)])

            def f_troff(eng):
                return eng.transpose(out=ps_trf[0:2, 0:8], in_=col[0:8, 0:2], identity=identf_s[0:8, 0:8])
            E.step("tensor", f_troff, sem=pe, wait=[(dv, dv.n)])

            def f_evoff2(eng):
                return eng.tensor_copy(out=offr[:, :], in_=ps_trf[0:2, 0:8])
            E.step("vector", f_evoff2, sem=dv, wait=[(pe, pe.n)])
            E.dma("gpsimd", offr1[:, :], offr[1:2, :], lg, wait=[(dv, dv.n)])
            def f_ob0(eng):
                return eng.matmul(out=ps_l[:, 0:8], lhsT=ones1f[:, :], rhs=offr[0:1, :], start=True, stop=True)
            E.step("tensor", f_ob0, sem=pe, wait=[(lg, lg.n), (dv, dv.n)])

            def f_evob0(eng):
                return eng.tensor_copy(out=obc[:, 0:8], in_=ps_l[:, 0:8])
            E.step("vector", f_evob0, sem=dv, wait=[(pe, pe.n)])

            def f_ob1(eng):
                return eng.matmul(out=ps_l[:, 0:8], lhsT=ones1f[:, :], rhs=offr1[0:1, :], start=True, stop=True)
            E.step("tensor", f_ob1, sem=pe, wait=[(dv, dv.n)])

            def f_evob1(eng):
                return eng.tensor_copy(out=obc[:, 8:16], in_=ps_l[:, 0:8])
            E.step("vector", f_evob1, sem=dv, wait=[(pe, pe.n)])
            if sparse:
                def f_io8(eng):
                    return eng.iota(out=io8r[:, :], pattern=[[1, 8]], base=0,
                                    channel_multiplier=0, allow_small_or_imprecise_dtypes=True)
                E.step("gpsimd", f_io8, sem=go)

                def f_io8b(eng):
                    return eng.matmul(out=ps_l[:, 0:8], lhsT=ones1f[:, :], rhs=io8r[:, :],
                                      start=True, stop=True)
                E.step("tensor", f_io8b, sem=pe, wait=[(go, go.n), (dv, dv.n)])

                def f_evio8(eng):
                    return eng.tensor_copy(out=io8[:, :], in_=ps_l[:, 0:8])
                E.step("vector", f_evio8, sem=dv, wait=[(pe, pe.n)])
                rbound = nc.gpsimd.alloc_register()
                E.emit("gpsimd", lambda eng: eng.reg_mov(rbound, 8 * CLOC - 1))
                E.step("vector", lambda eng: eng.memset(gbuf1[:, :], 0.0), sem=dv)
                for r in range(8 * CLOC // 128):
                    E.dma("gpsimd", disp_dram[128 * r:128 * (r + 1), :], gbuf1[:, :], lg,
                          wait=[(dv, dv.n)])
                zero_done = lg.n

        for i in range(NT):
            b_loc = i // TPB
            ib = i % TPB
            c_x = E.dma("sync", xt[:, :], x[128 * i:128 * (i + 1), :], ld, wait=[(dv, dv.n)])
            emit_ln_stats(E, bn6, bnag, xt, D, stats[:, 0:1], stats[:, 1:2], dv, ac, cc[:, 0:1],
                          wait=[(ld, c_x)])

            def f_norm(eng):
                return eng.tensor_scalar(out=xa[:, :], in0=xt[:, :], scalar1=stats[:, 0:1],
                                         scalar2=stats[:, 1:2], op0=ALU.subtract, op1=ALU.mult)
            E.step("vector", f_norm, sem=dv, wait=[(ac, ac.n), (lg, lg.n)])
            for c in range(8):
                def f_tr(eng, c=c):
                    return eng.transpose(out=ps_trf[:, :], in_=xa[:, 128 * c:128 * (c + 1)],
                                         identity=identf_s[:, :])
                E.step("tensor", f_tr, sem=pe, wait=[(dv, dv.n), (go, go.n)])

                def f_evt(eng):
                    return eng.tensor_copy(out=xfTc[:, :], in_=ps_trf[:, :])
                E.step("vector", f_evt, wait=[(pe, pe.n)])
                if not counts_only and not sparse:
                    def f_cast(eng, c=c):
                        return eng.tensor_copy(out=xfb[:, :], in_=xfTc[:, :])
                    E.step("vector", f_cast, sem=dv, wait=[(lg, lg.n)])
                    E.dma("gpsimd", xfT_dram[c, :, 128 * i:128 * (i + 1)], xfb[:, :], lg,
                          wait=[(dv, dv.n)])
                else:
                    E.step("vector", lambda eng: eng.engine_nop(), sem=dv)

                def f_rmm(eng, c=c):
                    return eng.matmul(out=ps_l[:, 0:8], lhsT=xfTc[:, :], rhs=rw_s[:, c, :],
                                      start=(c == 0), stop=(c == 7))
                E.step("tensor", f_rmm, sem=pe, wait=[(dv, dv.n)])

            def f_evl(eng):
                return eng.tensor_copy(out=lpad[:, 0:8], in_=ps_l[:, 0:8])
            E.step("vector", f_evl, wait=[(pe, pe.n)])

            def f_max8(eng):
                return eng.max(out=m8[:, :], in_=lpad[:, :])
            E.step("vector", f_max8)

            def f_oh1(eng):
                return eng.tensor_scalar(out=oh1[:, :], in0=lpad[:, 0:8], scalar1=m8[:, 0:1],
                                         scalar2=None, op0=ALU.is_equal)
            E.step("vector", f_oh1)

            def f_ohm(eng):
                return eng.tensor_scalar(out=oh2[:, :], in0=lpad[:, 0:8], scalar1=m8[:, 1:2],
                                         scalar2=None, op0=ALU.is_equal)
            E.step("vector", f_ohm)

            def f_not1(eng):
                return eng.tensor_scalar(out=tsc[:, :], in0=oh1[:, :], scalar1=-1.0, scalar2=1.0,
                                         op0=ALU.mult, op1=ALU.add)
            E.step("vector", f_not1)

            def f_oh2(eng):
                return eng.tensor_mul(out=oh2[:, :], in0=oh2[:, :], in1=tsc[:, :])
            E.step("vector", f_oh2)

            def f_ohb(eng):
                return eng.tensor_add(out=ohb[:, :], in0=oh1[:, :], in1=oh2[:, :])
            E.step("vector", f_ohb, sem=dv)

            def f_cum(eng, ib=ib):
                inst = eng.matmul(out=ps_cum[:, 0:8], lhsT=ltri_s[:, :], rhs=ohb[:, :],
                                  start=True, stop=(ib == 0))
                if ib != 0:
                    inst = eng.matmul(out=ps_cum[:, 0:8], lhsT=ones1f[:, :], rhs=carry[:, :],
                                      start=False, stop=True)
                return inst
            E.step("tensor", f_cum, sem=pe, wait=[(dv, dv.n)])

            def f_ccnt(eng):
                return eng.matmul(out=ps_cnt[:, 0:8], lhsT=onesf[:, :], rhs=ohb[:, :],
                                  start=True, stop=True)
            E.step("tensor", f_ccnt, sem=pe)

            def f_evcum(eng):
                return eng.tensor_copy(out=cum[:, :], in_=ps_cum[:, 0:8])
            E.step("vector", f_evcum, wait=[(pe, pe.n)])

            if ib == 0:
                def f_carry(eng):
                    return eng.tensor_copy(out=carry[:, :], in_=ps_cnt[:, 0:8])
                E.step("vector", f_carry, sem=dv, wait=[(lg, lg.n)])
            else:
                def f_carry(eng):
                    return eng.tensor_add(out=carry[:, :], in0=carry[:, :], in1=ps_cnt[:, 0:8])
                E.step("vector", f_carry, sem=dv)

            if counts_only:
                if ib == TPB - 1:
                    E.dma("gpsimd", cnt_out[b_loc:b_loc + 1, :], carry[:, :], lg, wait=[(dv, dv.n)])
                continue
            if sparse and ib == TPB - 1 and b_loc == 0:
                def f_cb0(eng):
                    return eng.matmul(out=ps_l[:, 0:8], lhsT=ones1f[:, :], rhs=carry[:, :],
                                      start=True, stop=True)
                E.step("tensor", f_cb0, sem=pe, wait=[(dv, dv.n)])

                def f_evcb0(eng):
                    return eng.tensor_copy(out=cb0[:, :], in_=ps_l[:, 0:8])
                E.step("vector", f_evcb0, sem=dv, wait=[(pe, pe.n)])

            def f_p1(eng):
                return eng.tensor_mul(out=tsc[:, :], in0=oh1[:, :], in1=cum[:, :])
            E.step("vector", f_p1)

            def f_p1s(eng):
                return eng.reduce_sum(out=col[:, 0:1], in_=tsc[:, :], axis=AX)
            E.step("vector", f_p1s)

            def f_o1(eng, b_loc=b_loc):
                return eng.tensor_mul(out=tsc[:, :], in0=oh1[:, :], in1=obc[:, 8 * b_loc:8 * (b_loc + 1)])
            E.step("vector", f_o1)

            def f_o1s(eng):
                return eng.reduce_sum(out=col[:, 1:2], in_=tsc[:, :], axis=AX)
            E.step("vector", f_o1s)

            def f_p2(eng):
                return eng.tensor_mul(out=tsc[:, :], in0=oh2[:, :], in1=cum[:, :])
            E.step("vector", f_p2)

            def f_p2s(eng):
                return eng.reduce_sum(out=col[:, 2:3], in_=tsc[:, :], axis=AX)
            E.step("vector", f_p2s)

            def f_o2(eng, b_loc=b_loc):
                return eng.tensor_mul(out=tsc[:, :], in0=oh2[:, :], in1=obc[:, 8 * b_loc:8 * (b_loc + 1)])
            E.step("vector", f_o2)

            def f_o2s(eng):
                return eng.reduce_sum(out=col[:, 3:4], in_=tsc[:, :], axis=AX)
            E.step("vector", f_o2s)

            def f_k1(eng):
                return eng.tensor_add(out=col[:, 4:5], in0=col[:, 0:1], in1=col[:, 1:2])
            E.step("vector", f_k1)

            def f_k1b(eng):
                return eng.tensor_scalar(out=col[:, 4:5], in0=col[:, 4:5], scalar1=-1.0, scalar2=CAP,
                                         op0=ALU.mult, op1=ALU.add)
            E.step("vector", f_k1b)

            def f_k1c(eng):
                return eng.tensor_scalar_max(col[:, 4:5], col[:, 4:5], 0.0)
            E.step("vector", f_k1c)

            def f_k1d(eng):
                return eng.tensor_scalar_min(col[:, 4:5], col[:, 4:5], 1.0)
            E.step("vector", f_k1d)

            def f_k2(eng):
                return eng.tensor_add(out=col[:, 5:6], in0=col[:, 2:3], in1=col[:, 3:4])
            E.step("vector", f_k2)

            def f_k2b(eng):
                return eng.tensor_scalar(out=col[:, 5:6], in0=col[:, 5:6], scalar1=-1.0, scalar2=CAP,
                                         op0=ALU.mult, op1=ALU.add)
            E.step("vector", f_k2b)

            def f_k2c(eng):
                return eng.tensor_scalar_max(col[:, 5:6], col[:, 5:6], 0.0)
            E.step("vector", f_k2c)

            def f_k2d(eng):
                return eng.tensor_scalar_min(col[:, 5:6], col[:, 5:6], 1.0)
            E.step("vector", f_k2d)

            def f_d(eng):
                return eng.tensor_sub(out=col[:, 6:7], in0=m8[:, 1:2], in1=m8[:, 0:1])
            E.step("vector", f_d, sem=dv)

            def f_e(eng):
                return eng.activation(out=col[:, 6:7], in_=col[:, 6:7], func=AF.Exp)
            E.step("scalar", f_e, sem=ac, wait=[(dv, dv.n)])

            def f_g1(eng):
                return eng.tensor_scalar_add(col[:, 7:8], col[:, 6:7], 1.0)
            E.step("vector", f_g1, wait=[(ac, ac.n)])

            def f_g1b(eng):
                return eng.reciprocal(out=col[:, 6:7], in_=col[:, 7:8])
            E.step("vector", f_g1b)

            def f_g2(eng):
                return eng.tensor_scalar(out=col[:, 7:8], in0=col[:, 6:7], scalar1=-1.0, scalar2=1.0,
                                         op0=ALU.mult, op1=ALU.add)
            E.step("vector", f_g2)

            def f_gk1(eng):
                return eng.tensor_mul(out=col[:, 4:5], in0=col[:, 4:5], in1=col[:, 6:7])
            E.step("vector", f_gk1)

            def f_gk2(eng):
                return eng.tensor_mul(out=col[:, 5:6], in0=col[:, 5:6], in1=col[:, 7:8])
            E.step("vector", f_gk2)

            if not sparse:
                def f_ge1(eng, i=i):
                    return eng.tensor_scalar(out=geff_store[:, i, :], in0=oh1[:, :], scalar1=col[:, 4:5],
                                             scalar2=None, op0=ALU.mult)
                E.step("vector", f_ge1)

                def f_ge2(eng):
                    return eng.tensor_scalar(out=tsc[:, :], in0=oh2[:, :], scalar1=col[:, 5:6],
                                             scalar2=None, op0=ALU.mult)
                E.step("vector", f_ge2)

                def f_ge(eng, i=i):
                    return eng.tensor_add(out=geff_store[:, i, :], in0=geff_store[:, i, :], in1=tsc[:, :])
                E.step("vector", f_ge, sem=dv)
            else:
                for k_ in (0, 1):
                    ohk = oh1 if k_ == 0 else oh2
                    pslice = col[:, 0:1] if k_ == 0 else col[:, 2:3]
                    gslice = col[:, 4:5] if k_ == 0 else col[:, 5:6]

                    def f_ev_(eng, ohk=ohk):
                        return eng.tensor_mul(out=tsc[:, :], in0=ohk[:, :], in1=io8[:, :])
                    E.step("vector", f_ev_)

                    def f_evs_(eng):
                        return eng.reduce_sum(out=col[:, 6:7], in_=tsc[:, :], axis=AX)
                    E.step("vector", f_evs_)

                    def f_sb_(eng):
                        return eng.tensor_scalar(out=col[:, 6:7], in0=col[:, 6:7], scalar1=float(CLOC),
                                                 scalar2=None, op0=ALU.mult)
                    E.step("vector", f_sb_)
                    if b_loc == 1:
                        def f_lv_(eng, ohk=ohk):
                            return eng.tensor_mul(out=tsc[:, :], in0=ohk[:, :], in1=cb0[:, :])
                        E.step("vector", f_lv_)

                        def f_lvs_(eng):
                            return eng.reduce_sum(out=col[:, 7:8], in_=tsc[:, :], axis=AX)
                        E.step("vector", f_lvs_)

                        def f_lp_(eng, pslice=pslice):
                            return eng.tensor_add(out=col[:, 7:8], in0=col[:, 7:8], in1=pslice)
                        E.step("vector", f_lp_)
                    else:
                        def f_lp_(eng, pslice=pslice):
                            return eng.tensor_copy(out=col[:, 7:8], in_=pslice)
                        E.step("vector", f_lp_)

                    def f_slot_(eng):
                        return eng.tensor_add(out=col[:, 6:7], in0=col[:, 6:7], in1=col[:, 7:8])
                    E.step("vector", f_slot_)

                    def f_v1_(eng):
                        return eng.tensor_scalar(out=col[:, 7:8], in0=col[:, 7:8], scalar1=-1.0,
                                                 scalar2=float(CLOC), op0=ALU.mult, op1=ALU.add)
                    E.step("vector", f_v1_)

                    def f_v2_(eng):
                        return eng.tensor_scalar_max(col[:, 7:8], col[:, 7:8], 0.0)
                    E.step("vector", f_v2_)

                    def f_v3_(eng):
                        return eng.tensor_scalar_min(col[:, 7:8], col[:, 7:8], 1.0)
                    E.step("vector", f_v3_)

                    def f_gk_(eng, i=i, k_=k_, gslice=gslice):
                        return eng.tensor_mul(out=gk_store[:, i, k_:k_ + 1], in0=gslice, in1=col[:, 7:8])
                    E.step("vector", f_gk_)

                    def f_inv_(eng):
                        return eng.tensor_scalar(out=col[:, 7:8], in0=col[:, 7:8], scalar1=-1.0,
                                                 scalar2=1.0, op0=ALU.mult, op1=ALU.add)
                    E.step("vector", f_inv_)

                    def f_idx_(eng):
                        return eng.scalar_tensor_tensor(out=col[:, 6:7], in0=col[:, 7:8], scalar=BIGIDX,
                                                        in1=col[:, 6:7], op0=ALU.mult, op1=ALU.add)
                    E.step("vector", f_idx_)

                    def f_u32_(eng, i=i, k_=k_):
                        return eng.tensor_copy(out=idx_store[:, i, k_:k_ + 1], in_=col[:, 6:7])
                    E.step("vector", f_u32_, sem=dv)

                    def f_sc_(eng, i=i, k_=k_):
                        return eng.indirect_dma_start(
                            out=disp_dram[:, :],
                            out_offset=IndirectOffsetOnAxis(idx_store[:, i, k_:k_ + 1], 0),
                            in_=xa[:, :], in_offset=None,
                            bounds_check=rbound, oob_is_err=False)
                    lg.n += 16
                    E.emit("gpsimd", f_sc_, wait=[(dv, dv.n), (lg, zero_done)], inc=[(lg, 16)])
                E.step("vector", lambda eng: eng.engine_nop(), sem=dv)

        if counts_only:
            return

        for e in range(8):
            wprev = lg.n
            for c in range(8):
                E.dma("gpsimd", wA[:, c, :], w1[e, 128 * c:128 * (c + 1), :], lg,
                      wait=[(pe, pe.n), (lg, wprev)])
            for c in range(32):
                E.dma("gpsimd", wB[:, c, :], w2[e, 128 * c:128 * (c + 1), :], lg,
                      wait=[(pe, pe.n), (lg, wprev)])
            E.dma("sync", b1r[:, :], b1[e:e + 1, :], ld, wait=[(pe, pe.n)])
            c_b2 = E.dma("gpsimd", b2r[:, :], b2[e:e + 1, :], lg, wait=[(pe, pe.n), (lg, wprev)])
            ld_w = lg.n
            for c in range(32):
                def f_trb(eng, c=c):
                    return eng.transpose(out=ps_trf[0:128, 0:1], in_=b1r[:, 128 * c:128 * (c + 1)],
                                         identity=identf_s[0:1, 0:1])
                E.step("tensor", f_trb, sem=pe, wait=[(ld, ld.n), (ac, ac.n), (dv, dv.n)])

                def f_evb(eng, c=c):
                    return eng.tensor_copy(out=b1c[:, c:c + 1], in_=ps_trf[0:128, 0:1])
                E.step("vector", f_evb, sem=dv, wait=[(pe, pe.n)])
            for stp in range(CLOC // 512 if sparse else NST):
                if sparse:
                    for q in range(4):
                        r0 = e * CLOC + 512 * stp + 128 * q
                        c_d = E.dma("sync", dtile[:, :], disp_dram[r0:r0 + 128, :], ld,
                                    wait=[(pe, pe.n), (dv, dv.n), (lg, lg.n)])
                        for c in range(8):
                            def f_trd(eng, c=c):
                                return eng.transpose(out=ps_trb[:, :], in_=dtile[:, 128 * c:128 * (c + 1)],
                                                     identity=identb_s[:, :])
                            E.step("tensor", f_trd, sem=pe, wait=[(ld, c_d), (dv, dv.n)])

                            def f_evd(eng, c=c, q=q):
                                return eng.tensor_copy(out=xfTs[:, c, 128 * q:128 * (q + 1)], in_=ps_trb[:, :])
                            E.step("vector", f_evd, sem=dv, wait=[(pe, pe.n)])
                    c_xf = ld.n
                else:
                    c_xf = 0
                    for c in range(8):
                        c_xf = E.dma("sync", xfTs[:, c, 0:512], xfT_dram[c, :, 512 * stp:512 * (stp + 1)], ld,
                                     wait=[(pe, pe.n), (dv, dv.n), (lg, lg.n)])
                xfts_dv = dv.n
                relu_done = {}
                for ich in range(32):
                    pz = ps_a if ich % 2 == 0 else ps_a2

                    def f_z(eng, ich=ich, pz=pz):
                        inst = None
                        for dch in range(8):
                            inst = eng.matmul(out=pz[:, :], lhsT=wA[:, dch, 128 * ich:128 * (ich + 1)],
                                              rhs=xfTs[:, dch, :], start=(dch == 0), stop=(dch == 7))
                        return inst
                    E.step("tensor", f_z, sem=pe,
                           wait=[(lg, ld_w), (ld, c_xf),
                                 (ac, relu_done.get(ich - 2, ac.n if ich < 2 else 0)),
                                 (dv, xfts_dv if ich == 0 else 0)])

                    def f_relu(eng, ich=ich, pz=pz):
                        return eng.activation(out=hT[:, ich, :], in_=pz[:, :], func=AF.Relu,
                                              bias=b1c[:, ich:ich + 1], scale=1.0)
                    relu_done[ich] = E.step("scalar", f_relu, sem=ac,
                                            wait=[(pe, pe.n), (dv, dv.n if ich == 0 else 0)])
                evy_done = {}
                for k8, (tt, dh) in enumerate([(a, b) for a in range(4) for b in range(2)]):
                    tile = 4 * stp + tt
                    pd = ps_d if k8 % 2 == 0 else ps_d2
                    yb = yt if k8 % 2 == 0 else ytB

                    def f_dn(eng, tt=tt, dh=dh, pd=pd):
                        inst = None
                        for ich in range(32):
                            inst = eng.matmul(out=pd[:, :], lhsT=hT[:, ich, 128 * tt:128 * (tt + 1)],
                                              rhs=wB[:, ich, 512 * dh:512 * (dh + 1)],
                                              start=(ich == 0), stop=False)
                        return eng.matmul(out=pd[:, :], lhsT=ones1[:, :],
                                          rhs=b2r[:, 512 * dh:512 * (dh + 1)], start=False, stop=True)
                    E.step("tensor", f_dn, sem=pe,
                           wait=[(ac, ac.n if k8 == 0 else 0), (lg, c_b2),
                                 (dv, dv.n if k8 < 2 else evy_done.get(k8 - 2, 0))])

                    if sparse:
                        ybs = ybf if k8 % 2 == 0 else ybfB

                        def f_evy(eng, pd=pd, ybs=ybs):
                            return eng.tensor_copy(out=ybs[:, :], in_=pd[:, :])
                        evy_done[k8] = E.step("vector", f_evy, sem=dv, wait=[(pe, pe.n), (st, st.n)])
                        r0 = e * CLOC + 512 * stp + 128 * tt
                        E.dma("scalar", eout_dram[r0:r0 + 128, 512 * dh:512 * (dh + 1)], ybs[:, :],
                              st, wait=[(dv, dv.n)])
                    else:
                        def f_evy(eng, tile=tile, e=e, pd=pd, yb=yb):
                            return eng.tensor_scalar(out=yb[:, :], in0=pd[:, :],
                                                     scalar1=geff_store[:, tile, e:e + 1],
                                                     scalar2=None, op0=ALU.mult)
                        evy_done[k8] = E.step("vector", f_evy, sem=dv, wait=[(pe, pe.n), (lg, lg.n)])
                        t0 = 512 * stp + 128 * tt

                        def f_acc(eng, t0=t0, dh=dh, e=e, yb=yb):
                            return eng.dma_start(out=y_dram[t0:t0 + 128, 512 * dh:512 * (dh + 1)],
                                                 in_=yb[:, :],
                                                 accum_op=(ALU.bypass if e == 0 else ALU.add))
                        lg.n += 16
                        E.emit("gpsimd", f_acc, wait=[(dv, dv.n)], inc=[(lg, 16)])

        if sparse:
            eout_done = st.n
            for i in range(NT):
                c_x = E.dma("sync", xt[:, :], x[128 * i:128 * (i + 1), :], ld,
                            wait=[(dv, dv.n), (st, st.n)])
                E.step("vector", lambda eng: eng.memset(gbuf1[:, :], 0.0), wait=[(lg, lg.n)])
                E.step("vector", lambda eng: eng.memset(gbuf2[:, :], 0.0), sem=dv)
                for k_, gb_ in ((0, gbuf1), (1, gbuf2)):
                    def f_ga(eng, i=i, k_=k_, gb_=gb_):
                        return eng.indirect_dma_start(
                            out=gb_[:, :], out_offset=None,
                            in_=eout_dram[:, :],
                            in_offset=IndirectOffsetOnAxis(idx_store[:, i, k_:k_ + 1], 0),
                            bounds_check=rbound, oob_is_err=False)
                    lg.n += 16
                    E.emit("gpsimd", f_ga, wait=[(dv, dv.n), (st, eout_done)], inc=[(lg, 16)])
                gath = lg.n

                def f_c1(eng, i=i):
                    return eng.tensor_scalar(out=xa[:, :], in0=gbuf1[:, :],
                                             scalar1=gk_store[:, i, 0:1], scalar2=None, op0=ALU.mult)
                E.step("vector", f_c1, wait=[(lg, gath), (st, st.n)])

                def f_c2(eng, i=i):
                    return eng.scalar_tensor_tensor(out=xa[:, :], in0=gbuf2[:, :],
                                                    scalar=gk_store[:, i, 1:2], in1=xa[:, :],
                                                    op0=ALU.mult, op1=ALU.add)
                E.step("vector", f_c2)

                def f_out(eng):
                    return eng.scalar_tensor_tensor(out=xa[:, :], in0=xa[:, :], scalar=0.5, in1=xt[:, :],
                                                    op0=ALU.mult, op1=ALU.add)
                E.step("vector", f_out, sem=dv, wait=[(ld, c_x)])
                E.dma("scalar", xo[128 * i:128 * (i + 1), :], xa[:, :], st, wait=[(dv, dv.n)])
        else:
            acc_done = lg.n
            for i in range(NT):
                c_x = E.dma("sync", xt[:, :], x[128 * i:128 * (i + 1), :], ld,
                            wait=[(dv, dv.n), (lg, acc_done)])
                c_r = E.dma("sync", rt[:, :], y_dram[128 * i:128 * (i + 1), :], ld,
                            wait=[(st, st.n)])

                def f_out(eng):
                    return eng.scalar_tensor_tensor(out=xa[:, :], in0=rt[:, :], scalar=0.5, in1=xt[:, :],
                                                    op0=ALU.mult, op1=ALU.add)
                E.step("vector", f_out, sem=dv, wait=[(ld, c_r)])
                E.dma("scalar", xo[128 * i:128 * (i + 1), :], xa[:, :], st, wait=[(dv, dv.n)])





def ffn0_io(nc, io):
    io["f0_router_w"] = nc.dram_tensor("f0_router_w", [D, 8], F32, kind="ExternalInput").ap()
    io["f0_w1"] = nc.dram_tensor("f0_w1", [8, D, 4096], F32, kind="ExternalInput").ap()
    io["f0_b1"] = nc.dram_tensor("f0_b1", [8, 4096], F32, kind="ExternalInput").ap()
    io["f0_w2"] = nc.dram_tensor("f0_w2", [8, 4096, D], F32, kind="ExternalInput").ap()
    io["f0_b2"] = nc.dram_tensor("f0_b2", [8, D], F32, kind="ExternalInput").ap()


def build_ffn_l0(counts_only):
    from contextlib import ExitStack
    nc = new_program()
    io = {"x": nc.dram_tensor("x", [TOK, D], F32, kind="ExternalInput").ap(),
          "identf": nc.dram_tensor("identf", [128, 128], F32, kind="ExternalInput").ap(),
          "f0_router_w": nc.dram_tensor("f0_router_w", [D, 8], F32, kind="ExternalInput").ap(),
          "ltri": nc.dram_tensor("ltri", [128, 128], F32, kind="ExternalInput").ap()}
    if counts_only:
        io["cnt_out"] = nc.dram_tensor("cnt_out", [2, 8], F32, kind="ExternalOutput").ap()
    else:
        io["identb"] = nc.dram_tensor("identb", [128, 128], BF16, kind="ExternalInput").ap()
        for n, sh in (("f0_w1", [8, D, 4096]), ("f0_b1", [8, 4096]), ("f0_w2", [8, 4096, D]),
                      ("f0_b2", [8, D])):
            io[n] = nc.dram_tensor(n, sh, F32, kind="ExternalInput").ap()
        io["cnt_all"] = nc.dram_tensor("cnt_all", [16, 8], F32, kind="ExternalInput").ap()
        io["bmask"] = nc.dram_tensor("bmask", [16, 2], F32, kind="ExternalInput").ap()
        io["xo"] = nc.dram_tensor("xo", [TOK, D], F32, kind="ExternalOutput").ap()
        io["xfT_dram"] = nc.dram_tensor("xfT_dram", [8, 128, TOK], BF16, kind="Internal").ap()
        io["y_dram"] = nc.dram_tensor("y_dram", [TOK, D], F32, kind="Internal").ap()
        io["disp_dram"] = nc.dram_tensor("disp_dram", [8 * CLOC, D], BF16, kind="Internal").ap()
        io["eout_dram"] = nc.dram_tensor("eout_dram", [8 * CLOC, D], BF16, kind="Internal").ap()
    with ExitStack() as octx:
        sems = make_sems(nc, octx)
        ld, pe, dv, ac, go, st, lg = sems
        E = Emitter(auto={"vector": dv, "scalar": ac, "gpsimd": go})
        emit_ffn_l0(nc, E, sems, io, counts_only)
        finish_program(nc, octx, E, sems)
    return nc

def make_l0_consts():
    identf = np.eye(128, dtype=np.float32)
    ltri = np.tril(np.ones((128, 128), np.float32), -1).T
    return identf, ltri


def make_bmask(core):
    bm = np.zeros((16, 2), np.float32)
    for b_loc, gb_ in enumerate(CORE_BATCHES[core]):
        bm[:gb_, b_loc] = 1.0
    return bm


_NC_CACHE = {}

def _get_nc(key, builder, *args):
    if key not in _NC_CACHE:
        _NC_CACHE[key] = builder(*args)
    return _NC_CACHE[key]


def _run_l0_ffn(xs, l0w, identf, ltri, identb):
    nc = _get_nc("l0cnt", build_ffn_l0, True)
    in_maps = [{"x": xs[i], "identf": identf, "ltri": ltri, "f0_router_w": l0w["router_w"]}
               for i in range(NCORES)]
    res = run_bass_kernel_spmd(nc, in_maps, core_ids=list(range(NCORES)))
    cnt_all = np.zeros((16, 8), np.float32)
    for i in range(NCORES):
        c = res.results[i]["cnt_out"]
        cnt_all[CORE_BATCHES[i][0]] = c[0]
        cnt_all[CORE_BATCHES[i][1]] = c[1]
    nc = _get_nc("l0ffn", build_ffn_l0, False)
    in_maps = []
    for i in range(NCORES):
        m = {"x": xs[i], "identf": identf, "identb": identb, "ltri": ltri,
             "cnt_all": cnt_all, "bmask": make_bmask(i)}
        m.update({"f0_" + k: v for k, v in l0w.items()})
        in_maps.append(m)
    res = run_bass_kernel_spmd(nc, in_maps, core_ids=list(range(NCORES)))
    return [r["xo"] for r in res.results]





def attn_io(nc, io, layer):
    if layer == 1:
        for n, sh in (("wq", [D, D]), ("wk", [D, D]), ("wv", [D, D]), ("wo", [D, D]),
                      ("gamma", [1, D]), ("beta", [1, D])):
            io[n] = nc.dram_tensor(n, sh, F32, kind="ExternalInput").ap()
    else:
        io["wqkvT"] = nc.dram_tensor("wqkvT", [D, 3 * D], F32, kind="ExternalInput").ap()
        io["woT"] = nc.dram_tensor("woT", [D, D], F32, kind="ExternalInput").ap()
        io["freqs"] = nc.dram_tensor("freqs", [1, 32], F32, kind="ExternalInput").ap()
        io["repmat"] = nc.dram_tensor("repmat", [32, 128], BF16, kind="ExternalInput").ap()
        io["repsign"] = nc.dram_tensor("repsign", [32, 128], BF16, kind="ExternalInput").ap()
        io["swapm"] = nc.dram_tensor("swapm", [128, 128], BF16, kind="ExternalInput").ap()
        io["pa_rows"] = nc.dram_tensor("pa_rows", [2, 16], F32, kind="ExternalInput").ap()


def build_part1():
    from contextlib import ExitStack
    nc = new_program()
    io = {"x": nc.dram_tensor("x", [TOK, D], F32, kind="ExternalInput").ap(),
          "identb": nc.dram_tensor("identb", [128, 128], BF16, kind="ExternalInput").ap(),
          "identf": nc.dram_tensor("identf", [128, 128], F32, kind="ExternalInput").ap(),
          "ltri": nc.dram_tensor("ltri", [128, 128], F32, kind="ExternalInput").ap(),
          "cnt_all": nc.dram_tensor("cnt_all", [16, 8], F32, kind="ExternalInput").ap(),
          "bmask": nc.dram_tensor("bmask", [16, 2], F32, kind="ExternalInput").ap(),
          "xo": nc.dram_tensor("xo", [TOK, D], F32, kind="ExternalOutput").ap(),
          "cnt_out": nc.dram_tensor("cnt_out", [2, 8], F32, kind="ExternalOutput").ap(),
          "xfT_dram": nc.dram_tensor("xfT_dram", [8, 128, TOK], BF16, kind="Internal").ap(),
          "y_dram": nc.dram_tensor("y_dram", [TOK, D], F32, kind="Internal").ap(),
          "disp_dram": nc.dram_tensor("disp_dram", [8 * CLOC, D], BF16, kind="Internal").ap(),
          "eout_dram": nc.dram_tensor("eout_dram", [8 * CLOC, D], BF16, kind="Internal").ap(),
          "xmid": nc.dram_tensor("xmid", [TOK, D], F32, kind="Internal").ap()}
    ffn0_io(nc, io)
    attn_io(nc, io, 0)
    with ExitStack() as octx:
        sems = make_sems(nc, octx)
        ld, pe, dv, ac, go, st, lg = sems
        E = Emitter(auto={"vector": dv, "scalar": ac, "gpsimd": go})
        io1 = dict(io); io1["xo"] = io["xmid"]
        emit_ffn_l0(NcPrefix(nc, "s1_"), E, sems, io1, False)
        full_barrier(E, sems)
        io2 = dict(io); io2["x"] = io["xmid"]
        emit_attn(NcPrefix(nc, "s2_"), E, sems, io2, 0)
        full_barrier(E, sems)
        io3 = dict(io); io3["x"] = io["xo"]
        emit_ffn_l0(NcPrefix(nc, "s3_"), E, sems, io3, True)
        finish_program(nc, octx, E, sems)
    return nc


def build_part2():
    from contextlib import ExitStack
    nc = new_program()
    io = {"x": nc.dram_tensor("x", [TOK, D], F32, kind="ExternalInput").ap(),
          "identb": nc.dram_tensor("identb", [128, 128], BF16, kind="ExternalInput").ap(),
          "identf": nc.dram_tensor("identf", [128, 128], F32, kind="ExternalInput").ap(),
          "ltri": nc.dram_tensor("ltri", [128, 128], F32, kind="ExternalInput").ap(),
          "cnt_all": nc.dram_tensor("cnt_all", [16, 8], F32, kind="ExternalInput").ap(),
          "bmask": nc.dram_tensor("bmask", [16, 2], F32, kind="ExternalInput").ap(),
          "xo": nc.dram_tensor("xo", [TOK, D], F32, kind="ExternalOutput").ap(),
          "xfT_dram": nc.dram_tensor("xfT_dram", [8, 128, TOK], BF16, kind="Internal").ap(),
          "y_dram": nc.dram_tensor("y_dram", [TOK, D], F32, kind="Internal").ap(),
          "disp_dram": nc.dram_tensor("disp_dram", [8 * CLOC, D], BF16, kind="Internal").ap(),
          "eout_dram": nc.dram_tensor("eout_dram", [8 * CLOC, D], BF16, kind="Internal").ap(),
          "disp1_dram": nc.dram_tensor("disp1_dram", [8 * C1LOC, D], BF16, kind="Internal").ap(),
          "eout1_dram": nc.dram_tensor("eout1_dram", [8 * C1LOC, D], BF16, kind="Internal").ap(),
          "xmid": nc.dram_tensor("xmid", [TOK, D], F32, kind="Internal").ap(),
          "xmid2": nc.dram_tensor("xmid2", [TOK, D], F32, kind="Internal").ap()}
    ffn0_io(nc, io)
    ffn1_io(nc, io)
    attn_io(nc, io, 1)
    with ExitStack() as octx:
        sems = make_sems(nc, octx)
        ld, pe, dv, ac, go, st, lg = sems
        E = Emitter(auto={"vector": dv, "scalar": ac, "gpsimd": go})
        io1 = dict(io); io1["xo"] = io["xmid"]
        emit_ffn_l0(NcPrefix(nc, "s1_"), E, sems, io1, False)
        full_barrier(E, sems)
        io2 = dict(io); io2["x"] = io["xmid"]; io2["xo"] = io["xmid2"]
        emit_ffn_l1(NcPrefix(nc, "s2_"), E, sems, io2)
        full_barrier(E, sems)
        io3 = dict(io); io3["x"] = io["xmid2"]; io3["xo"] = io["xmid"]
        emit_attn(NcPrefix(nc, "s3_"), E, sems, io3, 1)
        full_barrier(E, sems)
        io4 = dict(io); io4["x"] = io["xmid"]
        emit_ffn_l1(NcPrefix(nc, "s4_"), E, sems, io4)
        finish_program(nc, octx, E, sems)
    return nc


def kernel(**inputs):
    f32 = lambda a: np.ascontiguousarray(np.asarray(a), dtype=np.float32)
    x = f32(inputs["L0_attn_x"])
    identb, rep, repsign, swap, pos = make_consts()
    identf, ltri = make_l0_consts()
    common = {"identb": identb, "identf": identf, "ltri": ltri}
    f0 = dict(f0_router_w=f32(inputs["L0_ffn_w_router"]), f0_w1=f32(inputs["L0_ffn_w1"]),
              f0_b1=f32(inputs["L0_ffn_b1"]), f0_w2=f32(inputs["L0_ffn_w2"]),
              f0_b2=f32(inputs["L0_ffn_b2"]))
    a0 = dict(wqkvT=f32(inputs["L0_attn_qkv_w"]).T.copy(),
              woT=f32(inputs["L0_attn_out_w"]).T.copy(),
              freqs=f32(inputs["L0_attn_rope_freqs"]).reshape(1, 32),
              repmat=rep, repsign=repsign, swapm=swap,
              pa_rows=np.stack([f32(inputs["L0_attn_bias_p"]).reshape(16),
                                f32(inputs["L0_attn_bias_a"]).reshape(16)]).astype(np.float32))
    a1 = dict(wq=f32(inputs["L1_attn_Wq"]), wk=f32(inputs["L1_attn_Wk"]),
              wv=f32(inputs["L1_attn_Wv"]), wo=f32(inputs["L1_attn_Wo"]),
              gamma=f32(inputs["L1_attn_ln_gamma"]).reshape(1, D),
              beta=f32(inputs["L1_attn_ln_beta"]).reshape(1, D))
    f1 = dict(f1_ln_w=f32(inputs["L1_ffn_ln_w"]).reshape(1, D),
              f1_ln_b=f32(inputs["L1_ffn_ln_b"]).reshape(1, D),
              f1_router_w=f32(inputs["L1_ffn_router_w"]), f1_gate_w=f32(inputs["L1_ffn_gate_w"]),
              f1_up_w=f32(inputs["L1_ffn_up_w"]), f1_down_w=f32(inputs["L1_ffn_down_w"]),
              f1_sh_gate=f32(inputs["L1_ffn_sh_gate"]), f1_sh_up=f32(inputs["L1_ffn_sh_up"]),
              f1_sh_down=f32(inputs["L1_ffn_sh_down"]))
    xs = shard_x(x)

    nc = _get_nc("l0cnt", build_ffn_l0, True)
    in_maps = [{"x": xs[i], "identf": identf, "ltri": ltri, "f0_router_w": f0["f0_router_w"]}
               for i in range(NCORES)]
    res = run_bass_kernel_spmd(nc, in_maps, core_ids=list(range(NCORES)))
    cnt1 = np.zeros((16, 8), np.float32)
    for i in range(NCORES):
        c = res.results[i]["cnt_out"]
        cnt1[CORE_BATCHES[i][0]] = c[0]
        cnt1[CORE_BATCHES[i][1]] = c[1]

    nc = _get_nc("part1", build_part1)
    in_maps = []
    for i in range(NCORES):
        m = {"x": xs[i], "cnt_all": cnt1, "bmask": make_bmask(i)}
        m.update(common); m.update(f0); m.update(a0)
        in_maps.append(m)
    res = run_bass_kernel_spmd(nc, in_maps, core_ids=list(range(NCORES)))
    xs = [r["xo"] for r in res.results]
    cnt2 = np.zeros((16, 8), np.float32)
    for i in range(NCORES):
        c = res.results[i]["cnt_out"]
        cnt2[CORE_BATCHES[i][0]] = c[0]
        cnt2[CORE_BATCHES[i][1]] = c[1]

    nc = _get_nc("part2", build_part2)
    in_maps = []
    for i in range(NCORES):
        m = {"x": xs[i], "cnt_all": cnt2, "bmask": make_bmask(i)}
        m.update(common); m.update(f0); m.update(f1); m.update(a1)
        in_maps.append(m)
    res = run_bass_kernel_spmd(nc, in_maps, core_ids=list(range(NCORES)))
    return unshard_x([r["xo"] for r in res.results])


def kernel_8launch(**inputs):
    f32 = lambda a: np.ascontiguousarray(np.asarray(a), dtype=np.float32)
    x = f32(inputs["L0_attn_x"])
    identb, rep, repsign, swap, pos = make_consts()
    identf, ltri = make_l0_consts()

    l0w = dict(router_w=f32(inputs["L0_ffn_w_router"]), w1=f32(inputs["L0_ffn_w1"]),
               b1=f32(inputs["L0_ffn_b1"]), w2=f32(inputs["L0_ffn_w2"]), b2=f32(inputs["L0_ffn_b2"]))
    a0w = dict(wqkvT=f32(inputs["L0_attn_qkv_w"]).T.copy(),
               woT=f32(inputs["L0_attn_out_w"]).T.copy(),
               freqs=f32(inputs["L0_attn_rope_freqs"]).reshape(1, 32),
               pa_rows=np.stack([f32(inputs["L0_attn_bias_p"]).reshape(16),
                                 f32(inputs["L0_attn_bias_a"]).reshape(16)]).astype(np.float32))
    a1w = dict(wq=f32(inputs["L1_attn_Wq"]), wk=f32(inputs["L1_attn_Wk"]),
               wv=f32(inputs["L1_attn_Wv"]), wo=f32(inputs["L1_attn_Wo"]),
               gamma=f32(inputs["L1_attn_ln_gamma"]).reshape(1, D),
               beta=f32(inputs["L1_attn_ln_beta"]).reshape(1, D))
    f1w = dict(ln_w=f32(inputs["L1_ffn_ln_w"]).reshape(1, D),
               ln_b=f32(inputs["L1_ffn_ln_b"]).reshape(1, D),
               router_w=f32(inputs["L1_ffn_router_w"]), gate_w=f32(inputs["L1_ffn_gate_w"]),
               up_w=f32(inputs["L1_ffn_up_w"]), down_w=f32(inputs["L1_ffn_down_w"]),
               sh_gate=f32(inputs["L1_ffn_sh_gate"]), sh_up=f32(inputs["L1_ffn_sh_up"]),
               sh_down=f32(inputs["L1_ffn_sh_down"]))

    xs = shard_x(x)
    xs = _run_l0_ffn(xs, l0w, identf, ltri, identb)
    xs = run_attn(0, xs, a0w)
    xs = _run_l0_ffn(xs, l0w, identf, ltri, identb)
    xs = run_ffn_l1_cached(xs, f1w)
    xs = run_attn_cached(1, xs, a1w)
    xs = run_ffn_l1_cached(xs, f1w)
    return unshard_x(xs)


def run_ffn_l1_cached(xs, weights):
    nc = _get_nc("l1ffn", build_ffn_l1)
    identb, rep, repsign, swap, pos = make_consts()
    identf, ltri = make_l0_consts()
    in_maps = []
    for i in range(NCORES):
        m = {"x": xs[i], "identb": identb, "identf": identf, "ltri": ltri}
        m.update({"f1_" + k: v for k, v in weights.items()})
        in_maps.append(m)
    res = run_bass_kernel_spmd(nc, in_maps, core_ids=list(range(NCORES)))
    return [r["xo"] for r in res.results]


def run_attn_cached(layer, xs, weights):
    nc = _get_nc(("attn", layer), build_attn, layer)
    identb, rep, repsign, swap, pos = make_consts()
    in_maps = []
    for i in range(NCORES):
        m = {"x": xs[i], "identb": identb}
        if layer == 1:
            m.update(weights)
        else:
            m.update(weights, repmat=rep, repsign=repsign, swapm=swap)
        in_maps.append(m)
    res = run_bass_kernel_spmd(nc, in_maps, core_ids=list(range(NCORES)))
    return [r["xo"] for r in res.results]

